# version 19
# baseline (speedup 1.0000x reference)
from contextlib import ExitStack

import numpy as np
import concourse.bass as bass
import concourse.mybir as mybir
from concourse.bass_utils import run_bass_kernel_spmd

F32 = mybir.dt.float32
BF16 = mybir.dt.bfloat16
I32 = mybir.dt.int32
I8 = mybir.dt.int8
AF = mybir.ActivationFunctionType
AL = mybir.AluOpType
AX = mybir.AxisListType

D = 1024
NCH = 8
H0 = 16
HD0 = 64
WIN = 512
H1 = 8
D1 = 128
FF0 = 3072
FF1 = 4096
EPS_LN = 1e-5
EPS_GN = 1e-5
EPS_Q = 1e-8
INV127 = 1.0 / 127.0
TT = 512
RMAGIC = 12582912.0


_UID = [0]


def _u(name):
    _UID[0] += 1
    return f"{name}_{_UID[0]}"


class Sem:
    def __init__(self, nc, name):
        self.h = nc.semaphore(name).__enter__()
        self.v = 0

    def tag(self, inst, n=1):
        inst.then_inc(self.h, n)
        self.v += n
        return self.v


class Kb:

    def __init__(self, nc, T, ncores):
        self.nc = nc
        self.T = T
        self.NT = T // TT
        self.ncores = ncores
        self.ldh = Sem(nc, "ldh")
        self.ldh2 = Sem(nc, "ldh2")
        self.sth = Sem(nc, "sth")
        self.sth2 = Sem(nc, "sth2")
        self.lds = Sem(nc, "lds")
        self.lds2 = Sem(nc, "lds2")
        self.sts = Sem(nc, "sts")
        self.sts2 = Sem(nc, "sts2")
        self.pe = Sem(nc, "pe")
        self.ve = Sem(nc, "ve")
        self.se = Sem(nc, "se")
        self.ge = Sem(nc, "ge")
        self.cc = Sem(nc, "cc")
        self.ph = Sem(nc, "ph")
        self.nbar = 0

    def barrier(self, block):
        self.nbar += 1
        target = 5 * self.nbar
        h = self.ph.h
        self.ph.v = target
        pend = {"sync": [(self.ldh.h, self.ldh.v), (self.ldh2.h, self.ldh2.v),
                         (self.sts.h, self.sts.v), (self.sts2.h, self.sts2.v)],
                "scalar": [(self.sth.h, self.sth.v), (self.sth2.h, self.sth2.v)],
                "gpsimd": [(self.lds.h, self.lds.v), (self.lds2.h, self.lds2.v)]}
        for name, reg in (("sync", block.sync), ("scalar", block.scalar),
                          ("vector", block.vector), ("gpsimd", block.gpsimd),
                          ("tensor", block.tensor)):
            def body(e, h=h, target=target, p=pend.get(name)):
                for ph_, pv_ in (p or []):
                    e.wait_ge(ph_, pv_)
                e.sem_inc(h, 1)
                e.wait_ge(h, target)
            reg(body)


def _apply_ln_batched(e, X, src, mu, rstd, o32, obf, sl, th, residual,
                      tt_=TT):
    hw = tt_ // 2
    hs = slice(th * hw, (th + 1) * hw)
    mu_b = mu[:, hs].unsqueeze(1).to_broadcast([128, NCH, hw])
    rs_b = rstd[:, hs].unsqueeze(1).to_broadcast([128, NCH, hw])
    if not residual:
        xin = src[:, :, sl][:, :, hs]
        e.tensor_sub(o32[:, :, hs], xin, mu_b)
        e.drain()
        e.tensor_mul(o32[:, :, hs], o32[:, :, hs], rs_b)
        e.drain()
        if obf is not None:
            e.tensor_copy(obf[:, :, hs], o32[:, :, hs])
            e.drain()
    else:
        e.tensor_sub(o32[:, :, hs], src[:, :, hs], mu_b)
        e.drain()
        e.tensor_mul(o32[:, :, hs], o32[:, :, hs], rs_b)
        e.drain()
        Xs = X[:, :, sl]
        e.tensor_add(Xs[:, :, hs], Xs[:, :, hs], o32[:, :, hs])
        e.drain()


def ln_pre_phase(kb, block, bufs, dst_bf=None, dst_f32=None, want_max=False):
    nc, T, NT = kb.nc, kb.T, kb.NT
    X, ones = bufs["X"], bufs["ones128"]
    VCH = [0, 1, 2]
    GCH = [3, 4, 5, 6, 7]

    with (
        nc.sbuf_tensor(_u("ln_xb"), [128, NCH, TT], BF16) as xb,
        nc.sbuf_tensor(_u("ln_xsq"), [128, NCH, TT], BF16) as xsq,
        nc.sbuf_tensor(_u("ln_mu"), [128, TT], F32) as mu,
        nc.sbuf_tensor(_u("ln_es"), [128, TT], F32) as es,
        nc.sbuf_tensor(_u("ln_rstd"), [128, TT], F32) as rstd,
        nc.sbuf_tensor(_u("ln_o32"), [128, NCH, TT], F32) as o32,
        nc.sbuf_tensor(_u("ln_obf"), [128, NCH, TT], BF16) as obf,
        nc.sbuf_tensor(_u("ln_mxc"), [128, 1], F32) as mxc,
        nc.sbuf_tensor(_u("ln_mxt"), [128, 1], F32) as mxt,
        nc.psum_tensor(_u("ln_psA"), [128, TT], F32) as psA,
        nc.psum_tensor(_u("ln_psB"), [128, TT], F32) as psB,
    ):
        ve0, pe0, se0, ge0, ldh0 = kb.ve.v, kb.pe.v, kb.se.v, kb.ge.v, kb.ldh.v
        nst = (1 if dst_bf is not None else 0) + (1 if dst_f32 is not None else 0)
        VT = 5 if want_max else 4

        def st_done(i):
            return ldh0 + 16 * nst * (i + 1)

        def vec(e):
            for i in range(NT):
                sl = slice(i * TT, (i + 1) * TT)
                e.tensor_copy(xb.ap(), X[:, :, sl])
                e.drain()
                e.tensor_tensor(xsq.ap(), xb.ap(), xb.ap(), AL.mult)
                kb.ve.tag(e.drain())
                e.wait_ge(kb.pe.h, pe0 + i + 1)
                if i > 0:
                    e.wait_ge(kb.ge.h, ge0 + i)
                e.tensor_scalar(mu[:], psA[:], 1.0 / D, None, AL.mult)
                e.drain()
                e.tensor_scalar(es[:], psB[:], 1.0 / D, None, AL.mult)
                e.drain()
                e.tensor_tensor(rstd[:], mu[:], mu[:], AL.mult)
                e.drain()
                e.tensor_sub(es[:], es[:], rstd[:])
                kb.ve.tag(e.drain())
                e.wait_ge(kb.se.h, se0 + i + 1)
                e.reciprocal(rstd[:], es[:])
                kb.ve.tag(e.drain())
                if i > 0 and nst:
                    e.wait_ge(kb.ldh.h, st_done(i - 1))
                _apply_ln_batched(e, None, X, mu, rstd, o32, obf, sl, 0,
                                  residual=False)
                kb.ve.tag(e.drain())
                if want_max:
                    e.wait_ge(kb.ge.h, ge0 + i + 1)
                    for dc in range(NCH):
                        e.reduce_max(mxt[:], o32[:, dc, :], axis=AX.X)
                        e.drain()
                        if i == 0 and dc == 0:
                            e.tensor_copy(mxc[:], mxt[:])
                        else:
                            e.tensor_max(mxc[:], mxc[:], mxt[:])
                        e.drain()
                    kb.ve.tag(e.drain())

        def sca(e):
            for i in range(NT):
                e.wait_ge(kb.ve.h, ve0 + VT * i + 2)
                e.activation(es[:], es[:], AF.Sqrt, bias=bufs["ceps"][:], scale=1.0)
                kb.se.tag(e.drain())

        def ten(e):
            for i in range(NT):
                e.wait_ge(kb.ve.h, ve0 + VT * i + 1)
                for dc in range(NCH):
                    e.matmul(psA[:], ones[:], xb[:, dc, :], start=(dc == 0),
                             stop=(dc == NCH - 1))
                last = None
                for dc in range(NCH):
                    last = e.matmul(psB[:], ones[:], xsq[:, dc, :],
                                    start=(dc == 0), stop=(dc == NCH - 1))
                kb.pe.tag(last)

        def gp(e):
            for i in range(NT):
                sl = slice(i * TT, (i + 1) * TT)
                e.wait_ge(kb.ve.h, ve0 + VT * i + 3)
                if i > 0 and nst:
                    e.wait_ge(kb.ldh.h, st_done(i - 1))
                if i > 0 and want_max:
                    e.wait_ge(kb.ve.h, ve0 + VT * i)
                _apply_ln_batched(e, None, X, mu, rstd, o32, obf, sl, 1,
                                  residual=False)
                kb.ge.tag(e.drain())

        def st(e):
            for i in range(NT):
                sl_lo = i * TT
                e.wait_ge(kb.ve.h, ve0 + VT * i + 4)
                e.wait_ge(kb.ge.h, ge0 + i + 1)
                if dst_bf is not None:
                    dst = dst_bf.ap()[:, sl_lo:sl_lo + TT]
                    dst = dst.rearrange("(c p) t -> p c t", p=128)
                    kb.ldh.tag(e.dma_start(out=dst, in_=obf.ap()), 16)
                if dst_f32 is not None:
                    dst = dst_f32.ap()[:, sl_lo:sl_lo + TT]
                    dst = dst.rearrange("(c p) t -> p c t", p=128)
                    kb.ldh.tag(e.dma_start(out=dst, in_=o32.ap()), 16)
            if nst:
                e.wait_ge(kb.ldh.h, kb.ldh.v)
            assert kb.ldh.v == st_done(NT - 1), (kb.ldh.v, st_done(NT - 1))

        block.vector(vec)
        block.scalar(sca)
        block.tensor(ten)
        block.gpsimd(gp)
        block.sync(st)

        if want_max:
            def gmax(e):
                e.wait_ge(kb.ve.h, ve0 + VT * NT)
                e.tensor_reduce(bufs["mx1"][:], mxc[:], AX.C, AL.max)
                kb.ge.tag(e.drain())
            block.gpsimd(gmax)

        kb.barrier(block)


def simple_proj_phase(kb, block, bufs, w_dram, src_dram, dst_dram, func,
                      scale, out_dtype, w_rows=D, jobs=None):
    nc, T, NT = kb.nc, kb.T, kb.NT
    KC = w_rows // 128
    FC = D // 128
    if jobs is None:
        jobs = [(w_dram, dst_dram, func, scale, out_dtype)]
    NJ = len(jobs)
    with ExitStack() as ctx:
        wt = [ctx.enter_context(
            nc.sbuf_tensor(_u("pj_wt"), [128, KC, D], BF16)) for _ in jobs]
        ob = [ctx.enter_context(
            nc.sbuf_tensor(_u("pj_ob"), [128, FC, TT], j[4])) for j in jobs]
        mv0 = ctx.enter_context(
            nc.sbuf_tensor(_u("pj_mv0"), [128, KC, TT], BF16))
        mv1 = ctx.enter_context(
            nc.sbuf_tensor(_u("pj_mv1"), [128, KC, TT], BF16))
        psA = ctx.enter_context(nc.psum_tensor(_u("pj_psA"), [128, TT], F32))
        psB = ctx.enter_context(nc.psum_tensor(_u("pj_psB"), [128, TT], F32))

        mv = [mv0, mv1]
        ps = [psA, psB]
        L = [kb.ldh, kb.ldh2]
        S = [kb.sth, kb.sth2]
        ve0, pe0, se0, ge0, lds0 = (kb.ve.v, kb.pe.v, kb.se.v, kb.ge.v,
                                    kb.lds.v)
        l0 = [L[0].v, L[1].v]
        s0 = [S[0].v, S[1].v]

        def gld(e):
            for j, jb in enumerate(jobs):
                kb.lds.tag(e.dma_start(
                    out=wt[j].ap(),
                    in_=jb[0].ap().rearrange("(c p) f -> p c f", p=128)), 16)
        block.gpsimd(gld)

        def ld(e):
            for i in range(NT):
                if i >= 2:
                    e.wait_ge(kb.pe.h, pe0 + (i - 1) * FC * NJ)
                src = src_dram.ap()[:, i * TT:(i + 1) * TT]
                L[i % 2].tag(e.dma_start(
                    out=mv[i % 2].ap(),
                    in_=src.rearrange("(c p) t -> p c t", p=128)), 16)
        block.sync(ld)

        def ten(e):
            e.wait_ge(kb.lds.h, lds0 + 16 * NJ)
            g = 0
            for i in range(NT):
                e.wait_ge(L[i % 2].h, l0[i % 2] + 16 * (i // 2 + 1))
                for j in range(NJ):
                    for fc in range(FC):
                        if g >= 2:
                            e.wait_ge(kb.se.h, se0 + g - 1)
                        last = None
                        for kc in range(KC):
                            last = e.matmul(ps[g % 2][:],
                                            wt[j][:, kc, fc * 128:(fc + 1) * 128],
                                            mv[i % 2][:, kc, :],
                                            start=(kc == 0), stop=(kc == KC - 1))
                        kb.pe.tag(last)
                        g += 1
        block.tensor(ten)

        def sca(e):
            g = 0
            for i in range(NT):
                for j, jb in enumerate(jobs):
                    for fc in range(FC):
                        e.wait_ge(kb.pe.h, pe0 + g + 1)
                        if i > 0 and j == 0 and fc == 0:
                            p = (i - 1) % 2
                            e.wait_ge(S[p].h,
                                      s0[p] + 16 * NJ * ((i - 1) // 2 + 1))
                        e.activation(ob[j][:, fc, :], ps[g % 2][:], jb[2],
                                     bias=0.0, scale=jb[3])
                        kb.se.tag(e.drain())
                        g += 1
                for j, jb in enumerate(jobs):
                    dst = jb[1].ap()[:, i * TT:(i + 1) * TT]
                    dst = dst.rearrange("(c p) t -> p c t", p=128)
                    S[i % 2].tag(e.dma_start(out=dst, in_=ob[j].ap()), 16)
        block.scalar(sca)

        kb.barrier(block)


def v_proj_phase(kb, block, bufs, w_dram, src_dram, dst_dram, padded):
    nc, T = kb.nc, kb.T
    NTC = T // 128
    with (
        nc.sbuf_tensor(_u("vp_wt"), [128, NCH, D], BF16) as wt,
        nc.sbuf_tensor(_u("vp_mv0"), [128, NCH, 128], BF16) as mv0,
        nc.sbuf_tensor(_u("vp_mv1"), [128, NCH, 128], BF16) as mv1,
        nc.sbuf_tensor(_u("vp_ob0"), [128, 2, 512], BF16) as ob0,
        nc.sbuf_tensor(_u("vp_ob1"), [128, 2, 512], BF16) as ob1,
        nc.psum_tensor(_u("vp_psA"), [128, 512], F32) as psA,
        nc.psum_tensor(_u("vp_psB"), [128, 512], F32) as psB,
    ):
        mv = [mv0, mv1]
        ob = [ob0, ob1]
        ps = [psA, psB]
        L = [kb.ldh, kb.ldh2]
        S = [kb.sth, kb.sth2]
        ve0, pe0, se0, lds0 = kb.ve.v, kb.pe.v, kb.se.v, kb.lds.v
        l0 = [L[0].v, L[1].v]
        s0 = [S[0].v, S[1].v]

        def gld(e):
            kb.lds.tag(e.dma_start(
                out=wt.ap(),
                in_=w_dram.ap().rearrange("(c p) f -> p c f", p=128)), 16)
        block.gpsimd(gld)

        def ld(e):
            for i in range(NTC):
                if i >= 2:
                    e.wait_ge(kb.pe.h, pe0 + 2 * (i - 1))
                src = src_dram.ap()[:, i * 128:(i + 1) * 128]
                L[i % 2].tag(e.dma_start(
                    out=mv[i % 2].ap(),
                    in_=src.rearrange("(c p) t -> p c t", p=128)), 16)
        block.sync(ld)

        def ten(e):
            e.wait_ge(kb.lds.h, lds0 + 16)
            for i in range(NTC):
                e.wait_ge(L[i % 2].h, l0[i % 2] + 16 * (i // 2 + 1))
                if i >= 1:
                    e.wait_ge(kb.se.h, se0 + 2 * i)
                for fh in range(2):
                    last = None
                    for kc in range(NCH):
                        last = e.matmul(ps[fh][:],
                                        mv[i % 2][:, kc, :],
                                        wt[:, kc, fh * 512:(fh + 1) * 512],
                                        start=(kc == 0), stop=(kc == NCH - 1))
                    kb.pe.tag(last)
        block.tensor(ten)

        def sca(e):
            for i in range(NTC):
                if i >= 2:
                    j = (i - 2) % 2
                    e.wait_ge(S[j].h, s0[j] + 16 * ((i - 2) // 2 + 1))
                for fh in range(2):
                    e.wait_ge(kb.pe.h, pe0 + 2 * i + fh + 1)
                    e.activation(ob[i % 2][:, fh, :], ps[fh][:], AF.Copy,
                                 bias=0.0, scale=1.0)
                    kb.se.tag(e.drain())
                if padded:
                    dst = dst_dram.ap()[i * 128:(i + 1) * 128, :]
                    dst = dst.rearrange("p (h c) -> p h c", c=128)[:, :, 0:64]
                    srcap = ob[i % 2].ap().rearrange(
                        "p a t -> p (a t)").rearrange("p (h c) -> p h c", c=64)
                    S[i % 2].tag(e.dma_start(out=dst, in_=srcap), 16)
                else:
                    dst = dst_dram.ap()[i * 128:(i + 1) * 128, :]
                    S[i % 2].tag(e.dma_start(
                        out=dst, in_=ob[i % 2].ap().rearrange("p a t -> p (a t)")),
                        16)
        block.scalar(sca)

        kb.barrier(block)


def attn0_phase(kb, block, bufs, q_s, k_s, v_s, o_s):
    nc, T = kb.nc, kb.T
    NSEG = T // TT
    with (
        nc.sbuf_tensor(_u("at_qb0"), [128, TT], BF16) as qb0,
        nc.sbuf_tensor(_u("at_qb1"), [128, TT], BF16) as qb1,
        nc.sbuf_tensor(_u("at_kw0"), [128, 1024], BF16) as kw0,
        nc.sbuf_tensor(_u("at_kw1"), [128, 1024], BF16) as kw1,
        nc.sbuf_tensor(_u("at_vw0"), [128, 8, 256], BF16) as vw0,
        nc.sbuf_tensor(_u("at_vw1"), [128, 8, 256], BF16) as vw1,
        nc.sbuf_tensor(_u("at_pb0"), [128, 8, 2, TT], BF16) as pb0,
        nc.sbuf_tensor(_u("at_pb1"), [128, 8, 2, TT], BF16) as pb1,
        nc.sbuf_tensor(_u("at_msk"), [128, 8, TT], BF16) as msk,
        nc.sbuf_tensor(_u("at_ob"), [128, 8, TT], BF16) as obuf,
        nc.sbuf_tensor(_u("at_rc"), [128, TT], F32) as rc,
        nc.sbuf_tensor(_u("at_rc2"), [128, TT], F32) as rc2,
        nc.psum_tensor(_u("at_sc0"), [128, 2, TT], F32) as sc0,
        nc.psum_tensor(_u("at_sc1"), [128, 2, TT], F32) as sc1,
        nc.psum_tensor(_u("at_pOA"), [128, TT], F32) as pOA,
        nc.psum_tensor(_u("at_pOB"), [128, TT], F32) as pOB,
        nc.psum_tensor(_u("at_pSA"), [128, TT], F32) as pSA,
        nc.psum_tensor(_u("at_pSB"), [128, TT], F32) as pSB,
    ):
        qb = [qb0, qb1]
        kw = [kw0, kw1]
        vw = [vw0, vw1]
        pb = [pb0, pb1]
        sc = [sc0, sc1]
        L = [kb.ldh, kb.ldh2]
        S = [kb.sth, kb.sth2]
        ones = bufs["ones128"]

        def kcs_of(s):
            return list(range(4, 8)) if s == 0 else list(range(8))

        DVEI = (1, 3, 5, 7)

        def dve_idx(s):
            return [i for i in range(len(kcs_of(s))) if i in DVEI]

        def pool_idx(s):
            return [i for i in range(len(kcs_of(s))) if i not in DVEI]

        groups = [(s, hp) for s in range(NSEG) for hp in range(8)]
        NG = len(groups)
        ve0, pe0, se0, ge0 = kb.ve.v, kb.pe.v, kb.se.v, kb.ge.v
        l0 = [L[0].v, L[1].v]
        s0 = [S[0].v, S[1].v]

        def mk(e):
            for kc in range(8):
                e.affine_select(msk[:, kc, :], ones[:, 0:1].to_broadcast(
                    [128, TT]), [[1, TT]], AL.is_ge, 0.0,
                    base=512 - kc * 128, channel_multiplier=-1)
                e.drain()
                e.affine_select(msk[:, kc, :], msk[:, kc, :], [[-1, TT]],
                                AL.is_ge, 0.0, base=kc * 128 - 1,
                                channel_multiplier=1)
                e.drain()
            kb.ge.tag(e.drain())
        block.gpsimd(mk)
        ge0 += 1

        pe_sc = {}
        pe_pv = {}
        v = pe0
        for g, (s, hp) in enumerate(groups):
            for kci in range(len(kcs_of(s))):
                v += 1
                pe_sc[(g, kci)] = v
                if kci == 1 and g >= 1:
                    v += 1
                    pe_pv[g - 1] = v
        v += 1
        pe_pv[NG - 1] = v
        se_t = {}
        se_sg = {}
        v = se0
        for g, (s, hp) in enumerate(groups):
            for kci in range(len(kcs_of(s))):
                v += 1
                se_t[(g, kci)] = v
                if kci == 1 and g >= 1:
                    v += 1
                    se_sg[g - 1] = v
        v += 1
        se_sg[NG - 1] = v
        ge_m = {}
        v = ge0
        for g, (s, hp) in enumerate(groups):
            for kci in pool_idx(s):
                v += 1
                ge_m[(g, kci)] = v
        ve_m = {}
        ve_r = {}
        v = ve0
        for g, (s, hp) in enumerate(groups):
            for kci in dve_idx(s):
                v += 1
                ve_m[(g, kci)] = v
            v += 1
            ve_r[g] = v

        def last_same_parity(g, kci):
            n = len(kcs_of(groups[g][0]))
            j = n - 1
            while j % 2 != kci % 2:
                j -= 1
            return j

        def ld_done(g):
            return l0[g % 2] + 48 * (g // 2 + 1)
        def st_done(s):
            return s0[s % 2] + 16 * (s // 2 + 1)

        def ld2(e):
            for g, (s, hp) in enumerate(groups):
                if g >= 2:
                    e.wait_ge(kb.pe.h, pe_pv[g - 2])
                Q0 = s * TT
                kl = Q0 - 512
                L[g % 2].tag(e.dma_start(
                    out=qb[g % 2][:, :],
                    in_=q_s.ap()[hp * 128:(hp + 1) * 128, Q0:Q0 + TT]), 16)
                if s == 0:
                    L[g % 2].tag(e.dma_start(
                        out=kw[g % 2][:, 512:1024],
                        in_=k_s.ap()[hp * 128:(hp + 1) * 128, 0:512]), 16)
                    vsrc = v_s.ap()[0:512, hp * 256:(hp + 1) * 256]
                    L[g % 2].tag(e.dma_start(
                        out=vw[g % 2][:, 4:8, :],
                        in_=vsrc.rearrange("(c p) f -> p c f", p=128)), 16)
                else:
                    L[g % 2].tag(e.dma_start(
                        out=kw[g % 2][:, :],
                        in_=k_s.ap()[hp * 128:(hp + 1) * 128, kl:kl + 1024]),
                        16)
                    vsrc = v_s.ap()[kl:kl + 1024, hp * 256:(hp + 1) * 256]
                    L[g % 2].tag(e.dma_start(
                        out=vw[g % 2][:, :, :],
                        in_=vsrc.rearrange("(c p) f -> p c f", p=128)), 16)
        block.sync(ld2)

        def emit_pv(e, g):
            s, hp = groups[g]
            kcs = kcs_of(s)
            e.wait_ge(kb.ge.h, ge_m[(g, pool_idx(s)[-1])])
            e.wait_ge(kb.ve.h, ve_m[(g, dve_idx(s)[-1])])
            if g >= 1:
                e.wait_ge(kb.ve.h, ve_r[g - 1])
                e.wait_ge(kb.se.h, se_sg[g - 1])
            for kci, kc in enumerate(kcs):
                first, last_f = (kci == 0), (kci == len(kcs) - 1)
                e.matmul(pOA[0:64, :], vw[g % 2][:, kc, 0:64],
                         pb[g % 2][:, kc, 0, :], start=first,
                         stop=last_f, tile_position=(0, 0))
                e.matmul(pOB[64:128, :], vw[g % 2][:, kc, 128:192],
                         pb[g % 2][:, kc, 1, :], start=first,
                         stop=last_f, tile_position=(0, 64))
                e.matmul(pSA[0:64, :], ones[:, 0:64],
                         pb[g % 2][:, kc, 0, :], start=first,
                         stop=last_f, tile_position=(0, 0))
                lastmm = e.matmul(pSB[64:128, :], ones[:, 0:64],
                                  pb[g % 2][:, kc, 1, :], start=first,
                                  stop=last_f, tile_position=(0, 64))
            kb.pe.tag(lastmm)
            assert kb.pe.v == pe_pv[g]

        def ten(e):
            for g, (s, hp) in enumerate(groups):
                kcs = kcs_of(s)
                e.wait_ge(L[g % 2].h, ld_done(g))
                for kci, kc in enumerate(kcs):
                    if kci >= 2:
                        e.wait_ge(kb.se.h, se_t[(g, kci - 2)])
                    elif g >= 1:
                        e.wait_ge(kb.se.h,
                                  se_t[(g - 1, last_same_parity(g - 1, kci))])
                    st_a = kw[g % 2][0:64, kc * 128:(kc + 1) * 128]
                    st_b = kw[g % 2][64:128, kc * 128:(kc + 1) * 128]
                    e.matmul(sc[kci % 2][:, 0, :], st_a, qb[g % 2][0:64, :],
                             start=True, stop=True, tile_position=(0, 0))
                    last = e.matmul(sc[kci % 2][:, 1, :], st_b,
                                    qb[g % 2][64:128, :], start=True,
                                    stop=True, tile_position=(64, 0))
                    kb.pe.tag(last)
                    assert kb.pe.v == pe_sc[(g, kci)]
                    if kci == 1 and g >= 1:
                        emit_pv(e, g - 1)
            emit_pv(e, NG - 1)
        block.tensor(ten)

        def emit_sigma(e, g):
            s, hp = groups[g]
            e.wait_ge(kb.pe.h, pe_pv[g])
            if g >= 1:
                e.wait_ge(kb.ve.h, ve_r[g - 1])
            e.activation(rc[0:64, :], pSA[0:64, :], AF.Ln, bias=0.0,
                         scale=1.0)
            e.drain()
            e.activation(rc[64:128, :], pSB[64:128, :], AF.Ln, bias=0.0,
                         scale=1.0)
            e.drain()
            e.activation(rc2[0:64, :], rc[0:64, :], AF.Exp, bias=0.0,
                         scale=-1.0)
            e.drain()
            e.activation(rc2[64:128, :], rc[64:128, :], AF.Exp, bias=0.0,
                         scale=-1.0)
            kb.se.tag(e.drain())
            assert kb.se.v == se_sg[g]
            if hp == 7:
                e.wait_ge(kb.ve.h, ve_r[g])
                dst = o_s.ap()[:, s * TT:(s + 1) * TT]
                dst = dst.rearrange("(c p) t -> p c t", p=128)
                S[s % 2].tag(e.dma_start(out=dst, in_=obuf.ap()), 16)

        def sca(e):
            for g, (s, hp) in enumerate(groups):
                kcs = kcs_of(s)
                if g >= 2:
                    e.wait_ge(kb.pe.h, pe_pv[g - 2])
                for kci, kc in enumerate(kcs):
                    e.wait_ge(kb.pe.h, pe_sc[(g, kci)])
                    e.activation(pb[g % 2][:, kc, :, :], sc[kci % 2].ap(),
                                 AF.Exp, bias=0.0, scale=0.125)
                    kb.se.tag(e.drain())
                    assert kb.se.v == se_t[(g, kci)]
                    if kci == 1 and g >= 1:
                        emit_sigma(e, g - 1)
            emit_sigma(e, NG - 1)
        block.scalar(sca)

        def gp(e):
            e.wait_ge(kb.ge.h, ge0)
            for g, (s, hp) in enumerate(groups):
                kcs = kcs_of(s)
                for kci in pool_idx(s):
                    kc = kcs[kci]
                    e.wait_ge(kb.se.h, se_t[(g, kci)])
                    e.tensor_mul(pb[g % 2][:, kc, :, :],
                                 pb[g % 2][:, kc, :, :],
                                 msk[:, kc, :].unsqueeze(1).to_broadcast(
                                     [128, 2, TT]))
                    kb.ge.tag(e.drain())
                    assert kb.ge.v == ge_m[(g, kci)]
        block.gpsimd(gp)

        def vec(e):
            e.wait_ge(kb.ge.h, ge0)
            for g, (s, hp) in enumerate(groups):
                kcs = kcs_of(s)
                for kci in dve_idx(s):
                    kc = kcs[kci]
                    e.wait_ge(kb.se.h, se_t[(g, kci)])
                    e.tensor_mul(pb[g % 2][:, kc, :, :],
                                 pb[g % 2][:, kc, :, :],
                                 msk[:, kc, :].unsqueeze(1).to_broadcast(
                                     [128, 2, TT]))
                    kb.ve.tag(e.drain())
                    assert kb.ve.v == ve_m[(g, kci)]
                e.wait_ge(kb.se.h, se_sg[g])
                if hp == 0 and s >= 1:
                    e.wait_ge(S[(s - 1) % 2].h, st_done(s - 1))
                e.tensor_mul(obuf[0:64, hp, :], pOA[0:64, :], rc2[0:64, :])
                e.drain()
                e.tensor_mul(obuf[64:128, hp, :], pOB[64:128, :],
                             rc2[64:128, :])
                kb.ve.tag(e.drain())
                assert kb.ve.v == ve_r[g]
        block.vector(vec)

        kb.barrier(block)


def proj_postln_phase(kb, block, bufs, src_dram, KC, w_dram=None,
                      w_bf_dram=None, evac_scale=1.0, evac_scale_col=None,
                      evac_bias_col=None, w_tiles_dram=None, tt=TT):
    nc, T = kb.nc, kb.T
    NT = T // tt
    X, ones = bufs["X"], bufs["ones128"]
    FC = NCH
    VCH = [0, 1, 2]
    GCH = [3, 4, 5, 6, 7]
    resident = w_dram is not None
    with (
        nc.sbuf_tensor(_u("pp_wt"), [128, KC, D if resident else 256], BF16) as wt,
        nc.sbuf_tensor(_u("pp_mv0"), [128, KC, tt], BF16) as mv0,
        nc.sbuf_tensor(_u("pp_mv1"), [128, KC, tt], BF16) as mv1,
        nc.sbuf_tensor(_u("pp_y"), [128, NCH, tt], F32) as y32,
        nc.sbuf_tensor(_u("pp_yb"), [128, NCH, tt], BF16) as ybf,
        nc.sbuf_tensor(_u("pp_ys"), [128, NCH, tt], BF16) as ysq,
        nc.sbuf_tensor(_u("pp_mu"), [128, tt], F32) as mu,
        nc.sbuf_tensor(_u("pp_es"), [128, tt], F32) as es,
        nc.sbuf_tensor(_u("pp_rstd"), [128, tt], F32) as rstd,
        nc.psum_tensor(_u("pp_pY0"), [128, tt], F32) as pY0,
        nc.psum_tensor(_u("pp_pY1"), [128, tt], F32) as pY1,
        nc.psum_tensor(_u("pp_psA"), [128, tt], F32) as psA,
        nc.psum_tensor(_u("pp_psB"), [128, tt], F32) as psB,
    ):
        mv = [mv0, mv1]
        pY = [pY0, pY1]
        L = [kb.ldh, kb.ldh2]
        S = [kb.sth, kb.sth2]
        LD = [kb.lds, kb.lds2]
        ve0, pe0, se0, ge0, lds0 = (kb.ve.v, kb.pe.v, kb.se.v, kb.ge.v,
                                    kb.lds.v)
        l0 = [L[0].v, L[1].v]
        s0 = [S[0].v, S[1].v]
        d0 = [LD[0].v, LD[1].v]

        if resident:
            def gld(e):
                kb.lds.tag(e.dma_start(
                    out=wt.ap()[:, :, 0:D],
                    in_=w_dram.ap().rearrange("(c p) f -> p c f", p=128)), 16)
            block.gpsimd(gld)

        def ld(e):
            for i in range(NT):
                if i >= 2:
                    e.wait_ge(kb.pe.h, pe0 + (i - 2) * (FC + 1) + FC)
                src = src_dram.ap()[:, i * tt:(i + 1) * tt]
                L[i % 2].tag(e.dma_start(
                    out=mv[i % 2].ap(),
                    in_=src.rearrange("(c p) t -> p c t", p=128)), 16)
        block.sync(ld)

        def gidx(i, fc):
            return i * FC + fc

        def pe_proj(i, fc):
            return pe0 + i * (FC + 1) + fc + 1

        def pe_stats(i):
            return pe0 + (i + 1) * (FC + 1)

        def wst_done(g):
            return (LD[g % 2].h, d0[g % 2] + 16 * (g // 2 + 1))

        def pe_proj_g(g):
            return pe0 + (g // FC) * (FC + 1) + (g % FC) + 1

        def emit_wload(e, g):
            ii, ff = divmod(g, FC)
            if g >= 2:
                e.wait_ge(LD[g % 2].h, d0[g % 2] + 16 * ((g - 2) // 2 + 1))
                e.wait_ge(kb.pe.h, pe_proj_g(g - 2))
            srcw = w_bf_dram.ap()[:, ff * 128:(ff + 1) * 128]
            srcw = srcw.rearrange("(c p) f -> p c f", p=128)
            LD[g % 2].tag(e.dma_start(
                out=wt[:, :, (g % 2) * 128:(g % 2) * 128 + 128], in_=srcw), 16)

        def sca(e):
            for i in range(NT):
                for fc in range(FC):
                    g = gidx(i, fc)
                    e.wait_ge(kb.pe.h, pe_proj(i, fc))
                    if fc == 0 and i > 0:
                        e.wait_ge(kb.ve.h, ve0 + 4 * i)
                        e.wait_ge(kb.ge.h, ge0 + i)
                    if evac_scale_col is not None:
                        e.activation(y32[:, fc, :], pY[g % 2][:],
                                     AF.Identity, bias=evac_bias_col[:, fc:fc + 1],
                                     scale=evac_scale_col[:, fc:fc + 1])
                    else:
                        e.activation(y32[:, fc, :], pY[g % 2][:],
                                     AF.Identity, bias=0.0, scale=evac_scale)
                    kb.se.tag(e.drain())
                e.wait_ge(kb.ve.h, ve0 + 4 * i + 2)
                e.activation(es[:], es[:], AF.Sqrt, bias=bufs["ceps"][:],
                             scale=1.0)
                kb.se.tag(e.drain())
        block.scalar(sca)

        def ten(e):
            if resident:
                e.wait_ge(kb.lds.h, lds0 + 16)
            for i in range(NT):
                e.wait_ge(L[i % 2].h, l0[i % 2] + 16 * (i // 2 + 1))
                for fc in range(FC):
                    g = gidx(i, fc)
                    if not resident:
                        e.wait_ge(*wst_done(g))
                    if g >= 2:
                        gp_ = g - 2
                        tgt = se0 + (gp_ // FC) * (FC + 1) + (gp_ % FC) + 1
                        e.wait_ge(kb.se.h, tgt)
                    last = None
                    for kc in range(KC):
                        stat = (wt[:, kc, fc * 128:(fc + 1) * 128] if resident
                                else wt[:, kc, (g % 2) * 128:(g % 2) * 128 + 128])
                        last = e.matmul(pY[g % 2][:], stat, mv[i % 2][:, kc, :],
                                        start=(kc == 0), stop=(kc == KC - 1))
                    kb.pe.tag(last)
                    assert kb.pe.v == pe_proj(i, fc)
                e.wait_ge(kb.ve.h, ve0 + 4 * i + 1)
                for dc in range(NCH):
                    e.matmul(psA[:], ones[:, 0:128], ybf[:, dc, :],
                             start=(dc == 0), stop=(dc == NCH - 1))
                last = None
                for dc in range(NCH):
                    last = e.matmul(psB[:], ones[:, 0:128], ysq[:, dc, :],
                                    start=(dc == 0), stop=(dc == NCH - 1))
                kb.pe.tag(last)
                assert kb.pe.v == pe_stats(i)
        block.tensor(ten)

        def vec(e):
            for i in range(NT):
                sl = slice(i * tt, (i + 1) * tt)
                e.wait_ge(kb.se.h, se0 + i * (FC + 1) + FC)
                e.tensor_copy(ybf.ap(), y32.ap())
                e.drain()
                e.tensor_tensor(ysq.ap(), ybf.ap(), ybf.ap(), AL.mult)
                kb.ve.tag(e.drain())
                e.wait_ge(kb.pe.h, pe_stats(i))
                if i > 0:
                    e.wait_ge(kb.ge.h, ge0 + i)
                e.tensor_scalar(mu[:], psA[:], 1.0 / D, None, AL.mult)
                e.drain()
                e.tensor_scalar(es[:], psB[:], 1.0 / D, None, AL.mult)
                e.drain()
                e.tensor_tensor(rstd[:], mu[:], mu[:], AL.mult)
                e.drain()
                e.tensor_sub(es[:], es[:], rstd[:])
                kb.ve.tag(e.drain())
                e.wait_ge(kb.se.h, se0 + i * (FC + 1) + FC + 1)
                e.reciprocal(rstd[:], es[:])
                kb.ve.tag(e.drain())
                _apply_ln_batched(e, X, y32, mu, rstd, y32, None, sl, 0,
                                  residual=True, tt_=tt)
                kb.ve.tag(e.drain())

        block.vector(vec)

        def gp(e):
            wp = 0
            if not resident:
                while wp < min(FC + 2, NT * FC):
                    emit_wload(e, wp)
                    wp += 1
            for i in range(NT):
                sl = slice(i * tt, (i + 1) * tt)
                e.wait_ge(kb.ve.h, ve0 + 4 * i + 3)
                _apply_ln_batched(e, X, y32, mu, rstd, y32, None, sl, 1,
                                  residual=True, tt_=tt)
                kb.ge.tag(e.drain())
                if not resident:
                    while wp < min((i + 2) * FC + 2, NT * FC):
                        emit_wload(e, wp)
                        wp += 1
        block.gpsimd(gp)

        kb.barrier(block)


def cast_weights_phase(kb, block, bufs, pairs):
    nc = kb.nc
    jobs = []
    for pr in pairs:
        src, dst = pr[0], pr[1]
        i8 = len(pr) > 2 and pr[2]
        R, C = src.ap().shape
        for r in range(R // 128):
            jobs.append((src, dst, r, C, i8))
    CMAX = max(j[3] for j in jobs)
    with (
        nc.sbuf_tensor(_u("cw_b0"), [128, CMAX], BF16) as b0,
        nc.sbuf_tensor(_u("cw_b1"), [128, CMAX], BF16) as b1,
        nc.sbuf_tensor(_u("cw_c0"), [128, CMAX], I8) as c0,
        nc.sbuf_tensor(_u("cw_c1"), [128, CMAX], I8) as c1,
    ):
        bb = [b0, b1]
        cb = [c0, c1]
        Ls = [kb.lds, kb.lds2]
        Ss = [kb.sth, kb.sth2]
        l0 = [Ls[0].v, Ls[1].v]
        s0 = [Ss[0].v, Ss[1].v]

        def gld(e):
            for j, (src, dst, r, C, i8) in enumerate(jobs):
                if j >= 2:
                    e.wait_ge(Ss[j % 2].h, s0[j % 2] + 16 * ((j - 2) // 2 + 1))
                buf = cb[j % 2] if i8 else bb[j % 2]
                Ls[j % 2].tag(e.dma_start(
                    out=buf[:, 0:C],
                    in_=src.ap()[r * 128:(r + 1) * 128, :]), 16)
        block.gpsimd(gld)

        def st(e):
            for j, (src, dst, r, C, i8) in enumerate(jobs):
                e.wait_ge(Ls[j % 2].h, l0[j % 2] + 16 * (j // 2 + 1))
                buf = cb[j % 2] if i8 else bb[j % 2]
                Ss[j % 2].tag(e.dma_start(
                    out=dst.ap()[r * 128:(r + 1) * 128, :],
                    in_=buf[:, 0:C]), 16)
        block.scalar(st)

        kb.barrier(block)


def swiglu_phase(kb, block, bufs, wg_b, wu_b, xn_s, h_s):
    nc, T = kb.nc, kb.T
    TH = min(T // 2, 1024)
    NPASS = T // TH
    SUB = min(512, TH)
    NSUB = TH // SUB
    HC = FF0 // 128
    with (
        nc.sbuf_tensor(_u("sw_xn0"), [128, NCH, TH], BF16) as xn0,
        nc.sbuf_tensor(_u("sw_xn1"), [128, NCH, TH], BF16) as xn1,
        nc.sbuf_tensor(_u("sw_wg0"), [128, NCH, 128], BF16) as wg0,
        nc.sbuf_tensor(_u("sw_wg1"), [128, NCH, 128], BF16) as wg1,
        nc.sbuf_tensor(_u("sw_wu0"), [128, NCH, 128], BF16) as wu0,
        nc.sbuf_tensor(_u("sw_wu1"), [128, NCH, 128], BF16) as wu1,
        nc.sbuf_tensor(_u("sw_t10"), [128, SUB], F32) as t10,
        nc.sbuf_tensor(_u("sw_t11"), [128, SUB], F32) as t11,
        nc.sbuf_tensor(_u("sw_hb0"), [128, SUB], BF16) as hb0,
        nc.sbuf_tensor(_u("sw_hb1"), [128, SUB], BF16) as hb1,
        nc.psum_tensor(_u("sw_pG0"), [128, SUB], F32) as pG0,
        nc.psum_tensor(_u("sw_pG1"), [128, SUB], F32) as pG1,
        nc.psum_tensor(_u("sw_pU0"), [128, SUB], F32) as pU0,
        nc.psum_tensor(_u("sw_pU1"), [128, SUB], F32) as pU1,
    ):
        xnb = [xn0, xn1]
        wg = [wg0, wg1]
        wu = [wu0, wu1]
        t1 = [t10, t11]
        hb = [hb0, hb1]
        pG = [pG0, pG1]
        pU = [pU0, pU1]
        L = [kb.ldh, kb.ldh2]
        S = [kb.sth, kb.sth2]
        ve0, pe0, se0, ge0 = kb.ve.v, kb.pe.v, kb.se.v, kb.ge.v
        l0 = [L[0].v, L[1].v]
        s0 = [S[0].v, S[1].v]

        ld_tgt = {}
        lv = [l0[0], l0[1]]
        for h in range(NPASS):
            p = (h * (HC + 1)) % 2
            lv[p] += 16
            ld_tgt[("xn", h)] = (p, lv[p])
            for hc in range(HC):
                p2 = (h * (HC + 1) + hc + 1) % 2
                lv[p2] += 32
                ld_tgt[("w", h, hc)] = (p2, lv[p2])

        NG = NPASS * HC * NSUB

        def pe_t(g):
            return pe0 + g + 1

        def ld(e):
            for h in range(NPASS):
                p, tgt = ld_tgt[("xn", h)]
                if h >= 1:
                    e.wait_ge(kb.pe.h, pe_t((h * HC - 1) * NSUB))
                if h >= 2:
                    e.wait_ge(kb.pe.h, pe_t(((h - 1) * HC) * NSUB - 1 + 1))
                src = xn_s.ap()[:, h * TH:(h + 1) * TH]
                L[p].tag(e.dma_start(
                    out=xnb[h % 2].ap(),
                    in_=src.rearrange("(c p) t -> p c t", p=128)), 16)
                for hc in range(HC):
                    gslab = h * HC + hc
                    if gslab < 2:
                        e.wait_ge(L[p].h, tgt)
                    if gslab >= 2:
                        e.wait_ge(kb.pe.h, pe_t((gslab - 1) * NSUB))
                    p2, tgt2 = ld_tgt[("w", h, hc)]
                    L[p2].tag(e.dma_start(
                        out=wg[gslab % 2].ap(),
                        in_=wg_b.ap()[:, hc * 128:(hc + 1) * 128].rearrange(
                            "(c p) f -> p c f", p=128)), 16)
                    L[p2].tag(e.dma_start(
                        out=wu[gslab % 2].ap(),
                        in_=wu_b.ap()[:, hc * 128:(hc + 1) * 128].rearrange(
                            "(c p) f -> p c f", p=128)), 16)
        block.sync(ld)

        def ten(e):
            for h in range(NPASS):
                p, tgt = ld_tgt[("xn", h)]
                e.wait_ge(L[p].h, tgt)
                for hc in range(HC):
                    gslab = h * HC + hc
                    p2, tgt2 = ld_tgt[("w", h, hc)]
                    e.wait_ge(L[p2].h, tgt2)
                    for ts in range(NSUB):
                        g = gslab * NSUB + ts
                        if g >= 2:
                            e.wait_ge(kb.se.h, se0 + g - 1)
                            e.wait_ge(kb.ve.h, ve0 + g - 1)
                        sl = slice(ts * SUB, (ts + 1) * SUB)
                        for kc in range(NCH):
                            e.matmul(pG[g % 2][:], wg[gslab % 2][:, kc, :],
                                     xnb[h % 2][:, kc, sl], start=(kc == 0),
                                     stop=(kc == NCH - 1))
                        last = None
                        for kc in range(NCH):
                            last = e.matmul(pU[g % 2][:], wu[gslab % 2][:, kc, :],
                                            xnb[h % 2][:, kc, sl], start=(kc == 0),
                                            stop=(kc == NCH - 1))
                        kb.pe.tag(last)
                        assert kb.pe.v == pe_t(g)
        block.tensor(ten)

        def emit_store(e, g):
            gslab, ts = divmod(g, NSUB)
            h, hc = divmod(gslab, HC)
            e.wait_ge(kb.ve.h, ve0 + g + 1)
            t_lo = h * TH + ts * SUB
            S[g % 2].tag(e.dma_start(
                out=h_s.ap()[hc * 128:(hc + 1) * 128, t_lo:t_lo + SUB],
                in_=hb[g % 2][:, :]), 16)

        def sca(e):
            for g in range(NG):
                e.wait_ge(kb.pe.h, pe_t(g))
                if g >= 2:
                    e.wait_ge(kb.ve.h, ve0 + g - 1)
                e.activation(t1[g % 2][:], pG[g % 2][:], AF.Silu, bias=0.0,
                             scale=INV127)
                kb.se.tag(e.drain())
                if g >= 1:
                    emit_store(e, g - 1)
            emit_store(e, NG - 1)
        block.scalar(sca)

        def vec(e):
            for g in range(NG):
                e.wait_ge(kb.se.h, se0 + g + 1)
                if g >= 2:
                    j = (g - 2) % 2
                    e.wait_ge(S[j].h, s0[j] + 16 * ((g - 2) // 2 + 1))
                e.scalar_tensor_tensor(hb[g % 2][:], pU[g % 2][:], INV127,
                                       t1[g % 2][:], AL.mult, AL.mult)
                kb.ve.tag(e.drain())
        block.vector(vec)

        kb.barrier(block)


def gate_phase(kb, block, bufs, wg1, wg2, xn_s, g_s):
    nc, T, NT = kb.nc, kb.T, kb.NT
    with (
        nc.sbuf_tensor(_u("gt_w1"), [128, NCH, 128], BF16) as w1t,
        nc.sbuf_tensor(_u("gt_w2"), [128, D], BF16) as w2t,
        nc.sbuf_tensor(_u("gt_mv0"), [128, NCH, TT], BF16) as mv0,
        nc.sbuf_tensor(_u("gt_mv1"), [128, NCH, TT], BF16) as mv1,
        nc.sbuf_tensor(_u("gt_g1"), [128, TT], BF16) as g1b,
        nc.sbuf_tensor(_u("gt_ob"), [128, NCH, TT], F32) as ob,
        nc.psum_tensor(_u("gt_p1"), [128, TT], F32) as p1,
        nc.psum_tensor(_u("gt_p20"), [128, TT], F32) as p20,
        nc.psum_tensor(_u("gt_p21"), [128, TT], F32) as p21,
    ):
        mv = [mv0, mv1]
        p2 = [p20, p21]
        L = [kb.ldh, kb.ldh2]
        S = [kb.sth, kb.sth2]
        ve0, pe0, se0, lds0 = kb.ve.v, kb.pe.v, kb.se.v, kb.lds.v
        l0 = [L[0].v, L[1].v]
        s0 = [S[0].v, S[1].v]

        def gld(e):
            kb.lds.tag(e.dma_start(
                out=w1t.ap(),
                in_=wg1.ap().rearrange("(c p) f -> p c f", p=128)), 16)
            kb.lds.tag(e.dma_start(out=w2t.ap(), in_=wg2.ap()), 16)
        block.gpsimd(gld)

        def pe_g1(i):
            return pe0 + i * 9 + 1

        def pe_g2(i, dc):
            return pe0 + i * 9 + 1 + dc + 1

        def se_ev(i):
            return se0 + i * 9 + 1

        def se_sg(i, dc):
            return se0 + i * 9 + 1 + dc + 1

        def ld(e):
            for i in range(NT):
                if i >= 2:
                    e.wait_ge(kb.pe.h, pe_g1(i - 1))
                src = xn_s.ap()[:, i * TT:(i + 1) * TT]
                L[i % 2].tag(e.dma_start(
                    out=mv[i % 2].ap(),
                    in_=src.rearrange("(c p) t -> p c t", p=128)), 16)
        block.sync(ld)

        def ten(e):
            e.wait_ge(kb.lds.h, lds0 + 32)
            for i in range(NT):
                e.wait_ge(L[i % 2].h, l0[i % 2] + 16 * (i // 2 + 1))
                if i >= 1:
                    e.wait_ge(kb.se.h, se_ev(i - 1))
                last = None
                for kc in range(NCH):
                    last = e.matmul(p1[:], w1t[:, kc, :], mv[i % 2][:, kc, :],
                                    start=(kc == 0), stop=(kc == NCH - 1))
                kb.pe.tag(last)
                e.wait_ge(kb.se.h, se_ev(i))
                for dc in range(NCH):
                    if i * 8 + dc >= 2:
                        gprev = i * 8 + dc - 2
                        ip, dp = divmod(gprev, 8)
                        e.wait_ge(kb.se.h, se_sg(ip, dp))
                    kb.pe.tag(e.matmul(p2[(i * 8 + dc) % 2][:],
                                       w2t[:, dc * 128:(dc + 1) * 128],
                                       g1b[:], start=True, stop=True))
        block.tensor(ten)

        def sca(e):
            for i in range(NT):
                e.wait_ge(kb.pe.h, pe_g1(i))
                e.activation(g1b[:], p1[:], AF.Copy, bias=0.0, scale=1.0)
                kb.se.tag(e.drain())
                for dc in range(NCH):
                    e.wait_ge(kb.pe.h, pe_g2(i, dc))
                    if i > 0 and dc == 0:
                        j = (i - 1) % 2
                        e.wait_ge(S[j].h, s0[j] + 16 * ((i - 1) // 2 + 1))
                    e.activation(ob[:, dc, :], p2[(i * 8 + dc) % 2][:],
                                 AF.Sigmoid, bias=0.0, scale=1.0)
                    kb.se.tag(e.drain())
                dst = g_s.ap()[:, i * TT:(i + 1) * TT]
                dst = dst.rearrange("(c p) t -> p c t", p=128)
                S[i % 2].tag(e.dma_start(out=dst, in_=ob.ap()), 16)
        block.scalar(sca)

        kb.barrier(block)


def decay_prep_phase(kb, block, bufs, f_s, b_s, c_s):
    nc, T = kb.nc, kb.T
    NC = T // 128
    S32 = T // 32
    with (
        nc.sbuf_tensor(_u("dp_f0"), [128, T], F32) as f0,
        nc.sbuf_tensor(_u("dp_f1"), [128, T], F32) as f1,
        nc.sbuf_tensor(_u("dp_b1"), [128, T], F32) as b1,
        nc.sbuf_tensor(_u("dp_b2"), [128, T // 2], F32) as b2,
        nc.sbuf_tensor(_u("dp_ub"), [128, T], BF16) as ub,
        nc.sbuf_tensor(_u("dp_ub2"), [128, T], BF16) as ub2,
        nc.sbuf_tensor(_u("dp_ss"), [128, S32], F32) as ss,
        nc.sbuf_tensor(_u("dp_ss2"), [128, S32], F32) as ss2,
        nc.sbuf_tensor(_u("dp_cc"), [128, S32], F32) as cc,
    ):
        fb = [f0, f1]
        L = [kb.ldh, kb.ldh2]
        S = [kb.sth, kb.sth2]
        ve0, pe0, se0, ge0 = kb.ve.v, kb.pe.v, kb.se.v, kb.ge.v
        lds0 = kb.lds.v
        l0 = [L[0].v, L[1].v]
        s0 = [S[0].v, S[1].v]

        def st_bnd(dc):
            return s0[dc % 2] + 16 * (dc // 2 + 1)

        def V(t):
            return t.ap().rearrange("p (s c) -> p s c", c=32)

        def V4(t):
            return t.ap().rearrange("p (n q c) -> p n q c", q=4, c=32)

        def ld(e):
            for dc in range(NCH):
                if dc >= 2:
                    e.wait_ge(L[dc % 2].h,
                              l0[dc % 2] + 16 * ((dc - 2) // 2 + 1))
                    e.wait_ge(kb.se.h, se0 + dc - 1)
                    e.wait_ge(kb.ve.h, ve0 + 3 * (dc - 2) + 1)
                L[dc % 2].tag(e.dma_start(
                    out=fb[dc % 2].ap(),
                    in_=f_s.ap()[dc * 128:(dc + 1) * 128, :]), 16)
        block.sync(ld)

        def sca(e):
            for dc in range(NCH):
                e.wait_ge(L[dc % 2].h, l0[dc % 2] + 16 * (dc // 2 + 1))
                if dc >= 1:
                    e.wait_ge(S[(dc - 1) % 2].h, st_bnd(dc - 1))
                e.activation(b1[:], fb[dc % 2][:], AF.Ln, bias=0.0, scale=1.0)
                kb.se.tag(e.drain())
                e.wait_ge(kb.ve.h, ve0 + 3 * dc + 2)
                S[dc % 2].tag(e.dma_start(
                    out=b_s.ap()[dc * 128:(dc + 1) * 128, :], in_=b1[:]), 16)
        block.scalar(sca)

        def gpc(e):
            for dc in range(NCH):
                if dc >= 1:
                    e.wait_ge(kb.lds.h, lds0 + 16 * dc)
                e.wait_ge(kb.ve.h, ve0 + 3 * dc + 3)
                kb.lds.tag(e.dma_start(
                    out=c_s.ap()[dc * 128:(dc + 1) * 128, :], in_=ub[:]), 16)
            e.wait_ge(kb.lds.h, kb.lds.v)
        block.gpsimd(gpc)

        def hs(e, src, dst, k, op):
            e.tensor_copy(V(dst)[:, :, 0:k], V(src)[:, :, 0:k])
            e.drain()
            e.tensor_tensor(V(dst)[:, :, k:32], V(src)[:, :, k:32],
                            V(src)[:, :, 0:32 - k], op)
            e.drain()

        def vec(e):
            T2 = T // 2
            S2 = T2 // 32
            NC2 = NC // 2

            def Vh(t, h):
                return t.ap()[:, h * T2:(h + 1) * T2].rearrange(
                    "p (s c) -> p s c", c=32)

            def V2(t):
                return t.ap().rearrange("p (s c) -> p s c", c=32)

            def hsh(e, sv, dv, k, op):
                e.tensor_copy(dv[:, :, 0:k], sv[:, :, 0:k])
                e.drain()
                e.tensor_tensor(dv[:, :, k:32], sv[:, :, k:32],
                                sv[:, :, 0:32 - k], op)
                e.drain()

            for dc in range(NCH):
                e.wait_ge(L[dc % 2].h, l0[dc % 2] + 16 * (dc // 2 + 1))
                if dc >= 1:
                    e.wait_ge(kb.lds.h, lds0 + 16 * dc)
                e.tensor_single_scalar(ub[:], fb[dc % 2][:], 0.5, AL.is_equal)
                e.drain()
                e.tensor_scalar(ub[:], ub[:], 2.0, -1.0, AL.mult, AL.add)
                kb.ve.tag(e.drain())
                e.wait_ge(kb.se.h, se0 + dc + 1)
                if dc >= 1:
                    e.wait_ge(S[(dc - 1) % 2].h, st_bnd(dc - 1))
                for h in range(2):
                    bh = Vh(b1, h)
                    b2v = V2(b2)
                    ping = [bh, b2v]
                    for k in (1, 2, 4, 8, 16):
                        hsh(e, ping[0], ping[1], k, AL.add)
                        ping = [ping[1], ping[0]]
                    e.tensor_copy(ss.ap()[:, 0:S2].unsqueeze(2),
                                  b2v[:, :, 31:32])
                    e.drain()
                    sv = ss.ap()[:, 0:S2].rearrange("p (n q) -> p n q", q=4)
                    s2v = ss2.ap()[:, 0:S2].rearrange("p (n q) -> p n q", q=4)
                    e.tensor_copy(s2v[:, :, 0:1], sv[:, :, 0:1])
                    e.drain()
                    e.tensor_tensor(s2v[:, :, 1:4], sv[:, :, 1:4],
                                    sv[:, :, 0:3], AL.add)
                    e.drain()
                    e.tensor_copy(sv[:, :, 0:2], s2v[:, :, 0:2])
                    e.drain()
                    e.tensor_tensor(sv[:, :, 2:4], s2v[:, :, 2:4],
                                    s2v[:, :, 0:2], AL.add)
                    e.drain()
                    b14 = b1.ap()[:, h * T2:(h + 1) * T2].rearrange(
                        "p (n q c) -> p n q c", q=4, c=32)
                    b24 = b2.ap().rearrange("p (n q c) -> p n q c", q=4, c=32)
                    e.tensor_copy(b14[:, :, 0, :], b24[:, :, 0, :])
                    e.drain()
                    e.tensor_tensor(b14[:, :, 1:4, :], b24[:, :, 1:4, :],
                                    sv[:, :, 0:3].unsqueeze(3).to_broadcast(
                                        [128, NC2, 3, 32]), AL.add)
                    e.drain()
                kb.ve.tag(e.drain())
                ping = [ub, ub2]
                for k in (1, 2, 4, 8, 16):
                    hsh(e, V(ping[0]), V(ping[1]), k, AL.mult)
                    ping = [ping[1], ping[0]]
                e.tensor_copy(ss.ap().unsqueeze(2), V(ub2)[:, :, 31:32])
                e.drain()
                e.tensor_tensor_scan(ss2[:], ss[:], ss[:], 1.0, AL.mult,
                                     AL.bypass)
                e.drain()
                e.memset(cc[:, 0:1], 1.0)
                e.drain()
                e.tensor_copy(cc[:, 1:S32], ss2[:, 0:S32 - 1])
                e.drain()
                e.tensor_tensor(V(ub), V(ub2),
                                cc.ap().unsqueeze(2).to_broadcast(
                                    [128, S32, 32]), AL.mult)
                kb.ve.tag(e.drain())
        block.vector(vec)

        kb.barrier(block)


def gla_phase(kb, block, bufs, q_s, f_s, b_s, c_s, g_s, vtok_s, og_s):
    nc, T = kb.nc, kb.T
    NC = T // 128
    with ExitStack() as _stk:
        def sb(name, shape, dt):
            return _stk.enter_context(nc.sbuf_tensor(_u(name), shape, dt))

        def ps(name, shape, dt):
            return _stk.enter_context(nc.psum_tensor(_u(name), shape, dt))

        Fb = [sb("gl_F0", [128, NCH, 128], F32), sb("gl_F1", [128, NCH, 128], F32)]
        bb = [sb("gl_b0", [128, NCH, 128], F32), sb("gl_b1", [128, NCH, 128], F32)]
        Vb = [sb("gl_V0", [128, NCH, 128], BF16), sb("gl_V1", [128, NCH, 128], BF16)]
        gb = [sb("gl_g0", [128, NCH, 128], F32), sb("gl_g1", [128, NCH, 128], F32)]
        og = [sb("gl_og0", [128, NCH, 128], F32), sb("gl_og1", [128, NCH, 128], F32)]
        cb_ = sb("gl_c", [128, NCH, 128], F32)
        Qb = sb("gl_Q", [128, NCH, 128], BF16)
        EA = sb("gl_EA", [128, NCH, 128], F32)
        d1 = sb("gl_d1", [128, NCH, 128], F32)
        kt = sb("gl_kt", [128, NCH, 128], F32)
        KB = sb("gl_KB", [128, NCH, 128], BF16)
        QP = sb("gl_QP", [128, NCH, 128], BF16)
        QH = sb("gl_QH", [128, NCH, 128], BF16)
        KS = sb("gl_KS", [128, NCH, 128], BF16)
        Ktok = sb("gl_Kt", [128, NCH, 128], BF16)
        SBall = sb("gl_SB", [128, NCH, 128], BF16)
        ptb = sb("gl_pt", [128, NCH, 128], BF16)
        S = sb("gl_S", [128, NCH, 128], F32)
        tri = sb("gl_tri", [128, 32], BF16)
        Acol = sb("gl_Ac", [128, NCH, 4], F32)
        dd = sb("gl_dd", [128, NCH, 3], F32)
        eA4 = sb("gl_eA4", [128, NCH, 4], F32)
        dC = sb("gl_dC", [128, NCH, 1], F32)
        eDD = sb("gl_eDD", [128, NCH, 3], F32)
        eC = sb("gl_eC", [128, NCH, 1], F32)
        dec = [sb("gl_dec0", [128, NCH, 1], F32), sb("gl_dec1", [128, NCH, 1], F32)]
        oP = [ps("gl_oP0", [128, NCH, 128], F32), ps("gl_oP1", [128, NCH, 128], F32)]
        Pp = ps("gl_Pp", [128, NCH, 128], F32)
        SAp = ps("gl_SA", [128, NCH, 128], F32)

        L = [kb.ldh, kb.ldh2]
        ST = [kb.sth, kb.sth2]
        SO = [kb.sts, kb.sts2]
        l0 = [L[0].v, L[1].v]
        st0 = [ST[0].v, ST[1].v]
        so0 = [SO[0].v, SO[1].v]
        ve0, pe0, se0, ge0 = kb.ve.v, kb.pe.v, kb.se.v, kb.ge.v

        veV1 = {n: ve0 + 1 + 5 * n + 1 for n in range(NC)}
        veV2 = {n: ve0 + 1 + 5 * n + 2 for n in range(NC)}
        veEV = {n: ve0 + 1 + 5 * n + 3 for n in range(NC)}
        veSU = {n: ve0 + 1 + 5 * n + 4 for n in range(NC)}
        veOG = {n: ve0 + 1 + 5 * n + 5 for n in range(NC)}
        seS1 = {n: se0 + 2 * n + 1 for n in range(NC)}
        seED = {n: se0 + 2 * n + 2 for n in range(NC)}
        geSB = {n: ge0 + 1 + 7 * n + 1 for n in range(NC)}
        geG1 = {n: ge0 + 1 + 7 * n + 2 for n in range(NC)}
        geKD = {n: ge0 + 1 + 7 * n + 3 for n in range(NC)}
        geR = {(n, i): ge0 + 1 + 7 * n + 3 + i for n in range(NC)
               for i in (1, 2, 3)}
        geKS = {n: ge0 + 1 + 7 * n + 7 for n in range(NC)}
        peS = {}
        pePV = {}
        peSA = {}
        v = pe0
        for n in range(NC):
            for i in range(4):
                v += 1
                peS[(n, i)] = v
            for h in range(8):
                v += 1
                pePV[(n, h)] = v
            v += 1
            peSA[n] = v

        def ld_done(n):
            return l0[n % 2] + 96 * (n // 2 + 1)

        def so_done(n):
            return so0[n % 2] + 16 * (n // 2 + 1)

        def tr_done(n):
            return st0[n % 2] + 128 * (n // 2 + 1)

        def init_v(e):
            e.memset(S.ap(), 0.0)
            e.drain()
            e.memset(ptb.ap(), 0.0)
            kb.ve.tag(e.drain())
        block.vector(init_v)

        def init_g(e):
            for q in range(4):
                e.affine_select(tri[q * 32:(q + 1) * 32, :],
                                bufs["ones1k"][q * 32:(q + 1) * 32, 0:32],
                                [[1, 32]], AL.is_ge, 0.0, base=0,
                                channel_multiplier=-1)
                e.drain()
            kb.ge.tag(e.drain())
        block.gpsimd(init_g)

        def ld(e):
            for n in range(NC):
                par = n % 2
                if n >= 2:
                    e.wait_ge(kb.pe.h, peSA[n - 2])
                    e.wait_ge(kb.ve.h, veOG[n - 2])
                    e.wait_ge(kb.se.h, seS1[n - 2])
                    e.wait_ge(kb.ge.h, geKD[n - 2])
                sl = slice(n * 128, (n + 1) * 128)
                rr = lambda apx: apx.rearrange("(c p) t -> p c t", p=128)
                L[par].tag(e.dma_start(out=Fb[par].ap(),
                                       in_=rr(f_s.ap()[:, sl])), 16)
                L[par].tag(e.dma_start(out=bb[par].ap(),
                                       in_=rr(b_s.ap()[:, sl])), 16)
                L[par].tag(e.dma_start(out=gb[par].ap(),
                                       in_=rr(g_s.ap()[:, sl])), 16)
                vsrc = vtok_s.ap()[sl, :]
                L[par].tag(e.dma_start(
                    out=Vb[par].ap(),
                    in_=vsrc.rearrange("p (c f) -> p c f", f=128)), 16)
                if n >= 1:
                    e.wait_ge(kb.ve.h, veV2[n - 1])
                    e.wait_ge(kb.ge.h, geKD[n - 1])
                L[par].tag(e.dma_start(out=cb_.ap(),
                                       in_=rr(c_s.ap()[:, sl])), 16)
                L[par].tag(e.dma_start(out=Qb.ap(),
                                       in_=rr(q_s.ap()[:, sl])), 16)
                if n >= 1:
                    e.wait_ge(kb.ve.h, veOG[n - 1])
                    dst = og_s.ap()[:, (n - 1) * 128:n * 128]
                    SO[(n - 1) % 2].tag(e.dma_start(
                        out=dst.rearrange("(c p) t -> p c t", p=128),
                        in_=og[(n - 1) % 2].ap()), 16)
            e.wait_ge(kb.ve.h, veOG[NC - 1])
            dst = og_s.ap()[:, (NC - 1) * 128:NC * 128]
            SO[(NC - 1) % 2].tag(e.dma_start(
                out=dst.rearrange("(c p) t -> p c t", p=128),
                in_=og[(NC - 1) % 2].ap()), 16)
        block.sync(ld)

        def sca(e):
            for n in range(NC):
                par = n % 2
                e.wait_ge(kb.ve.h, veV1[n])
                if n >= 1:
                    e.wait_ge(kb.ge.h, geKD[n - 1])
                e.activation(eA4.ap(), Acol.ap(), AF.Exp, bias=0.0, scale=1.0)
                e.drain()
                e.activation(eDD.ap(), dd.ap(), AF.Exp, bias=0.0, scale=1.0)
                e.drain()
                e.activation(eC.ap(), dC.ap(), AF.Exp, bias=0.0, scale=1.0)
                e.drain()
                e.activation(dec[par].ap(), bb[par].ap()[:, :, 127:128],
                             AF.Exp, bias=0.0, scale=1.0)
                kb.se.tag(e.drain())
                assert kb.se.v == seS1[n]
                e.wait_ge(kb.ge.h, geG1[n])
                e.activation(EA.ap(), d1.ap(), AF.Exp, bias=0.0, scale=-1.0)
                e.drain()
                e.activation(d1.ap(), d1.ap(), AF.Exp, bias=0.0, scale=1.0)
                kb.se.tag(e.drain())
                assert kb.se.v == seED[n]
                e.wait_ge(kb.ge.h, geKS[n])
                if n >= 1:
                    e.wait_ge(kb.pe.h, peSA[n - 1])
                for h in range(8):
                    ST[par].tag(e.dma_start(out=Ktok[:, h, :],
                                            in_=KS[:, h, :],
                                            transpose=True), 16)
        block.scalar(sca)

        def vec(e):
            for n in range(NC):
                par = n % 2
                e.wait_ge(L[par].h, ld_done(n))
                if n >= 1:
                    e.wait_ge(kb.se.h, seS1[n - 1])
                    e.wait_ge(kb.ge.h, geG1[n - 1])
                e.memset(Acol[:, :, 0:1], 0.0)
                e.drain()
                e.tensor_copy(
                    Acol.ap()[:, :, 1:4].unsqueeze(3),
                    bb[par].ap().rearrange("p h (q s) -> p h q s",
                                           s=32)[:, :, 0:3, 31:32])
                e.drain()
                e.tensor_sub(dd.ap(), Acol.ap()[:, :, 1:4],
                             Acol.ap()[:, :, 0:3])
                e.drain()
                e.tensor_sub(dC.ap(), bb[par].ap()[:, :, 127:128],
                             Acol.ap()[:, :, 3:4])
                kb.ve.tag(e.drain())
                assert kb.ve.v == veV1[n]
                e.wait_ge(kb.se.h, seED[n])
                if n >= 2:
                    e.wait_ge(SO[par].h, so_done(n - 2))
                tmp = og[par]
                e.tensor_mul(tmp.ap(), cb_.ap(), EA.ap())
                e.drain()
                e.scalar_tensor_tensor(QH.ap(), Qb.ap(), 1.0 / 16.0,
                                       tmp.ap(), AL.mult, AL.mult)
                e.drain()
                e.tensor_mul(
                    tmp.ap().rearrange("p h (q s) -> p h q s", s=32),
                    tmp.ap().rearrange("p h (q s) -> p h q s", s=32),
                    eA4.ap().unsqueeze(3).to_broadcast([128, NCH, 4, 32]))
                e.drain()
                e.scalar_tensor_tensor(QP.ap(), Qb.ap(), 1.0 / 16.0,
                                       tmp.ap(), AL.mult, AL.mult)
                kb.ve.tag(e.drain())
                assert kb.ve.v == veV2[n]
                e.wait_ge(kb.pe.h, peS[(n, 3)])
                for q in range(4):
                    pr = slice(q * 32, (q + 1) * 32)
                    e.tensor_mul(ptb.ap()[pr, :, q * 32:(q + 1) * 32],
                                 Pp.ap()[pr, :, q * 32:(q + 1) * 32],
                                 tri.ap()[pr, :].unsqueeze(1).to_broadcast(
                                     [32, NCH, 32]))
                    e.drain()
                for q in range(3):
                    pr = slice(q * 32, (q + 1) * 32)
                    e.tensor_copy(ptb.ap()[pr, :, (q + 1) * 32:128],
                                  Pp.ap()[pr, :, (q + 1) * 32:128])
                    e.drain()
                kb.ve.tag(e.drain())
                assert kb.ve.v == veEV[n]
                e.wait_ge(kb.pe.h, peSA[n])
                e.tensor_mul(S.ap(), S.ap(),
                             dec[par].ap().to_broadcast([128, NCH, 128]))
                e.drain()
                e.tensor_add(S.ap(), S.ap(), SAp.ap())
                kb.ve.tag(e.drain())
                assert kb.ve.v == veSU[n]
                e.tensor_mul(og[par].ap(), oP[par].ap(), gb[par].ap())
                kb.ve.tag(e.drain())
                assert kb.ve.v == veOG[n]
        block.vector(vec)

        def gp(e):
            for n in range(NC):
                par = n % 2
                e.wait_ge(kb.ve.h, veSU[n - 1] if n >= 1 else ve0 + 1)
                if n >= 1:
                    e.wait_ge(kb.pe.h, pePV[(n - 1, 7)])
                e.tensor_copy(SBall.ap(), S.ap())
                kb.ge.tag(e.drain())
                assert kb.ge.v == geSB[n]
                e.wait_ge(kb.ve.h, veV1[n])
                if n >= 1:
                    e.wait_ge(kb.se.h, seED[n - 1])
                e.tensor_sub(
                    d1.ap().rearrange("p h (q s) -> p h q s", s=32),
                    Acol.ap().unsqueeze(3).to_broadcast([128, NCH, 4, 32]),
                    bb[par].ap().rearrange("p h (q s) -> p h q s", s=32))
                kb.ge.tag(e.drain())
                assert kb.ge.v == geG1[n]
                e.tensor_mul(kt.ap(), Fb[par].ap(), cb_.ap())
                e.drain()
                e.tensor_sub(kt.ap(), cb_.ap(), kt.ap())
                e.drain()
                e.wait_ge(kb.se.h, seED[n])
                if n >= 1:
                    e.wait_ge(kb.pe.h, peS[(n - 1, 3)])
                e.tensor_mul(KB.ap(), kt.ap(), d1.ap())
                kb.ge.tag(e.drain())
                assert kb.ge.v == geKD[n]
                for i in (1, 2, 3):
                    e.wait_ge(kb.pe.h, peS[(n, i - 1)])
                    e.tensor_mul(KB.ap()[:, :, 0:i * 32],
                                 KB.ap()[:, :, 0:i * 32],
                                 eDD.ap()[:, :, i - 1:i].to_broadcast(
                                     [128, NCH, i * 32]))
                    kb.ge.tag(e.drain())
                    assert kb.ge.v == geR[(n, i)]
                e.wait_ge(kb.pe.h, peS[(n, 3)])
                e.tensor_mul(KS.ap(), KB.ap(),
                             eC.ap().to_broadcast([128, NCH, 128]))
                kb.ge.tag(e.drain())
                assert kb.ge.v == geKS[n]
        block.gpsimd(gp)

        def ten(e):
            for n in range(NC):
                par = n % 2
                for i in range(4):
                    if i == 0:
                        e.wait_ge(kb.ve.h, veV2[n])
                        e.wait_ge(kb.ge.h, geKD[n])
                        if n >= 1:
                            e.wait_ge(kb.ve.h, veEV[n - 1])
                    else:
                        e.wait_ge(kb.ge.h, geR[(n, i)])
                    last = None
                    for h in range(8):
                        last = e.matmul(Pp[:, h, i * 32:(i + 1) * 32],
                                        KB[:, h, :],
                                        QH[:, h, i * 32:(i + 1) * 32],
                                        start=True, stop=True)
                    kb.pe.tag(last)
                    assert kb.pe.v == peS[(n, i)]
                e.wait_ge(kb.ve.h, veEV[n])
                e.wait_ge(kb.ge.h, geSB[n])
                if n >= 2:
                    e.wait_ge(kb.ve.h, veOG[n - 2])
                e.wait_ge(L[par].h, ld_done(n))
                for h in range(8):
                    e.matmul(oP[par][:, h, :], SBall[:, h, :], QP[:, h, :],
                             start=True, stop=False)
                    last = e.matmul(oP[par][:, h, :], Vb[par][:, h, :],
                                    ptb[:, h, :], start=False, stop=True)
                    kb.pe.tag(last)
                    assert kb.pe.v == pePV[(n, h)]
                e.wait_ge(ST[par].h, tr_done(n))
                if n >= 1:
                    e.wait_ge(kb.ve.h, veSU[n - 1])
                last = None
                for h in range(8):
                    last = e.matmul(SAp[:, h, :], Ktok[:, h, :],
                                    Vb[par][:, h, :], start=True, stop=True)
                kb.pe.tag(last)
                assert kb.pe.v == peSA[n]
        block.tensor(ten)

        kb.barrier(block)


def gn_phase(kb, block, bufs, src_f32, dst_bf, nw_sb):
    nc, T, NT = kb.nc, kb.T, kb.NT
    ones = bufs["ones128"]
    VCH = [0, 1, 2]
    GCH = [3, 4, 5, 6, 7]
    with (
        nc.sbuf_tensor(_u("gn_y0"), [128, NCH, TT], F32) as y0,
        nc.sbuf_tensor(_u("gn_y1"), [128, NCH, TT], F32) as y1,
        nc.sbuf_tensor(_u("gn_yb"), [128, NCH, TT], BF16) as ybf,
        nc.sbuf_tensor(_u("gn_ys"), [128, NCH, TT], BF16) as ysq,
        nc.sbuf_tensor(_u("gn_mu"), [128, TT], F32) as mu,
        nc.sbuf_tensor(_u("gn_es"), [128, TT], F32) as es,
        nc.sbuf_tensor(_u("gn_rstd"), [128, TT], F32) as rstd,
        nc.sbuf_tensor(_u("gn_ob"), [128, NCH, TT], BF16) as obf,
        nc.psum_tensor(_u("gn_psA"), [128, TT], F32) as psA,
        nc.psum_tensor(_u("gn_psB"), [128, TT], F32) as psB,
    ):
        yb = [y0, y1]
        L = [kb.ldh, kb.ldh2]
        S = [kb.sth, kb.sth2]
        ve0, pe0, se0, ge0 = kb.ve.v, kb.pe.v, kb.se.v, kb.ge.v
        l0 = [L[0].v, L[1].v]
        s0 = [S[0].v, S[1].v]

        def ld(e):
            for i in range(NT):
                if i >= 2:
                    e.wait_ge(kb.ve.h, ve0 + 4 * (i - 1))
                    e.wait_ge(kb.ge.h, ge0 + (i - 1))
                src = src_f32.ap()[:, i * TT:(i + 1) * TT]
                L[i % 2].tag(e.dma_start(
                    out=yb[i % 2].ap(),
                    in_=src.rearrange("(c p) t -> p c t", p=128)), 16)
        block.sync(ld)

        def vec(e):
            for i in range(NT):
                e.wait_ge(L[i % 2].h, l0[i % 2] + 16 * (i // 2 + 1))
                e.tensor_copy(ybf.ap(), yb[i % 2].ap())
                e.drain()
                e.tensor_tensor(ysq.ap(), ybf.ap(), ybf.ap(), AL.mult)
                kb.ve.tag(e.drain())
                e.wait_ge(kb.pe.h, pe0 + i + 1)
                if i > 0:
                    e.wait_ge(kb.ge.h, ge0 + i)
                e.tensor_scalar(mu[:], psA[:], 1.0 / D, None, AL.mult)
                e.drain()
                e.tensor_scalar(es[:], psB[:], 1.0 / D, None, AL.mult)
                e.drain()
                e.tensor_tensor(rstd[:], mu[:], mu[:], AL.mult)
                e.drain()
                e.tensor_sub(es[:], es[:], rstd[:])
                kb.ve.tag(e.drain())
                e.wait_ge(kb.se.h, se0 + i + 1)
                e.reciprocal(rstd[:], es[:])
                kb.ve.tag(e.drain())
                if i > 0:
                    j = (i - 1) % 2
                    e.wait_ge(S[j].h, s0[j] + 16 * ((i - 1) // 2 + 1))
                for dc in VCH:
                    y_ = yb[i % 2]
                    e.tensor_sub(y_[:, dc, :], y_[:, dc, :], mu[:])
                    e.drain()
                    e.tensor_mul(y_[:, dc, :], y_[:, dc, :], rstd[:])
                    e.drain()
                    e.tensor_mul(obf[:, dc, :], y_[:, dc, :],
                                 nw_sb[:, dc:dc + 1].to_broadcast([128, TT]))
                    e.drain()
                kb.ve.tag(e.drain())
        block.vector(vec)

        def sca(e):
            for i in range(NT):
                e.wait_ge(kb.ve.h, ve0 + 4 * i + 2)
                e.activation(es[:], es[:], AF.Sqrt, bias=bufs["ceps"][:],
                             scale=1.0)
                kb.se.tag(e.drain())
                e.wait_ge(kb.ve.h, ve0 + 4 * i + 4)
                e.wait_ge(kb.ge.h, ge0 + i + 1)
                dst = dst_bf.ap()[:, i * TT:(i + 1) * TT]
                S[i % 2].tag(e.dma_start(
                    out=dst.rearrange("(c p) t -> p c t", p=128),
                    in_=obf.ap()), 16)
        block.scalar(sca)

        def ten(e):
            for i in range(NT):
                e.wait_ge(kb.ve.h, ve0 + 4 * i + 1)
                for dc in range(NCH):
                    e.matmul(psA[:], ones[:, 0:128], ybf[:, dc, :],
                             start=(dc == 0), stop=(dc == NCH - 1))
                last = None
                for dc in range(NCH):
                    last = e.matmul(psB[:], ones[:, 0:128], ysq[:, dc, :],
                                    start=(dc == 0), stop=(dc == NCH - 1))
                kb.pe.tag(last)
        block.tensor(ten)

        def gp(e):
            for i in range(NT):
                e.wait_ge(kb.ve.h, ve0 + 4 * i + 3)
                for dc in GCH:
                    y_ = yb[i % 2]
                    e.tensor_sub(y_[:, dc, :], y_[:, dc, :], mu[:])
                    e.drain()
                    e.tensor_mul(y_[:, dc, :], y_[:, dc, :], rstd[:])
                    e.drain()
                    e.tensor_mul(obf[:, dc, :], y_[:, dc, :],
                                 nw_sb[:, dc:dc + 1].to_broadcast([128, TT]))
                    e.drain()
                kb.ge.tag(e.drain())
        block.gpsimd(gp)

        kb.barrier(block)


def quant_weights_phase(kb, block, bufs, w1, w2, w1qT_s, w2qT_s, sw1, sw2,
                        cc_in, cc_out, gcol):
    nc = kb.nc
    jobs = ([("w1", r) for r in range(FF1 // 128)] +
            [("w2", r) for r in range(D // 128)])
    units = []
    for j, (w, r) in enumerate(jobs):
        for sub in range(1 if w == "w1" else 4):
            units.append((j, w, r, sub))

    def units_upto(j):
        if j < 32:
            return j + 1
        return 32 + 4 * (j - 31)

    with (
        nc.sbuf_tensor(_u("qw_a0"), [128, FF1], F32) as a0,
        nc.sbuf_tensor(_u("qw_a1"), [128, FF1], F32) as a1,
        nc.sbuf_tensor(_u("qw_q0"), [128, FF1], BF16) as q0,
        nc.sbuf_tensor(_u("qw_q1"), [128, FF1], BF16) as q1,
        nc.sbuf_tensor(_u("qw_tq"), [128, 2, 8, 128], BF16) as tq,
        nc.sbuf_tensor(_u("qw_rc"), [128, 1], F32) as rcol,
        nc.sbuf_tensor(_u("qw_g1"), [1, 1], F32) as gm1,
    ):
        ab = [a0, a1]
        qb = [q0, q1]
        L = [kb.ldh, kb.ldh2]
        S = [kb.sth, kb.sth2]
        TS = [kb.sts, kb.sts2]
        LD = [kb.lds, kb.lds2]
        l0 = [L[0].v, L[1].v]
        s0 = [S[0].v, S[1].v]
        t0 = [TS[0].v, TS[1].v]
        d0 = [kb.lds.v + 32, kb.lds2.v]
        ve0, ge0, cc0 = kb.ve.v, kb.ge.v, kb.cc.v

        def srcof(w):
            return (w1 if w == "w1" else w2).ap()

        def emit_unit(e, u):
            j, w, r, sub = units[u]
            if sub == 0:
                e.wait_ge(kb.ve.h, ve0 + 2 * j + 2)
            if u >= 2:
                e.wait_ge(LD[u % 2].h, d0[u % 2] + 16 * ((u - 2) // 2 + 1))
            for c in range(8):
                col = (8 * sub + c) * 128
                TS[u % 2].tag(e.dma_start(
                    out=tq[:, u % 2, c, :],
                    in_=qb[j % 2][:, col:col + 128], transpose=True), 16)

        def ld(e):
            for j, (w, r) in enumerate(jobs):
                if j >= 2:
                    e.wait_ge(kb.ve.h, ve0 + (j - 2) * 2 + 2)
                src = srcof(w)
                C = src.shape[1]
                L[j % 2].tag(e.dma_start(out=ab[j % 2][:, 0:C],
                                         in_=src[r * 128:(r + 1) * 128, :]),
                             16)
        block.sync(ld)

        def st(e):
            for u in range(len(units)):
                emit_unit(e, u)
        block.scalar(st)

        def vec(e):
            for j, (w, r) in enumerate(jobs):
                C = srcof(w).shape[1]
                sw = sw1 if w == "w1" else sw2
                e.wait_ge(L[j % 2].h, l0[j % 2] + 16 * (j // 2 + 1))
                a = ab[j % 2]
                e.reduce_max(rcol[:], a[:, 0:C], axis=AX.X,
                             apply_absolute_value=True)
                e.drain()
                e.tensor_scalar(rcol[:], rcol[:], EPS_Q, 1.0 / 127.0, AL.max,
                                AL.mult)
                e.drain()
                e.tensor_copy(sw[:, r:r + 1], rcol[:])
                e.drain()
                e.reciprocal(rcol[:], rcol[:])
                e.drain()
                e.tensor_scalar(a[:, 0:C], a[:, 0:C], rcol[:], None, AL.mult)
                kb.ve.tag(e.drain())
                e.tensor_scalar(a[:, 0:C], a[:, 0:C], RMAGIC, -RMAGIC,
                                AL.add, AL.add)
                e.drain()
                e.tensor_scalar(a[:, 0:C], a[:, 0:C], -128.0, 127.0, AL.max,
                                AL.min)
                e.drain()
                if j >= 2:
                    U = units_upto(j - 2)
                    e.wait_ge(TS[0].h, t0[0] + 128 * ((U + 1) // 2))
                    e.wait_ge(TS[1].h, t0[1] + 128 * (U // 2))
                e.tensor_copy(qb[j % 2][:, 0:C], a[:, 0:C])
                kb.ve.tag(e.drain())
        block.vector(vec)

        def gp(e):
            kb.lds.tag(e.dma_start(out=cc_in[:, :], in_=bufs["mx1"][:, :]), 16)
            e.wait_ge(kb.lds.h, kb.lds.v)
            kb.cc.tag(e.collective_compute(
                "AllReduce", AL.max,
                replica_groups=[list(range(kb.ncores))],
                ins=[cc_in.ap().opt()],
                outs=[cc_out.ap().opt()]))
            e.wait_ge(kb.cc.h, cc0 + 1)
            kb.lds.tag(e.dma_start(out=gcol.ap(),
                                   in_=cc_out.ap().to_broadcast([128, 1])), 16)
            e.wait_ge(kb.lds.h, kb.lds.v)
            e.tensor_scalar(gcol.ap(), gcol.ap(), EPS_Q, None, AL.max)
            kb.ge.tag(e.drain())
            assert kb.lds.v == d0[0] and kb.lds2.v == d0[1]
            for u, (j, w, r, sub) in enumerate(units):
                if u >= 2:
                    e.wait_ge(LD[u % 2].h, d0[u % 2] + 16 * ((u - 2) // 2 + 1))
                e.wait_ge(TS[u % 2].h, t0[u % 2] + 128 * (u // 2 + 1))
                if w == "w1":
                    dst = w1qT_s.ap()[:, r * 128:(r + 1) * 128]
                else:
                    dst = w2qT_s.ap()[sub * 1024:(sub + 1) * 1024,
                                      r * 128:(r + 1) * 128]
                LD[u % 2].tag(e.dma_start(
                    out=dst.rearrange("(c p) f -> p c f", p=128),
                    in_=tq[:, u % 2, :, :]), 16)
        block.gpsimd(gp)

        kb.barrier(block)


def mm1_phase(kb, block, bufs, xn32_s, w1qT_s, y_s, sw1, b1col, gcol,
              cc_in2, cc_out2, g2col):
    nc, T = kb.nc, kb.T
    TH = T // 2
    SUB = min(256, TH)
    NSUB = TH // SUB
    FC = FF1 // 128
    NGH = FC * NSUB
    with ExitStack() as _stk:
        def sb(name, shape, dt):
            return _stk.enter_context(nc.sbuf_tensor(_u(name), shape, dt))
        xq = sb("m1_xq", [128, NCH, TH], BF16)
        xl = [sb("m1_x0", [128, NCH, SUB], F32),
              sb("m1_x1", [128, NCH, SUB], F32)]
        mb_ = sb("m1_m", [128, NCH, SUB], F32)
        wt = [sb("m1_w0", [128, NCH, 128], BF16),
              sb("m1_w1", [128, NCH, 128], BF16)]
        yb = sb("m1_y", [128, 4, SUB], F32)
        esc = sb("m1_es", [128, FC], F32)
        ebi = sb("m1_eb", [128, FC], F32)
        rin = sb("m1_ri", [128, 1], F32)
        hmx = sb("m1_hm", [128, 1], F32)
        hmt = sb("m1_ht", [128, 1], F32)
        hm1 = sb("m1_h1", [1, 1], F32)
        tcol = sb("m1_tc", [128, FC], F32)
        pY = [_stk.enter_context(nc.psum_tensor(_u("m1_p%d" % i),
                                                [128, SUB], F32))
              for i in range(4)]
        L = [kb.ldh, kb.ldh2]
        S = [kb.sth, kb.sth2]
        SO = [kb.sts, kb.sts2]
        l0 = [L[0].v, L[1].v]
        s0 = [S[0].v, S[1].v]
        so0 = [SO[0].v, SO[1].v]
        ve0, pe0, se0, ge0, cc0 = kb.ve.v, kb.pe.v, kb.se.v, kb.ge.v, kb.cc.v

        vq = ve0 + 2
        gq = ge0 + 1

        def veQmid(j):
            half, tq = divmod(j, NSUB)
            return vq + half * (NSUB + NGH) + tq + 1

        def veMax(g):
            half, gg = divmod(g, NGH)
            return vq + half * (NSUB + NGH) + NSUB + gg + 1

        ve_end = vq + 2 * (NSUB + NGH)


        def prol_v(e):
            e.reciprocal(rin[:], gcol[:])
            e.drain()
            e.tensor_scalar(rin[:], rin[:], 255.0, None, AL.mult)
            e.drain()
            e.tensor_scalar(esc.ap(), sw1.ap(), gcol[:], None, AL.mult)
            e.drain()
            e.tensor_scalar(esc.ap(), esc.ap(), 1.0 / 255.0, None, AL.mult)
            e.drain()
            e.reciprocal(tcol.ap(), esc.ap())
            e.drain()
            e.tensor_tensor(tcol.ap(), b1col.ap(), tcol.ap(), AL.mult)
            e.drain()
            e.tensor_scalar(tcol.ap(), tcol.ap(), RMAGIC, -RMAGIC,
                            AL.add, AL.add)
            kb.ve.tag(e.drain())
            e.tensor_tensor(ebi.ap(), tcol.ap(), esc.ap(), AL.mult)
            kb.ve.tag(e.drain())
        block.vector(prol_v)

        def prol_g(e):
            kb.ge.tag(e.engine_nop())
        block.gpsimd(prol_g)

        def emit_store(e, g):
            hh, gg = divmod(g, NGH)
            fcp, tqp = divmod(gg, NSUB)
            e.wait_ge(kb.ve.h, veMax(g))
            if g >= 2:
                e.wait_ge(SO[g % 2].h, so0[g % 2] + 16 * ((g - 2) // 2 + 1))
            t_lo = hh * TH + tqp * SUB
            dst = y_s.ap()[fcp * 128:(fcp + 1) * 128, t_lo:t_lo + SUB]
            SO[g % 2].tag(e.dma_start(out=dst, in_=yb[:, g % 4, :]), 16)

        def ld(e):
            for half in range(2):
                for tq in range(NSUB):
                    j = half * NSUB + tq
                    if j >= 2:
                        e.wait_ge(kb.ve.h, veQmid(j - 2))
                    src = xn32_s.ap()[:, half * TH + tq * SUB:
                                      half * TH + (tq + 1) * SUB]
                    L[j % 2].tag(e.dma_start(
                        out=xl[j % 2].ap(),
                        in_=src.rearrange("(c p) t -> p c t", p=128)), 16)
                for fc in range(FC):
                    if fc >= 1:
                        for g in range(half * NGH + (fc - 1) * NSUB,
                                       half * NGH + fc * NSUB):
                            emit_store(e, g)
                for g in range(half * NGH + (FC - 1) * NSUB,
                               half * NGH + FC * NSUB):
                    emit_store(e, g)
        block.sync(ld)

        LD = [kb.lds, kb.lds2]
        d0 = [LD[0].v, LD[1].v]

        def gpw(e):
            for half in range(2):
                for fc in range(FC):
                    gw = half * FC + fc
                    if gw >= 2:
                        e.wait_ge(LD[gw % 2].h,
                                  d0[gw % 2] + 16 * ((gw - 2) // 2 + 1))
                        e.wait_ge(kb.pe.h, pe0 + (gw - 1) * NSUB)
                    LD[gw % 2].tag(e.dma_start(
                        out=wt[gw % 2].ap(),
                        in_=w1qT_s.ap()[:, fc * 128:(fc + 1) * 128].rearrange(
                            "(c p) f -> p c f", p=128)), 16)
        block.gpsimd(gpw)

        def quant_ops(e, j):
            a = xl[j % 2]
            half, tq = divmod(j, NSUB)
            e.tensor_scalar(a.ap(), a.ap(), rin[:], None, AL.mult)
            e.drain()
            e.tensor_scalar(a.ap(), a.ap(), RMAGIC, -RMAGIC, AL.add, AL.add)
            e.drain()
            e.tensor_scalar(a.ap(), a.ap(), 0.0, 255.0, AL.max, AL.min)
            e.drain()
            e.tensor_copy(xq[:, :, tq * SUB:(tq + 1) * SUB], a.ap())
            e.drain()

        def vecq(e):
            for half in range(2):
                for tq in range(NSUB):
                    j = half * NSUB + tq
                    e.wait_ge(L[j % 2].h, l0[j % 2] + 16 * (j // 2 + 1))
                    if tq == 0 and half == 1:
                        e.wait_ge(kb.pe.h, pe0 + NGH)
                    quant_ops(e, j)
                    kb.ve.tag(e.drain())
                    assert kb.ve.v == veQmid(j)
                for gg in range(NGH):
                    g = half * NGH + gg
                    e.wait_ge(kb.se.h, se0 + g + 1)
                    e.reduce_max(hmt[:], yb[:, g % 4, :], axis=AX.X)
                    e.drain()
                    if g == 0:
                        e.tensor_copy(hmx[:], hmt[:])
                    else:
                        e.tensor_max(hmx[:], hmx[:], hmt[:])
                    kb.ve.tag(e.drain())
                    assert kb.ve.v == veMax(g)
        block.vector(vecq)

        def gpq(e):
            e.wait_ge(kb.ve.h, ve_end)
            e.tensor_reduce(hm1.ap(), hmx.ap(), AX.C, AL.max)
            e.drain()
            kb.lds.tag(e.dma_start(out=cc_in2[:, :], in_=hm1[:, :]), 16)
            e.wait_ge(kb.lds.h, kb.lds.v)
            kb.cc.tag(e.collective_compute(
                "AllReduce", AL.max,
                replica_groups=[list(range(kb.ncores))],
                ins=[cc_in2.ap().opt()],
                outs=[cc_out2.ap().opt()]))
            e.wait_ge(kb.cc.h, cc0 + 1)
            kb.lds.tag(e.dma_start(out=g2col.ap(),
                                   in_=cc_out2.ap().to_broadcast([128, 1])),
                       16)
            e.wait_ge(kb.lds.h, kb.lds.v)
            e.tensor_scalar(g2col.ap(), g2col.ap(), EPS_Q, None, AL.max)
            kb.ge.tag(e.drain())
        block.gpsimd(gpq)

        def wst_done(gw):
            return (LD[gw % 2].h, d0[gw % 2] + 16 * (gw // 2 + 1))

        def sca(e):
            e.wait_ge(kb.ve.h, vq)
            g = 0
            for half in range(2):
                for fc in range(FC):
                    for tq in range(NSUB):
                        e.wait_ge(kb.pe.h, pe0 + g + 1)
                        if g >= 4:
                            jj = (g - 4) % 2
                            e.wait_ge(SO[jj].h,
                                      so0[jj] + 16 * ((g - 4) // 2 + 1))
                        e.activation(yb[:, g % 4, :], pY[g % 4][:],
                                     AF.Identity, bias=ebi[:, fc:fc + 1],
                                     scale=esc[:, fc:fc + 1])
                        kb.se.tag(e.drain())
                        g += 1
        block.scalar(sca)

        def ten(e):
            g = 0
            for half in range(2):
                for fc in range(FC):
                    gw = half * FC + fc
                    e.wait_ge(*wst_done(gw))
                    for tq in range(NSUB):
                        j = half * NSUB + tq
                        e.wait_ge(kb.ve.h, veQmid(j))
                        if g >= 4:
                            e.wait_ge(kb.se.h, se0 + g - 3)
                        last = None
                        for dc in range(NCH):
                            last = e.matmul(
                                pY[g % 4][:], wt[gw % 2][:, dc, :],
                                xq[:, dc, tq * SUB:(tq + 1) * SUB],
                                start=(dc == 0), stop=(dc == NCH - 1))
                        kb.pe.tag(last)
                        g += 1
        block.tensor(ten)

        kb.barrier(block)


def transpose_dram_phase(kb, block, bufs, src, dst):
    nc = kb.nc
    R, C = src.ap().shape
    jobs = [(r, c) for c in range(C // 128) for r in range(R // 128)]
    assert len(jobs) % 2 == 0
    with (
        nc.sbuf_tensor(_u("tw_bb"), [128, 8, 128], BF16) as bb,
    ):
        L = [kb.ldh, kb.ldh2]
        S = [kb.sth, kb.sth2]
        l0 = [L[0].v, L[1].v]
        s0 = [S[0].v, S[1].v]

        def ld(e):
            for j, (r, c) in enumerate(jobs):
                if j >= 2:
                    e.wait_ge(L[j % 2].h,
                              l0[j % 2] + 16 * ((j - 2) // 2 + 1))
                if j >= 8:
                    p = (j - 8) // 2
                    e.wait_ge(S[p % 2].h, s0[p % 2] + 16 * (p // 2 + 1))
                L[j % 2].tag(e.dma_start(
                    out=bb[:, j % 8, :],
                    in_=src.ap()[r * 128:(r + 1) * 128, c * 128:(c + 1) * 128],
                    transpose=True), 16)
        block.sync(ld)

        def st(e):
            for p in range(len(jobs) // 2):
                r, c = jobs[2 * p]
                if p >= 2:
                    e.wait_ge(S[p % 2].h,
                              s0[p % 2] + 16 * ((p - 2) // 2 + 1))
                e.wait_ge(L[0].h, l0[0] + 16 * (p + 1))
                e.wait_ge(L[1].h, l0[1] + 16 * (p + 1))
                sl = (2 * p) % 8
                S[p % 2].tag(e.dma_start(
                    out=dst.ap()[c * 128:(c + 1) * 128,
                                 r * 128:(r + 2) * 128],
                    in_=bb[:, sl:sl + 2, :]), 16)
        block.scalar(st)

        kb.barrier(block)


def hq_quant_phase(kb, block, bufs, y_s, hq_s, sw2, b2col, g2col, esc2, ebi2):
    nc, T = kb.nc, kb.T
    HW = T // 2
    jobs = [(r, hf) for r in range(FF1 // 128) for hf in range(2)]
    NJ = len(jobs)
    with (
        nc.sbuf_tensor(_u("hq_a0"), [128, HW], F32) as a0,
        nc.sbuf_tensor(_u("hq_a1"), [128, HW], F32) as a1,
        nc.sbuf_tensor(_u("hq_q0"), [128, HW], BF16) as q0,
        nc.sbuf_tensor(_u("hq_q1"), [128, HW], BF16) as q1,
        nc.sbuf_tensor(_u("hq_r2"), [128, 1], F32) as r2,
        nc.sbuf_tensor(_u("hq_tc"), [128, NCH], F32) as tcol,
    ):
        ab = [a0, a1]
        qb = [q0, q1]
        L = [kb.ldh, kb.ldh2]
        S = [kb.sth, kb.sth2]
        l0 = [L[0].v, L[1].v]
        s0 = [S[0].v, S[1].v]
        ve0, ge0 = kb.ve.v, kb.ge.v

        def prol_v(e):
            e.reciprocal(r2[:], g2col[:])
            e.drain()
            e.tensor_scalar(r2[:], r2[:], 255.0, None, AL.mult)
            e.drain()
            e.tensor_scalar(esc2.ap(), sw2.ap(), g2col[:], None, AL.mult)
            e.drain()
            e.tensor_scalar(esc2.ap(), esc2.ap(), 1.0 / 255.0, None, AL.mult)
            e.drain()
            e.reciprocal(tcol.ap(), esc2.ap())
            e.drain()
            e.tensor_tensor(tcol.ap(), b2col.ap(), tcol.ap(), AL.mult)
            e.drain()
            e.tensor_scalar(tcol.ap(), tcol.ap(), RMAGIC, -RMAGIC,
                            AL.add, AL.add)
            kb.ve.tag(e.drain())
            e.tensor_tensor(ebi2.ap(), tcol.ap(), esc2.ap(), AL.mult)
            kb.ve.tag(e.drain())
        block.vector(prol_v)

        def prol_g(e):
            kb.ge.tag(e.engine_nop())
        block.gpsimd(prol_g)

        vq = ve0 + 2
        gq = ge0 + 1

        def jdone(j):
            return kb.ve.h, vq + j + 1

        def ld(e):
            for j, (r, hf) in enumerate(jobs):
                if j >= 2:
                    hh, vv = jdone(j - 2)
                    e.wait_ge(hh, vv)
                L[j % 2].tag(e.dma_start(
                    out=ab[j % 2][:, :],
                    in_=y_s.ap()[r * 128:(r + 1) * 128,
                                 hf * HW:(hf + 1) * HW]), 16)
        block.sync(ld)

        def quant_slab(e, j, tag):
            a = ab[j % 2]
            e.wait_ge(L[j % 2].h, l0[j % 2] + 16 * (j // 2 + 1))
            e.tensor_scalar(a[:], a[:], r2[:], None, AL.mult)
            e.drain()
            e.tensor_scalar(a[:], a[:], RMAGIC, -RMAGIC, AL.add, AL.add)
            e.drain()
            e.tensor_scalar(a[:], a[:], 0.0, 255.0, AL.max, AL.min)
            e.drain()
            if j >= 2:
                jj = (j - 2) % 2
                e.wait_ge(S[jj].h, s0[jj] + 16 * ((j - 2) // 2 + 1))
            e.tensor_copy(qb[j % 2][:], a[:])
            e.drain()
            tag()

        def vecq(e):
            for j in range(NJ):
                quant_slab(e, j, lambda: kb.ve.tag(e.drain()))
        block.vector(vecq)

        def st(e):
            for j, (r, hf) in enumerate(jobs):
                hh, vv = jdone(j)
                e.wait_ge(hh, vv)
                S[j % 2].tag(e.dma_start(
                    out=hq_s.ap()[r * 128:(r + 1) * 128,
                                  hf * HW:(hf + 1) * HW],
                    in_=qb[j % 2][:, :]), 16)
        block.scalar(st)

        kb.barrier(block)



def build_nc(T=4096, ncores=8, upto=99):
    nc = bass.Bass("TRN2", target_bir_lowering=False, debug=False)
    kb = Kb(nc, T, ncores)

    xT = nc.dram_tensor("xT", [D, T], F32, kind="ExternalInput")
    l0_wqT = nc.dram_tensor("l0_wqT", [D, D], F32, kind="ExternalInput")
    l0_wkT = nc.dram_tensor("l0_wkT", [D, D], F32, kind="ExternalInput")
    l0_wvT = nc.dram_tensor("l0_wvT", [D, D], F32, kind="ExternalInput")
    l0_woT = nc.dram_tensor("l0_woT", [D, D], F32, kind="ExternalInput")
    outT = nc.dram_tensor("outT", [D, T], F32, kind="ExternalOutput")

    l0_wgT = nc.dram_tensor("l0_wgT", [D, FF0], I32, kind="ExternalInput")
    l0_wuT = nc.dram_tensor("l0_wuT", [D, FF0], I32, kind="ExternalInput")
    l0_wdT = nc.dram_tensor("l0_wdT", [FF0, D], I32, kind="ExternalInput")
    l1_wq = nc.dram_tensor("l1_wq", [D, D], F32, kind="ExternalInput")
    l1_wk = nc.dram_tensor("l1_wk", [D, D], F32, kind="ExternalInput")
    l1_wv = nc.dram_tensor("l1_wv", [D, D], F32, kind="ExternalInput")
    l1_wo = nc.dram_tensor("l1_wo", [D, D], F32, kind="ExternalInput")
    l1_wg1 = nc.dram_tensor("l1_wg1", [D, D1], F32, kind="ExternalInput")
    l1_wg2 = nc.dram_tensor("l1_wg2", [D1, D], F32, kind="ExternalInput")
    l1_nw = nc.dram_tensor("l1_nw", [128, NCH], F32, kind="ExternalInput")
    l1_w1 = nc.dram_tensor("l1_w1", [FF1, D], F32, kind="ExternalInput")
    l1_b1 = nc.dram_tensor("l1_b1", [128, FF1 // 128], F32, kind="ExternalInput")
    l1_w2 = nc.dram_tensor("l1_w2", [D, FF1], F32, kind="ExternalInput")
    l1_b2 = nc.dram_tensor("l1_b2", [128, NCH], F32, kind="ExternalInput")

    xn_s = nc.dram_tensor("xn_s", [D, T], BF16)
    q_s = nc.dram_tensor("q_s", [D, T], BF16)
    k_s = nc.dram_tensor("k_s", [D, T], BF16)
    v_s = nc.dram_tensor("v_s", [T, 2 * D], BF16)
    o_s = nc.dram_tensor("o_s", [D, T], BF16)
    wg_b = nc.dram_tensor("wg_b", [D, FF0], BF16)
    wu_b = nc.dram_tensor("wu_b", [D, FF0], BF16)
    wd_b = nc.dram_tensor("wd_b", [FF0, D], I8)
    h_s = nc.dram_tensor("h_s", [FF0, T], BF16)
    f_s = nc.dram_tensor("f_s", [D, T], F32)
    g_s = nc.dram_tensor("g_s", [D, T], F32)
    vtok_s = nc.dram_tensor("vtok_s", [T, D], BF16)
    b_s = nc.dram_tensor("b_s", [D, T], F32)
    c_s = nc.dram_tensor("c_s", [D, T], F32)
    og_s = nc.dram_tensor("og_s", [D, T], F32)
    xn32_s = nc.dram_tensor("xn32_s", [D, T], F32)
    y_s = nc.dram_tensor("y_s", [FF1, T], F32)
    hq_s = nc.dram_tensor("hq_s", [FF1, T], BF16)
    w1qT_s = nc.dram_tensor("w1qT_s", [D, FF1], I8)
    w2qT_s = nc.dram_tensor("w2qT_s", [FF1, D], I8)
    cc_in = nc.dram_tensor("cc_in", [1, 1], F32)
    cc_out = nc.dram_tensor("cc_out", [1, 1], F32)
    cc_in2 = nc.dram_tensor("cc_in2", [1, 1], F32)
    cc_out2 = nc.dram_tensor("cc_out2", [1, 1], F32)

    with (
        nc.sbuf_tensor("X", [128, NCH, T], F32) as X,
        nc.sbuf_tensor("ones1k", [128, 1024], BF16) as ones1k,
        nc.sbuf_tensor("ident128", [128, 128], BF16) as ident,
        nc.sbuf_tensor("mx1", [1, 1], F32) as mx1,
        nc.sbuf_tensor("ceps", [128, 1], F32) as ceps,
        nc.sbuf_tensor("nw_sb", [128, NCH], F32) as nw_sb,
        nc.sbuf_tensor("sw1", [128, FF1 // 128], F32) as sw1,
        nc.sbuf_tensor("sw2", [128, NCH], F32) as sw2,
        nc.sbuf_tensor("gcol", [128, 1], F32) as gcol,
        nc.sbuf_tensor("g2col", [128, 1], F32) as g2col,
        nc.sbuf_tensor("b1col", [128, FF1 // 128], F32) as b1col,
        nc.sbuf_tensor("b2col", [128, NCH], F32) as b2col,
        nc.sbuf_tensor("esc2", [128, NCH], F32) as esc2,
        nc.sbuf_tensor("ebi2", [128, NCH], F32) as ebi2,
        nc.Block() as block,
    ):
        ones = ones1k.ap()[:, 0:128]
        bufs = dict(X=X, ones128=ones, ident128=ident, mx1=mx1, ceps=ceps,
                    ones1k=ones1k)

        def ld(e):
            kb.ldh.tag(e.dma_start(
                out=X.ap(),
                in_=xT.ap().rearrange("(c p) t -> p c t", p=128)), 16)
            kb.ldh.tag(e.dma_start(out=nw_sb.ap(), in_=l1_nw.ap()), 16)
            kb.ldh.tag(e.dma_start(out=b1col.ap(), in_=l1_b1.ap()), 16)
            kb.ldh.tag(e.dma_start(out=b2col.ap(), in_=l1_b2.ap()), 16)
        block.sync(ld)

        def init_v(e):
            e.memset(ones1k[:], 1.0)
            e.memset(ceps[:], EPS_LN)
            e.wait_ge(kb.ldh.h, kb.ldh.v)
            kb.ve.tag(e.drain())
        block.vector(init_v)

        def init_g(e):
            e.wait_ge(kb.ve.h, kb.ve.v)
            e.affine_select(ident[:], ones[:, 0:128], [[1, 128]], AL.is_equal,
                            0.0, base=0, channel_multiplier=-1)
            kb.ge.tag(e.drain())
        block.gpsimd(init_g)

        def init_vpad(e):
            e.wait_ge(kb.ve.h, kb.ve.v)
            for i in range(T // 128):
                dst = v_s.ap()[i * 128:(i + 1) * 128, :]
                dst = dst.rearrange("p (h c) -> p h c", c=128)[:, :, 64:128]
                kb.sth.tag(e.dma_start(
                    out=dst,
                    in_=ones1k.ap().rearrange("p (h c) -> p h c", c=64)), 16)
        block.scalar(init_vpad)

        kb.barrier(block)

        ln_pre_phase(kb, block, bufs, dst_bf=xn_s)
        if upto >= 2:
            simple_proj_phase(kb, block, bufs, None, xn_s, None, None,
                              None, None, jobs=[
                                  (l0_wqT, q_s, AF.Copy, 1.0, BF16),
                                  (l0_wkT, k_s, AF.Copy, 1.0, BF16)])
            v_proj_phase(kb, block, bufs, l0_wvT, xn_s, v_s, padded=True)
        if upto >= 3:
            attn0_phase(kb, block, bufs, q_s, k_s, v_s, o_s)
        if upto >= 4:
            proj_postln_phase(kb, block, bufs, o_s, NCH, w_dram=l0_woT)
        if upto >= 5:
            cast_weights_phase(kb, block, bufs, [(l0_wgT, wg_b), (l0_wuT, wu_b),
                                                 (l0_wdT, wd_b, True)])
            ln_pre_phase(kb, block, bufs, dst_bf=xn_s)
            swiglu_phase(kb, block, bufs, wg_b, wu_b, xn_s, h_s)
            proj_postln_phase(kb, block, bufs, h_s, FF0 // 128,
                              w_bf_dram=wd_b, evac_scale=INV127, tt=256)
        if upto >= 5.1:
            ln_pre_phase(kb, block, bufs, dst_bf=xn_s)
            simple_proj_phase(kb, block, bufs, None, xn_s, None, None,
                              None, None, jobs=[
                                  (l1_wq, q_s, AF.Silu, 1.0, BF16),
                                  (l1_wk, f_s, AF.Sigmoid, -1.0, F32)])
        if upto >= 5.2:
            pass
        if upto >= 5.3:
            v_proj_phase(kb, block, bufs, l1_wv, xn_s, vtok_s, padded=False)
        if upto >= 5.4:
            gate_phase(kb, block, bufs, l1_wg1, l1_wg2, xn_s, g_s)
        if upto >= 5.5:
            decay_prep_phase(kb, block, bufs, f_s, b_s, c_s)
        if upto >= 5.6:
            gla_phase(kb, block, bufs, q_s, f_s, b_s, c_s, g_s, vtok_s, og_s)
        if upto >= 5.7:
            gn_phase(kb, block, bufs, og_s, o_s, nw_sb)
        if upto >= 6:
            proj_postln_phase(kb, block, bufs, o_s, NCH, w_dram=l1_wo)
        if upto >= 6.1:
            ln_pre_phase(kb, block, bufs, dst_f32=xn32_s, want_max=True)
        if upto >= 6.2:
            quant_weights_phase(kb, block, bufs, l1_w1, l1_w2, w1qT_s,
                                w2qT_s, sw1, sw2, cc_in, cc_out, gcol)
        if upto >= 6.3:
            pass
        if upto >= 6.4:
            mm1_phase(kb, block, bufs, xn32_s, w1qT_s, y_s, sw1, b1col, gcol,
                      cc_in2, cc_out2, g2col)
        if upto >= 6.5:
            hq_quant_phase(kb, block, bufs, y_s, hq_s, sw2, b2col, g2col,
                           esc2, ebi2)
        if upto >= 7:
            proj_postln_phase(kb, block, bufs, hq_s, FF1 // 128,
                              w_bf_dram=w2qT_s, evac_scale_col=esc2,
                              evac_bias_col=ebi2, tt=256)

        def st(e):
            last = e.dma_start(out=outT.ap().rearrange("(c p) t -> p c t", p=128),
                               in_=X.ap())
            kb.ldh.tag(last, 16)
            e.wait_ge(kb.ldh.h, kb.ldh.v)
            e.wait_ge(kb.sth.h, kb.sth.v)
            e.wait_ge(kb.lds.h, kb.lds.v)
        block.sync(st)

    return nc


_NC_CACHE = {}


def kernel(**inputs):
    B, Tfull = 8, 4096

    def A(x, dt=np.float32):
        return np.ascontiguousarray(np.asarray(x), dtype=dt)

    def T_(x, dt=np.float32):
        return np.ascontiguousarray(np.asarray(x).T, dtype=dt)

    if "nc" not in _NC_CACHE:
        _NC_CACHE["nc"] = build_nc(T=Tfull, ncores=B)
    nc = _NC_CACHE["nc"]
    x = np.asarray(inputs["L0_attn_x"], dtype=np.float32)
    xT = np.ascontiguousarray(x.transpose(0, 2, 1))
    common = {
        "l0_wqT": T_(inputs["L0_attn_Wq"]),
        "l0_wkT": T_(inputs["L0_attn_Wk"]),
        "l0_wvT": T_(inputs["L0_attn_Wv"]),
        "l0_woT": T_(inputs["L0_attn_Wo"]),
        "l0_wgT": T_(inputs["L0_ffn_w_gate"], np.int32),
        "l0_wuT": T_(inputs["L0_ffn_w_up"], np.int32),
        "l0_wdT": T_(inputs["L0_ffn_w_down"], np.int32),
        "l1_wq": A(inputs["L1_attn_wq"]),
        "l1_wk": A(inputs["L1_attn_wk"]),
        "l1_wv": A(inputs["L1_attn_wv"]),
        "l1_wo": A(inputs["L1_attn_wo"]),
        "l1_wg1": A(inputs["L1_attn_wg1"]),
        "l1_wg2": A(inputs["L1_attn_wg2"]),
        "l1_nw": np.ascontiguousarray(
            np.asarray(inputs["L1_attn_norm_w"], dtype=np.float32)
            .reshape(NCH, 128).T),
        "l1_w1": A(inputs["L1_ffn_w1"]),
        "l1_b1": np.ascontiguousarray(
            np.asarray(inputs["L1_ffn_b1"], dtype=np.float32)
            .reshape(FF1 // 128, 128).T),
        "l1_w2": A(inputs["L1_ffn_w2"]),
        "l1_b2": np.ascontiguousarray(
            np.asarray(inputs["L1_ffn_b2"], dtype=np.float32)
            .reshape(NCH, 128).T),
    }
    in_maps = [{"xT": np.ascontiguousarray(xT[i]), **common} for i in range(B)]
    res = run_bass_kernel_spmd(nc, in_maps, core_ids=list(range(B)))
    out = np.stack([np.asarray(r["outT"], dtype=np.float32).T
                    for r in res.results])
    return np.ascontiguousarray(out)


# version 20
# speedup vs baseline: 1.0101x; 1.0101x over previous
from contextlib import ExitStack

import numpy as np
import concourse.bass as bass
import concourse.mybir as mybir
from concourse.bass_utils import run_bass_kernel_spmd

F32 = mybir.dt.float32
BF16 = mybir.dt.bfloat16
I32 = mybir.dt.int32
AF = mybir.ActivationFunctionType
AL = mybir.AluOpType
AX = mybir.AxisListType

D = 1024
NCH = 8
H0 = 16
HD0 = 64
WIN = 512
H1 = 8
D1 = 128
FF0 = 3072
FF1 = 4096
EPS_LN = 1e-5
EPS_GN = 1e-5
EPS_Q = 1e-8
INV127 = 1.0 / 127.0
TT = 512
RMAGIC = 12582912.0


_UID = [0]


def _u(name):
    _UID[0] += 1
    return f"{name}_{_UID[0]}"


class Sem:
    def __init__(self, nc, name):
        self.h = nc.semaphore(name).__enter__()
        self.v = 0

    def tag(self, inst, n=1):
        inst.then_inc(self.h, n)
        self.v += n
        return self.v


class Kb:

    def __init__(self, nc, T, ncores):
        self.nc = nc
        self.T = T
        self.NT = T // TT
        self.ncores = ncores
        self.ldh = Sem(nc, "ldh")
        self.ldh2 = Sem(nc, "ldh2")
        self.sth = Sem(nc, "sth")
        self.sth2 = Sem(nc, "sth2")
        self.lds = Sem(nc, "lds")
        self.lds2 = Sem(nc, "lds2")
        self.sts = Sem(nc, "sts")
        self.sts2 = Sem(nc, "sts2")
        self.pe = Sem(nc, "pe")
        self.ve = Sem(nc, "ve")
        self.se = Sem(nc, "se")
        self.ge = Sem(nc, "ge")
        self.cc = Sem(nc, "cc")
        self.ph = Sem(nc, "ph")
        self.nbar = 0

    def barrier(self, block):
        self.nbar += 1
        target = 5 * self.nbar
        h = self.ph.h
        self.ph.v = target
        pend = {"sync": [(self.ldh.h, self.ldh.v), (self.ldh2.h, self.ldh2.v),
                         (self.sts.h, self.sts.v), (self.sts2.h, self.sts2.v)],
                "scalar": [(self.sth.h, self.sth.v), (self.sth2.h, self.sth2.v)],
                "gpsimd": [(self.lds.h, self.lds.v), (self.lds2.h, self.lds2.v)]}
        for name, reg in (("sync", block.sync), ("scalar", block.scalar),
                          ("vector", block.vector), ("gpsimd", block.gpsimd),
                          ("tensor", block.tensor)):
            def body(e, h=h, target=target, p=pend.get(name)):
                for ph_, pv_ in (p or []):
                    e.wait_ge(ph_, pv_)
                e.sem_inc(h, 1)
                e.wait_ge(h, target)
            reg(body)


def _apply_ln_batched(e, X, src, mu, rstd, o32, obf, sl, th, residual,
                      tt_=TT):
    hw = tt_ // 2
    hs = slice(th * hw, (th + 1) * hw)
    mu_b = mu[:, hs].unsqueeze(1).to_broadcast([128, NCH, hw])
    rs_b = rstd[:, hs].unsqueeze(1).to_broadcast([128, NCH, hw])
    if not residual:
        xin = src[:, :, sl][:, :, hs]
        e.tensor_sub(o32[:, :, hs], xin, mu_b)
        e.drain()
        e.tensor_mul(o32[:, :, hs], o32[:, :, hs], rs_b)
        e.drain()
        if obf is not None:
            e.tensor_copy(obf[:, :, hs], o32[:, :, hs])
            e.drain()
    else:
        e.tensor_sub(o32[:, :, hs], src[:, :, hs], mu_b)
        e.drain()
        e.tensor_mul(o32[:, :, hs], o32[:, :, hs], rs_b)
        e.drain()
        Xs = X[:, :, sl]
        e.tensor_add(Xs[:, :, hs], Xs[:, :, hs], o32[:, :, hs])
        e.drain()


def ln_pre_phase(kb, block, bufs, dst_bf=None, dst_f32=None, want_max=False):
    nc, T, NT = kb.nc, kb.T, kb.NT
    X, ones = bufs["X"], bufs["ones128"]
    VCH = [0, 1, 2]
    GCH = [3, 4, 5, 6, 7]

    with (
        nc.sbuf_tensor(_u("ln_xb"), [128, NCH, TT], BF16) as xb,
        nc.sbuf_tensor(_u("ln_xsq"), [128, NCH, TT], BF16) as xsq,
        nc.sbuf_tensor(_u("ln_mu"), [128, TT], F32) as mu,
        nc.sbuf_tensor(_u("ln_es"), [128, TT], F32) as es,
        nc.sbuf_tensor(_u("ln_rstd"), [128, TT], F32) as rstd,
        nc.sbuf_tensor(_u("ln_o32"), [128, NCH, TT], F32) as o32,
        nc.sbuf_tensor(_u("ln_obf"), [128, NCH, TT], BF16) as obf,
        nc.sbuf_tensor(_u("ln_mxc"), [128, 1], F32) as mxc,
        nc.sbuf_tensor(_u("ln_mxt"), [128, 1], F32) as mxt,
        nc.psum_tensor(_u("ln_psA"), [128, TT], F32) as psA,
        nc.psum_tensor(_u("ln_psB"), [128, TT], F32) as psB,
    ):
        ve0, pe0, se0, ge0, ldh0 = kb.ve.v, kb.pe.v, kb.se.v, kb.ge.v, kb.ldh.v
        nst = (1 if dst_bf is not None else 0) + (1 if dst_f32 is not None else 0)
        VT = 5 if want_max else 4

        def st_done(i):
            return ldh0 + 16 * nst * (i + 1)

        def vec(e):
            for i in range(NT):
                sl = slice(i * TT, (i + 1) * TT)
                e.tensor_copy(xb.ap(), X[:, :, sl])
                e.drain()
                e.tensor_tensor(xsq.ap(), xb.ap(), xb.ap(), AL.mult)
                kb.ve.tag(e.drain())
                e.wait_ge(kb.pe.h, pe0 + i + 1)
                if i > 0:
                    e.wait_ge(kb.ge.h, ge0 + i)
                e.tensor_scalar(mu[:], psA[:], 1.0 / D, None, AL.mult)
                e.drain()
                e.tensor_scalar(es[:], psB[:], 1.0 / D, None, AL.mult)
                e.drain()
                e.tensor_tensor(rstd[:], mu[:], mu[:], AL.mult)
                e.drain()
                e.tensor_sub(es[:], es[:], rstd[:])
                kb.ve.tag(e.drain())
                e.wait_ge(kb.se.h, se0 + i + 1)
                e.reciprocal(rstd[:], es[:])
                kb.ve.tag(e.drain())
                if i > 0 and nst:
                    e.wait_ge(kb.ldh.h, st_done(i - 1))
                _apply_ln_batched(e, None, X, mu, rstd, o32, obf, sl, 0,
                                  residual=False)
                kb.ve.tag(e.drain())
                if want_max:
                    e.wait_ge(kb.ge.h, ge0 + i + 1)
                    for dc in range(NCH):
                        e.reduce_max(mxt[:], o32[:, dc, :], axis=AX.X)
                        e.drain()
                        if i == 0 and dc == 0:
                            e.tensor_copy(mxc[:], mxt[:])
                        else:
                            e.tensor_max(mxc[:], mxc[:], mxt[:])
                        e.drain()
                    kb.ve.tag(e.drain())

        def sca(e):
            for i in range(NT):
                e.wait_ge(kb.ve.h, ve0 + VT * i + 2)
                e.activation(es[:], es[:], AF.Sqrt, bias=bufs["ceps"][:], scale=1.0)
                kb.se.tag(e.drain())

        def ten(e):
            for i in range(NT):
                e.wait_ge(kb.ve.h, ve0 + VT * i + 1)
                for dc in range(NCH):
                    e.matmul(psA[:], ones[:], xb[:, dc, :], start=(dc == 0),
                             stop=(dc == NCH - 1))
                last = None
                for dc in range(NCH):
                    last = e.matmul(psB[:], ones[:], xsq[:, dc, :],
                                    start=(dc == 0), stop=(dc == NCH - 1))
                kb.pe.tag(last)

        def gp(e):
            for i in range(NT):
                sl = slice(i * TT, (i + 1) * TT)
                e.wait_ge(kb.ve.h, ve0 + VT * i + 3)
                if i > 0 and nst:
                    e.wait_ge(kb.ldh.h, st_done(i - 1))
                if i > 0 and want_max:
                    e.wait_ge(kb.ve.h, ve0 + VT * i)
                _apply_ln_batched(e, None, X, mu, rstd, o32, obf, sl, 1,
                                  residual=False)
                kb.ge.tag(e.drain())

        def st(e):
            for i in range(NT):
                sl_lo = i * TT
                e.wait_ge(kb.ve.h, ve0 + VT * i + 4)
                e.wait_ge(kb.ge.h, ge0 + i + 1)
                if dst_bf is not None:
                    dst = dst_bf.ap()[:, sl_lo:sl_lo + TT]
                    dst = dst.rearrange("(c p) t -> p c t", p=128)
                    kb.ldh.tag(e.dma_start(out=dst, in_=obf.ap()), 16)
                if dst_f32 is not None:
                    dst = dst_f32.ap()[:, sl_lo:sl_lo + TT]
                    dst = dst.rearrange("(c p) t -> p c t", p=128)
                    kb.ldh.tag(e.dma_start(out=dst, in_=o32.ap()), 16)
            if nst:
                e.wait_ge(kb.ldh.h, kb.ldh.v)
            assert kb.ldh.v == st_done(NT - 1), (kb.ldh.v, st_done(NT - 1))

        block.vector(vec)
        block.scalar(sca)
        block.tensor(ten)
        block.gpsimd(gp)
        block.sync(st)

        if want_max:
            def gmax(e):
                e.wait_ge(kb.ve.h, ve0 + VT * NT)
                e.tensor_reduce(bufs["mx1"][:], mxc[:], AX.C, AL.max)
                kb.ge.tag(e.drain())
            block.gpsimd(gmax)

        kb.barrier(block)


def simple_proj_phase(kb, block, bufs, w_dram, src_dram, dst_dram, func,
                      scale, out_dtype, w_rows=D, jobs=None):
    nc, T, NT = kb.nc, kb.T, kb.NT
    KC = w_rows // 128
    FC = D // 128
    if jobs is None:
        jobs = [(w_dram, dst_dram, func, scale, out_dtype)]
    NJ = len(jobs)
    with ExitStack() as ctx:
        wt = [ctx.enter_context(
            nc.sbuf_tensor(_u("pj_wt"), [128, KC, D], BF16)) for _ in jobs]
        ob = [ctx.enter_context(
            nc.sbuf_tensor(_u("pj_ob"), [128, FC, TT], j[4])) for j in jobs]
        mv0 = ctx.enter_context(
            nc.sbuf_tensor(_u("pj_mv0"), [128, KC, TT], BF16))
        mv1 = ctx.enter_context(
            nc.sbuf_tensor(_u("pj_mv1"), [128, KC, TT], BF16))
        psA = ctx.enter_context(nc.psum_tensor(_u("pj_psA"), [128, TT], F32))
        psB = ctx.enter_context(nc.psum_tensor(_u("pj_psB"), [128, TT], F32))

        mv = [mv0, mv1]
        ps = [psA, psB]
        L = [kb.ldh, kb.ldh2]
        S = [kb.sth, kb.sth2]
        ve0, pe0, se0, ge0, lds0 = (kb.ve.v, kb.pe.v, kb.se.v, kb.ge.v,
                                    kb.lds.v)
        l0 = [L[0].v, L[1].v]
        s0 = [S[0].v, S[1].v]

        def gld(e):
            for j, jb in enumerate(jobs):
                kb.lds.tag(e.dma_start(
                    out=wt[j].ap(),
                    in_=jb[0].ap().rearrange("(c p) f -> p c f", p=128)), 16)
        block.gpsimd(gld)

        def ld(e):
            for i in range(NT):
                if i >= 2:
                    e.wait_ge(kb.pe.h, pe0 + (i - 1) * FC * NJ)
                src = src_dram.ap()[:, i * TT:(i + 1) * TT]
                L[i % 2].tag(e.dma_start(
                    out=mv[i % 2].ap(),
                    in_=src.rearrange("(c p) t -> p c t", p=128)), 16)
        block.sync(ld)

        def ten(e):
            e.wait_ge(kb.lds.h, lds0 + 16 * NJ)
            g = 0
            for i in range(NT):
                e.wait_ge(L[i % 2].h, l0[i % 2] + 16 * (i // 2 + 1))
                for j in range(NJ):
                    for fc in range(FC):
                        if g >= 2:
                            e.wait_ge(kb.se.h, se0 + g - 1)
                        last = None
                        for kc in range(KC):
                            last = e.matmul(ps[g % 2][:],
                                            wt[j][:, kc, fc * 128:(fc + 1) * 128],
                                            mv[i % 2][:, kc, :],
                                            start=(kc == 0), stop=(kc == KC - 1))
                        kb.pe.tag(last)
                        g += 1
        block.tensor(ten)

        def sca(e):
            g = 0
            for i in range(NT):
                for j, jb in enumerate(jobs):
                    for fc in range(FC):
                        e.wait_ge(kb.pe.h, pe0 + g + 1)
                        if i > 0 and j == 0 and fc == 0:
                            p = (i - 1) % 2
                            e.wait_ge(S[p].h,
                                      s0[p] + 16 * NJ * ((i - 1) // 2 + 1))
                        e.activation(ob[j][:, fc, :], ps[g % 2][:], jb[2],
                                     bias=0.0, scale=jb[3])
                        kb.se.tag(e.drain())
                        g += 1
                for j, jb in enumerate(jobs):
                    dst = jb[1].ap()[:, i * TT:(i + 1) * TT]
                    dst = dst.rearrange("(c p) t -> p c t", p=128)
                    S[i % 2].tag(e.dma_start(out=dst, in_=ob[j].ap()), 16)
        block.scalar(sca)

        kb.barrier(block)


def v_proj_phase(kb, block, bufs, w_dram, src_dram, dst_dram, padded):
    nc, T = kb.nc, kb.T
    NTC = T // 128
    with (
        nc.sbuf_tensor(_u("vp_wt"), [128, NCH, D], BF16) as wt,
        nc.sbuf_tensor(_u("vp_mv0"), [128, NCH, 128], BF16) as mv0,
        nc.sbuf_tensor(_u("vp_mv1"), [128, NCH, 128], BF16) as mv1,
        nc.sbuf_tensor(_u("vp_ob0"), [128, 2, 512], BF16) as ob0,
        nc.sbuf_tensor(_u("vp_ob1"), [128, 2, 512], BF16) as ob1,
        nc.psum_tensor(_u("vp_psA"), [128, 512], F32) as psA,
        nc.psum_tensor(_u("vp_psB"), [128, 512], F32) as psB,
    ):
        mv = [mv0, mv1]
        ob = [ob0, ob1]
        ps = [psA, psB]
        L = [kb.ldh, kb.ldh2]
        S = [kb.sth, kb.sth2]
        ve0, pe0, se0, lds0 = kb.ve.v, kb.pe.v, kb.se.v, kb.lds.v
        l0 = [L[0].v, L[1].v]
        s0 = [S[0].v, S[1].v]

        def gld(e):
            kb.lds.tag(e.dma_start(
                out=wt.ap(),
                in_=w_dram.ap().rearrange("(c p) f -> p c f", p=128)), 16)
        block.gpsimd(gld)

        def ld(e):
            for i in range(NTC):
                if i >= 2:
                    e.wait_ge(kb.pe.h, pe0 + 2 * (i - 1))
                src = src_dram.ap()[:, i * 128:(i + 1) * 128]
                L[i % 2].tag(e.dma_start(
                    out=mv[i % 2].ap(),
                    in_=src.rearrange("(c p) t -> p c t", p=128)), 16)
        block.sync(ld)

        def ten(e):
            e.wait_ge(kb.lds.h, lds0 + 16)
            for i in range(NTC):
                e.wait_ge(L[i % 2].h, l0[i % 2] + 16 * (i // 2 + 1))
                if i >= 1:
                    e.wait_ge(kb.se.h, se0 + 2 * i)
                for fh in range(2):
                    last = None
                    for kc in range(NCH):
                        last = e.matmul(ps[fh][:],
                                        mv[i % 2][:, kc, :],
                                        wt[:, kc, fh * 512:(fh + 1) * 512],
                                        start=(kc == 0), stop=(kc == NCH - 1))
                    kb.pe.tag(last)
        block.tensor(ten)

        def sca(e):
            for i in range(NTC):
                if i >= 2:
                    j = (i - 2) % 2
                    e.wait_ge(S[j].h, s0[j] + 16 * ((i - 2) // 2 + 1))
                for fh in range(2):
                    e.wait_ge(kb.pe.h, pe0 + 2 * i + fh + 1)
                    e.activation(ob[i % 2][:, fh, :], ps[fh][:], AF.Copy,
                                 bias=0.0, scale=1.0)
                    kb.se.tag(e.drain())
                if padded:
                    dst = dst_dram.ap()[i * 128:(i + 1) * 128, :]
                    dst = dst.rearrange("p (h c) -> p h c", c=128)[:, :, 0:64]
                    srcap = ob[i % 2].ap().rearrange(
                        "p a t -> p (a t)").rearrange("p (h c) -> p h c", c=64)
                    S[i % 2].tag(e.dma_start(out=dst, in_=srcap), 16)
                else:
                    dst = dst_dram.ap()[i * 128:(i + 1) * 128, :]
                    S[i % 2].tag(e.dma_start(
                        out=dst, in_=ob[i % 2].ap().rearrange("p a t -> p (a t)")),
                        16)
        block.scalar(sca)

        kb.barrier(block)


def attn0_phase(kb, block, bufs, q_s, k_s, v_s, o_s):
    nc, T = kb.nc, kb.T
    NSEG = T // TT
    with (
        nc.sbuf_tensor(_u("at_qb0"), [128, TT], BF16) as qb0,
        nc.sbuf_tensor(_u("at_qb1"), [128, TT], BF16) as qb1,
        nc.sbuf_tensor(_u("at_kw0"), [128, 1024], BF16) as kw0,
        nc.sbuf_tensor(_u("at_kw1"), [128, 1024], BF16) as kw1,
        nc.sbuf_tensor(_u("at_vw0"), [128, 8, 256], BF16) as vw0,
        nc.sbuf_tensor(_u("at_vw1"), [128, 8, 256], BF16) as vw1,
        nc.sbuf_tensor(_u("at_pb0"), [128, 8, 2, TT], BF16) as pb0,
        nc.sbuf_tensor(_u("at_pb1"), [128, 8, 2, TT], BF16) as pb1,
        nc.sbuf_tensor(_u("at_msk"), [128, 8, TT], BF16) as msk,
        nc.sbuf_tensor(_u("at_ob"), [128, 8, TT], BF16) as obuf,
        nc.sbuf_tensor(_u("at_rc"), [128, TT], F32) as rc,
        nc.sbuf_tensor(_u("at_rc2"), [128, TT], F32) as rc2,
        nc.psum_tensor(_u("at_sc0"), [128, 2, TT], F32) as sc0,
        nc.psum_tensor(_u("at_sc1"), [128, 2, TT], F32) as sc1,
        nc.psum_tensor(_u("at_pOA"), [128, TT], F32) as pOA,
        nc.psum_tensor(_u("at_pOB"), [128, TT], F32) as pOB,
        nc.psum_tensor(_u("at_pSA"), [128, TT], F32) as pSA,
        nc.psum_tensor(_u("at_pSB"), [128, TT], F32) as pSB,
    ):
        qb = [qb0, qb1]
        kw = [kw0, kw1]
        vw = [vw0, vw1]
        pb = [pb0, pb1]
        sc = [sc0, sc1]
        L = [kb.ldh, kb.ldh2]
        S = [kb.sth, kb.sth2]
        ones = bufs["ones128"]

        def kcs_of(s):
            return list(range(4, 8)) if s == 0 else list(range(8))

        DVEI = (1, 3, 5, 7)

        def dve_idx(s):
            return [i for i in range(len(kcs_of(s))) if i in DVEI]

        def pool_idx(s):
            return [i for i in range(len(kcs_of(s))) if i not in DVEI]

        groups = [(s, hp) for s in range(NSEG) for hp in range(8)]
        NG = len(groups)
        ve0, pe0, se0, ge0 = kb.ve.v, kb.pe.v, kb.se.v, kb.ge.v
        l0 = [L[0].v, L[1].v]
        s0 = [S[0].v, S[1].v]

        def mk(e):
            for kc in range(8):
                e.affine_select(msk[:, kc, :], ones[:, 0:1].to_broadcast(
                    [128, TT]), [[1, TT]], AL.is_ge, 0.0,
                    base=512 - kc * 128, channel_multiplier=-1)
                e.drain()
                e.affine_select(msk[:, kc, :], msk[:, kc, :], [[-1, TT]],
                                AL.is_ge, 0.0, base=kc * 128 - 1,
                                channel_multiplier=1)
                e.drain()
            kb.ge.tag(e.drain())
        block.gpsimd(mk)
        ge0 += 1

        pe_sc = {}
        pe_pv = {}
        v = pe0
        for g, (s, hp) in enumerate(groups):
            for kci in range(len(kcs_of(s))):
                v += 1
                pe_sc[(g, kci)] = v
                if kci == 1 and g >= 1:
                    v += 1
                    pe_pv[g - 1] = v
        v += 1
        pe_pv[NG - 1] = v
        se_t = {}
        se_sg = {}
        v = se0
        for g, (s, hp) in enumerate(groups):
            for kci in range(len(kcs_of(s))):
                v += 1
                se_t[(g, kci)] = v
                if kci == 1 and g >= 1:
                    v += 1
                    se_sg[g - 1] = v
        v += 1
        se_sg[NG - 1] = v
        ge_m = {}
        v = ge0
        for g, (s, hp) in enumerate(groups):
            for kci in pool_idx(s):
                v += 1
                ge_m[(g, kci)] = v
        ve_m = {}
        ve_r = {}
        v = ve0
        for g, (s, hp) in enumerate(groups):
            for kci in dve_idx(s):
                v += 1
                ve_m[(g, kci)] = v
            v += 1
            ve_r[g] = v

        def last_same_parity(g, kci):
            n = len(kcs_of(groups[g][0]))
            j = n - 1
            while j % 2 != kci % 2:
                j -= 1
            return j

        def ld_done(g):
            return l0[g % 2] + 48 * (g // 2 + 1)
        def st_done(s):
            return s0[s % 2] + 16 * (s // 2 + 1)

        def ld2(e):
            for g, (s, hp) in enumerate(groups):
                if g >= 2:
                    e.wait_ge(kb.pe.h, pe_pv[g - 2])
                Q0 = s * TT
                kl = Q0 - 512
                L[g % 2].tag(e.dma_start(
                    out=qb[g % 2][:, :],
                    in_=q_s.ap()[hp * 128:(hp + 1) * 128, Q0:Q0 + TT]), 16)
                if s == 0:
                    L[g % 2].tag(e.dma_start(
                        out=kw[g % 2][:, 512:1024],
                        in_=k_s.ap()[hp * 128:(hp + 1) * 128, 0:512]), 16)
                    vsrc = v_s.ap()[0:512, hp * 256:(hp + 1) * 256]
                    L[g % 2].tag(e.dma_start(
                        out=vw[g % 2][:, 4:8, :],
                        in_=vsrc.rearrange("(c p) f -> p c f", p=128)), 16)
                else:
                    L[g % 2].tag(e.dma_start(
                        out=kw[g % 2][:, :],
                        in_=k_s.ap()[hp * 128:(hp + 1) * 128, kl:kl + 1024]),
                        16)
                    vsrc = v_s.ap()[kl:kl + 1024, hp * 256:(hp + 1) * 256]
                    L[g % 2].tag(e.dma_start(
                        out=vw[g % 2][:, :, :],
                        in_=vsrc.rearrange("(c p) f -> p c f", p=128)), 16)
        block.sync(ld2)

        def emit_pv(e, g):
            s, hp = groups[g]
            kcs = kcs_of(s)
            e.wait_ge(kb.ge.h, ge_m[(g, pool_idx(s)[-1])])
            e.wait_ge(kb.ve.h, ve_m[(g, dve_idx(s)[-1])])
            if g >= 1:
                e.wait_ge(kb.ve.h, ve_r[g - 1])
                e.wait_ge(kb.se.h, se_sg[g - 1])
            for kci, kc in enumerate(kcs):
                first, last_f = (kci == 0), (kci == len(kcs) - 1)
                e.matmul(pOA[0:64, :], vw[g % 2][:, kc, 0:64],
                         pb[g % 2][:, kc, 0, :], start=first,
                         stop=last_f, tile_position=(0, 0))
                e.matmul(pOB[64:128, :], vw[g % 2][:, kc, 128:192],
                         pb[g % 2][:, kc, 1, :], start=first,
                         stop=last_f, tile_position=(0, 64))
                e.matmul(pSA[0:64, :], ones[:, 0:64],
                         pb[g % 2][:, kc, 0, :], start=first,
                         stop=last_f, tile_position=(0, 0))
                lastmm = e.matmul(pSB[64:128, :], ones[:, 0:64],
                                  pb[g % 2][:, kc, 1, :], start=first,
                                  stop=last_f, tile_position=(0, 64))
            kb.pe.tag(lastmm)
            assert kb.pe.v == pe_pv[g]

        def ten(e):
            for g, (s, hp) in enumerate(groups):
                kcs = kcs_of(s)
                e.wait_ge(L[g % 2].h, ld_done(g))
                for kci, kc in enumerate(kcs):
                    if kci >= 2:
                        e.wait_ge(kb.se.h, se_t[(g, kci - 2)])
                    elif g >= 1:
                        e.wait_ge(kb.se.h,
                                  se_t[(g - 1, last_same_parity(g - 1, kci))])
                    st_a = kw[g % 2][0:64, kc * 128:(kc + 1) * 128]
                    st_b = kw[g % 2][64:128, kc * 128:(kc + 1) * 128]
                    e.matmul(sc[kci % 2][:, 0, :], st_a, qb[g % 2][0:64, :],
                             start=True, stop=True, tile_position=(0, 0))
                    last = e.matmul(sc[kci % 2][:, 1, :], st_b,
                                    qb[g % 2][64:128, :], start=True,
                                    stop=True, tile_position=(64, 0))
                    kb.pe.tag(last)
                    assert kb.pe.v == pe_sc[(g, kci)]
                    if kci == 1 and g >= 1:
                        emit_pv(e, g - 1)
            emit_pv(e, NG - 1)
        block.tensor(ten)

        def emit_sigma(e, g):
            s, hp = groups[g]
            e.wait_ge(kb.pe.h, pe_pv[g])
            if g >= 1:
                e.wait_ge(kb.ve.h, ve_r[g - 1])
            e.activation(rc[0:64, :], pSA[0:64, :], AF.Ln, bias=0.0,
                         scale=1.0)
            e.drain()
            e.activation(rc[64:128, :], pSB[64:128, :], AF.Ln, bias=0.0,
                         scale=1.0)
            e.drain()
            e.activation(rc2[0:64, :], rc[0:64, :], AF.Exp, bias=0.0,
                         scale=-1.0)
            e.drain()
            e.activation(rc2[64:128, :], rc[64:128, :], AF.Exp, bias=0.0,
                         scale=-1.0)
            kb.se.tag(e.drain())
            assert kb.se.v == se_sg[g]
            if hp == 7:
                e.wait_ge(kb.ve.h, ve_r[g])
                dst = o_s.ap()[:, s * TT:(s + 1) * TT]
                dst = dst.rearrange("(c p) t -> p c t", p=128)
                S[s % 2].tag(e.dma_start(out=dst, in_=obuf.ap()), 16)

        def sca(e):
            for g, (s, hp) in enumerate(groups):
                kcs = kcs_of(s)
                if g >= 2:
                    e.wait_ge(kb.pe.h, pe_pv[g - 2])
                for kci, kc in enumerate(kcs):
                    e.wait_ge(kb.pe.h, pe_sc[(g, kci)])
                    e.activation(pb[g % 2][:, kc, :, :], sc[kci % 2].ap(),
                                 AF.Exp, bias=0.0, scale=0.125)
                    kb.se.tag(e.drain())
                    assert kb.se.v == se_t[(g, kci)]
                    if kci == 1 and g >= 1:
                        emit_sigma(e, g - 1)
            emit_sigma(e, NG - 1)
        block.scalar(sca)

        def gp(e):
            e.wait_ge(kb.ge.h, ge0)
            for g, (s, hp) in enumerate(groups):
                kcs = kcs_of(s)
                for kci in pool_idx(s):
                    kc = kcs[kci]
                    e.wait_ge(kb.se.h, se_t[(g, kci)])
                    e.tensor_mul(pb[g % 2][:, kc, :, :],
                                 pb[g % 2][:, kc, :, :],
                                 msk[:, kc, :].unsqueeze(1).to_broadcast(
                                     [128, 2, TT]))
                    kb.ge.tag(e.drain())
                    assert kb.ge.v == ge_m[(g, kci)]
        block.gpsimd(gp)

        def vec(e):
            e.wait_ge(kb.ge.h, ge0)
            for g, (s, hp) in enumerate(groups):
                kcs = kcs_of(s)
                for kci in dve_idx(s):
                    kc = kcs[kci]
                    e.wait_ge(kb.se.h, se_t[(g, kci)])
                    e.tensor_mul(pb[g % 2][:, kc, :, :],
                                 pb[g % 2][:, kc, :, :],
                                 msk[:, kc, :].unsqueeze(1).to_broadcast(
                                     [128, 2, TT]))
                    kb.ve.tag(e.drain())
                    assert kb.ve.v == ve_m[(g, kci)]
                e.wait_ge(kb.se.h, se_sg[g])
                if hp == 0 and s >= 1:
                    e.wait_ge(S[(s - 1) % 2].h, st_done(s - 1))
                e.tensor_mul(obuf[0:64, hp, :], pOA[0:64, :], rc2[0:64, :])
                e.drain()
                e.tensor_mul(obuf[64:128, hp, :], pOB[64:128, :],
                             rc2[64:128, :])
                kb.ve.tag(e.drain())
                assert kb.ve.v == ve_r[g]
        block.vector(vec)

        kb.barrier(block)


def proj_postln_phase(kb, block, bufs, src_dram, KC, w_dram=None,
                      w_bf_dram=None, evac_scale=1.0, evac_scale_col=None,
                      evac_bias_col=None, w_tiles_dram=None, tt=TT):
    nc, T = kb.nc, kb.T
    NT = T // tt
    X, ones = bufs["X"], bufs["ones128"]
    FC = NCH
    VCH = [0, 1, 2]
    GCH = [3, 4, 5, 6, 7]
    resident = w_dram is not None
    with (
        nc.sbuf_tensor(_u("pp_wt"), [128, KC, D if resident else 256], BF16) as wt,
        nc.sbuf_tensor(_u("pp_mv0"), [128, KC, tt], BF16) as mv0,
        nc.sbuf_tensor(_u("pp_mv1"), [128, KC, tt], BF16) as mv1,
        nc.sbuf_tensor(_u("pp_y"), [128, NCH, tt], F32) as y32,
        nc.sbuf_tensor(_u("pp_yb"), [128, NCH, tt], BF16) as ybf,
        nc.sbuf_tensor(_u("pp_ys"), [128, NCH, tt], BF16) as ysq,
        nc.sbuf_tensor(_u("pp_mu"), [128, tt], F32) as mu,
        nc.sbuf_tensor(_u("pp_es"), [128, tt], F32) as es,
        nc.sbuf_tensor(_u("pp_rstd"), [128, tt], F32) as rstd,
        nc.psum_tensor(_u("pp_pY0"), [128, tt], F32) as pY0,
        nc.psum_tensor(_u("pp_pY1"), [128, tt], F32) as pY1,
        nc.psum_tensor(_u("pp_psA"), [128, tt], F32) as psA,
        nc.psum_tensor(_u("pp_psB"), [128, tt], F32) as psB,
    ):
        mv = [mv0, mv1]
        pY = [pY0, pY1]
        L = [kb.ldh, kb.ldh2]
        S = [kb.sth, kb.sth2]
        ve0, pe0, se0, ge0, lds0 = (kb.ve.v, kb.pe.v, kb.se.v, kb.ge.v,
                                    kb.lds.v)
        l0 = [L[0].v, L[1].v]
        s0 = [S[0].v, S[1].v]

        if resident:
            def gld(e):
                kb.lds.tag(e.dma_start(
                    out=wt.ap()[:, :, 0:D],
                    in_=w_dram.ap().rearrange("(c p) f -> p c f", p=128)), 16)
            block.gpsimd(gld)

        def gidx(i, fc):
            return i * FC + fc

        def pe_proj(i, fc):
            return pe0 + i * (FC + 1) + fc + 1

        def pe_stats(i):
            return pe0 + (i + 1) * (FC + 1)

        def wst_done(g):
            return s0[g % 2] + 16 * (g // 2 + 1)

        def emit_wload(e, g):
            ii, ff = divmod(g, FC)
            if g >= 2:
                e.wait_ge(kb.pe.h, pe0 + ((g - 2) // FC) * (FC + 1)
                          + ((g - 2) % FC) + 1)
            srcw = w_bf_dram.ap()[:, ff * 128:(ff + 1) * 128]
            srcw = srcw.rearrange("(c p) f -> p c f", p=128)
            S[g % 2].tag(e.dma_start(
                out=wt[:, :, (g % 2) * 128:(g % 2) * 128 + 128], in_=srcw), 16)

        def ld(e):
            wp = 0
            for i in range(NT):
                if i >= 2:
                    e.wait_ge(kb.pe.h, pe0 + (i - 2) * (FC + 1) + FC)
                src = src_dram.ap()[:, i * tt:(i + 1) * tt]
                L[i % 2].tag(e.dma_start(
                    out=mv[i % 2].ap(),
                    in_=src.rearrange("(c p) t -> p c t", p=128)), 16)
                if not resident:
                    while wp < min((i + 1) * FC + 2, NT * FC):
                        if wp % 2 == 0:
                            emit_wload(e, wp)
                        wp += 1
            if not resident:
                while wp < NT * FC:
                    if wp % 2 == 0:
                        emit_wload(e, wp)
                    wp += 1
        block.sync(ld)

        def sca(e):
            if not resident:
                emit_wload(e, 1)
            for i in range(NT):
                for fc in range(FC):
                    g = gidx(i, fc)
                    e.wait_ge(kb.pe.h, pe_proj(i, fc))
                    if fc == 0 and i > 0:
                        e.wait_ge(kb.ve.h, ve0 + 4 * i)
                        e.wait_ge(kb.ge.h, ge0 + i)
                    if evac_scale_col is not None:
                        e.activation(y32[:, fc, :], pY[g % 2][:],
                                     AF.Identity, bias=evac_bias_col[:, fc:fc + 1],
                                     scale=evac_scale_col[:, fc:fc + 1])
                    else:
                        e.activation(y32[:, fc, :], pY[g % 2][:],
                                     AF.Identity, bias=0.0, scale=evac_scale)
                    kb.se.tag(e.drain())
                    if (not resident) and g + 2 < NT * FC and (g % 2) == 1:
                        emit_wload(e, g + 2)
                e.wait_ge(kb.ve.h, ve0 + 4 * i + 2)
                e.activation(es[:], es[:], AF.Sqrt, bias=bufs["ceps"][:],
                             scale=1.0)
                kb.se.tag(e.drain())
        block.scalar(sca)

        def ten(e):
            if resident:
                e.wait_ge(kb.lds.h, lds0 + 16)
            for i in range(NT):
                e.wait_ge(L[i % 2].h, l0[i % 2] + 16 * (i // 2 + 1))
                for fc in range(FC):
                    g = gidx(i, fc)
                    if not resident:
                        e.wait_ge(S[g % 2].h, wst_done(g))
                    if g >= 2:
                        gp_ = g - 2
                        tgt = se0 + (gp_ // FC) * (FC + 1) + (gp_ % FC) + 1
                        e.wait_ge(kb.se.h, tgt)
                    last = None
                    for kc in range(KC):
                        stat = (wt[:, kc, fc * 128:(fc + 1) * 128] if resident
                                else wt[:, kc, (g % 2) * 128:(g % 2) * 128 + 128])
                        last = e.matmul(pY[g % 2][:], stat, mv[i % 2][:, kc, :],
                                        start=(kc == 0), stop=(kc == KC - 1))
                    kb.pe.tag(last)
                    assert kb.pe.v == pe_proj(i, fc)
                e.wait_ge(kb.ve.h, ve0 + 4 * i + 1)
                for dc in range(NCH):
                    e.matmul(psA[:], ones[:, 0:128], ybf[:, dc, :],
                             start=(dc == 0), stop=(dc == NCH - 1))
                last = None
                for dc in range(NCH):
                    last = e.matmul(psB[:], ones[:, 0:128], ysq[:, dc, :],
                                    start=(dc == 0), stop=(dc == NCH - 1))
                kb.pe.tag(last)
                assert kb.pe.v == pe_stats(i)
        block.tensor(ten)

        def vec(e):
            for i in range(NT):
                sl = slice(i * tt, (i + 1) * tt)
                e.wait_ge(kb.se.h, se0 + i * (FC + 1) + FC)
                e.tensor_copy(ybf.ap(), y32.ap())
                e.drain()
                e.tensor_tensor(ysq.ap(), ybf.ap(), ybf.ap(), AL.mult)
                kb.ve.tag(e.drain())
                e.wait_ge(kb.pe.h, pe_stats(i))
                if i > 0:
                    e.wait_ge(kb.ge.h, ge0 + i)
                e.tensor_scalar(mu[:], psA[:], 1.0 / D, None, AL.mult)
                e.drain()
                e.tensor_scalar(es[:], psB[:], 1.0 / D, None, AL.mult)
                e.drain()
                e.tensor_tensor(rstd[:], mu[:], mu[:], AL.mult)
                e.drain()
                e.tensor_sub(es[:], es[:], rstd[:])
                kb.ve.tag(e.drain())
                e.wait_ge(kb.se.h, se0 + i * (FC + 1) + FC + 1)
                e.reciprocal(rstd[:], es[:])
                kb.ve.tag(e.drain())
                _apply_ln_batched(e, X, y32, mu, rstd, y32, None, sl, 0,
                                  residual=True, tt_=tt)
                kb.ve.tag(e.drain())

        block.vector(vec)

        def gp(e):
            for i in range(NT):
                sl = slice(i * tt, (i + 1) * tt)
                e.wait_ge(kb.ve.h, ve0 + 4 * i + 3)
                _apply_ln_batched(e, X, y32, mu, rstd, y32, None, sl, 1,
                                  residual=True, tt_=tt)
                kb.ge.tag(e.drain())
        block.gpsimd(gp)

        kb.barrier(block)


def cast_weights_phase(kb, block, bufs, pairs):
    nc = kb.nc
    jobs = []
    for src, dst in pairs:
        R, C = src.ap().shape
        for r in range(R // 128):
            jobs.append((src, dst, r, C))
    CMAX = max(j[3] for j in jobs)
    with (
        nc.sbuf_tensor(_u("cw_b0"), [128, CMAX], BF16) as b0,
        nc.sbuf_tensor(_u("cw_b1"), [128, CMAX], BF16) as b1,
    ):
        bb = [b0, b1]
        Ls = [kb.lds, kb.lds2]
        Ss = [kb.sth, kb.sth2]
        l0 = [Ls[0].v, Ls[1].v]
        s0 = [Ss[0].v, Ss[1].v]

        def gld(e):
            for j, (src, dst, r, C) in enumerate(jobs):
                if j >= 2:
                    e.wait_ge(Ss[j % 2].h, s0[j % 2] + 16 * ((j - 2) // 2 + 1))
                Ls[j % 2].tag(e.dma_start(
                    out=bb[j % 2][:, 0:C],
                    in_=src.ap()[r * 128:(r + 1) * 128, :]), 16)
        block.gpsimd(gld)

        def st(e):
            for j, (src, dst, r, C) in enumerate(jobs):
                e.wait_ge(Ls[j % 2].h, l0[j % 2] + 16 * (j // 2 + 1))
                Ss[j % 2].tag(e.dma_start(
                    out=dst.ap()[r * 128:(r + 1) * 128, :],
                    in_=bb[j % 2][:, 0:C]), 16)
        block.scalar(st)

        kb.barrier(block)


def swiglu_phase(kb, block, bufs, wg_b, wu_b, xn_s, h_s):
    nc, T = kb.nc, kb.T
    TH = min(T // 2, 1024)
    NPASS = T // TH
    SUB = min(512, TH)
    NSUB = TH // SUB
    HC = FF0 // 128
    with (
        nc.sbuf_tensor(_u("sw_xn0"), [128, NCH, TH], BF16) as xn0,
        nc.sbuf_tensor(_u("sw_xn1"), [128, NCH, TH], BF16) as xn1,
        nc.sbuf_tensor(_u("sw_wg0"), [128, NCH, 128], BF16) as wg0,
        nc.sbuf_tensor(_u("sw_wg1"), [128, NCH, 128], BF16) as wg1,
        nc.sbuf_tensor(_u("sw_wu0"), [128, NCH, 128], BF16) as wu0,
        nc.sbuf_tensor(_u("sw_wu1"), [128, NCH, 128], BF16) as wu1,
        nc.sbuf_tensor(_u("sw_t10"), [128, SUB], F32) as t10,
        nc.sbuf_tensor(_u("sw_t11"), [128, SUB], F32) as t11,
        nc.sbuf_tensor(_u("sw_hb0"), [128, SUB], BF16) as hb0,
        nc.sbuf_tensor(_u("sw_hb1"), [128, SUB], BF16) as hb1,
        nc.psum_tensor(_u("sw_pG0"), [128, SUB], F32) as pG0,
        nc.psum_tensor(_u("sw_pG1"), [128, SUB], F32) as pG1,
        nc.psum_tensor(_u("sw_pU0"), [128, SUB], F32) as pU0,
        nc.psum_tensor(_u("sw_pU1"), [128, SUB], F32) as pU1,
    ):
        xnb = [xn0, xn1]
        wg = [wg0, wg1]
        wu = [wu0, wu1]
        t1 = [t10, t11]
        hb = [hb0, hb1]
        pG = [pG0, pG1]
        pU = [pU0, pU1]
        L = [kb.ldh, kb.ldh2]
        S = [kb.sth, kb.sth2]
        ve0, pe0, se0, ge0 = kb.ve.v, kb.pe.v, kb.se.v, kb.ge.v
        l0 = [L[0].v, L[1].v]
        s0 = [S[0].v, S[1].v]

        ld_tgt = {}
        lv = [l0[0], l0[1]]
        for h in range(NPASS):
            p = (h * (HC + 1)) % 2
            lv[p] += 16
            ld_tgt[("xn", h)] = (p, lv[p])
            for hc in range(HC):
                p2 = (h * (HC + 1) + hc + 1) % 2
                lv[p2] += 32
                ld_tgt[("w", h, hc)] = (p2, lv[p2])

        NG = NPASS * HC * NSUB

        def pe_t(g):
            return pe0 + g + 1

        def ld(e):
            for h in range(NPASS):
                p, tgt = ld_tgt[("xn", h)]
                if h >= 1:
                    e.wait_ge(kb.pe.h, pe_t((h * HC - 1) * NSUB))
                if h >= 2:
                    e.wait_ge(kb.pe.h, pe_t(((h - 1) * HC) * NSUB - 1 + 1))
                src = xn_s.ap()[:, h * TH:(h + 1) * TH]
                L[p].tag(e.dma_start(
                    out=xnb[h % 2].ap(),
                    in_=src.rearrange("(c p) t -> p c t", p=128)), 16)
                for hc in range(HC):
                    gslab = h * HC + hc
                    if gslab < 2:
                        e.wait_ge(L[p].h, tgt)
                    if gslab >= 2:
                        e.wait_ge(kb.pe.h, pe_t((gslab - 1) * NSUB))
                    p2, tgt2 = ld_tgt[("w", h, hc)]
                    L[p2].tag(e.dma_start(
                        out=wg[gslab % 2].ap(),
                        in_=wg_b.ap()[:, hc * 128:(hc + 1) * 128].rearrange(
                            "(c p) f -> p c f", p=128)), 16)
                    L[p2].tag(e.dma_start(
                        out=wu[gslab % 2].ap(),
                        in_=wu_b.ap()[:, hc * 128:(hc + 1) * 128].rearrange(
                            "(c p) f -> p c f", p=128)), 16)
        block.sync(ld)

        def ten(e):
            for h in range(NPASS):
                p, tgt = ld_tgt[("xn", h)]
                e.wait_ge(L[p].h, tgt)
                for hc in range(HC):
                    gslab = h * HC + hc
                    p2, tgt2 = ld_tgt[("w", h, hc)]
                    e.wait_ge(L[p2].h, tgt2)
                    for ts in range(NSUB):
                        g = gslab * NSUB + ts
                        if g >= 2:
                            e.wait_ge(kb.se.h, se0 + g - 1)
                            e.wait_ge(kb.ve.h, ve0 + g - 1)
                        sl = slice(ts * SUB, (ts + 1) * SUB)
                        for kc in range(NCH):
                            e.matmul(pG[g % 2][:], wg[gslab % 2][:, kc, :],
                                     xnb[h % 2][:, kc, sl], start=(kc == 0),
                                     stop=(kc == NCH - 1))
                        last = None
                        for kc in range(NCH):
                            last = e.matmul(pU[g % 2][:], wu[gslab % 2][:, kc, :],
                                            xnb[h % 2][:, kc, sl], start=(kc == 0),
                                            stop=(kc == NCH - 1))
                        kb.pe.tag(last)
                        assert kb.pe.v == pe_t(g)
        block.tensor(ten)

        def emit_store(e, g):
            gslab, ts = divmod(g, NSUB)
            h, hc = divmod(gslab, HC)
            e.wait_ge(kb.ve.h, ve0 + g + 1)
            t_lo = h * TH + ts * SUB
            S[g % 2].tag(e.dma_start(
                out=h_s.ap()[hc * 128:(hc + 1) * 128, t_lo:t_lo + SUB],
                in_=hb[g % 2][:, :]), 16)

        def sca(e):
            for g in range(NG):
                e.wait_ge(kb.pe.h, pe_t(g))
                if g >= 2:
                    e.wait_ge(kb.ve.h, ve0 + g - 1)
                e.activation(t1[g % 2][:], pG[g % 2][:], AF.Silu, bias=0.0,
                             scale=INV127)
                kb.se.tag(e.drain())
                if g >= 1:
                    emit_store(e, g - 1)
            emit_store(e, NG - 1)
        block.scalar(sca)

        def vec(e):
            for g in range(NG):
                e.wait_ge(kb.se.h, se0 + g + 1)
                if g >= 2:
                    j = (g - 2) % 2
                    e.wait_ge(S[j].h, s0[j] + 16 * ((g - 2) // 2 + 1))
                e.scalar_tensor_tensor(hb[g % 2][:], pU[g % 2][:], INV127,
                                       t1[g % 2][:], AL.mult, AL.mult)
                kb.ve.tag(e.drain())
        block.vector(vec)

        kb.barrier(block)


def gate_phase(kb, block, bufs, wg1, wg2, xn_s, g_s):
    nc, T, NT = kb.nc, kb.T, kb.NT
    with (
        nc.sbuf_tensor(_u("gt_w1"), [128, NCH, 128], BF16) as w1t,
        nc.sbuf_tensor(_u("gt_w2"), [128, D], BF16) as w2t,
        nc.sbuf_tensor(_u("gt_mv0"), [128, NCH, TT], BF16) as mv0,
        nc.sbuf_tensor(_u("gt_mv1"), [128, NCH, TT], BF16) as mv1,
        nc.sbuf_tensor(_u("gt_g1"), [128, TT], BF16) as g1b,
        nc.sbuf_tensor(_u("gt_ob"), [128, NCH, TT], F32) as ob,
        nc.psum_tensor(_u("gt_p1"), [128, TT], F32) as p1,
        nc.psum_tensor(_u("gt_p20"), [128, TT], F32) as p20,
        nc.psum_tensor(_u("gt_p21"), [128, TT], F32) as p21,
    ):
        mv = [mv0, mv1]
        p2 = [p20, p21]
        L = [kb.ldh, kb.ldh2]
        S = [kb.sth, kb.sth2]
        ve0, pe0, se0, lds0 = kb.ve.v, kb.pe.v, kb.se.v, kb.lds.v
        l0 = [L[0].v, L[1].v]
        s0 = [S[0].v, S[1].v]

        def gld(e):
            kb.lds.tag(e.dma_start(
                out=w1t.ap(),
                in_=wg1.ap().rearrange("(c p) f -> p c f", p=128)), 16)
            kb.lds.tag(e.dma_start(out=w2t.ap(), in_=wg2.ap()), 16)
        block.gpsimd(gld)

        def pe_g1(i):
            return pe0 + i * 9 + 1

        def pe_g2(i, dc):
            return pe0 + i * 9 + 1 + dc + 1

        def se_ev(i):
            return se0 + i * 9 + 1

        def se_sg(i, dc):
            return se0 + i * 9 + 1 + dc + 1

        def ld(e):
            for i in range(NT):
                if i >= 2:
                    e.wait_ge(kb.pe.h, pe_g1(i - 1))
                src = xn_s.ap()[:, i * TT:(i + 1) * TT]
                L[i % 2].tag(e.dma_start(
                    out=mv[i % 2].ap(),
                    in_=src.rearrange("(c p) t -> p c t", p=128)), 16)
        block.sync(ld)

        def ten(e):
            e.wait_ge(kb.lds.h, lds0 + 32)
            for i in range(NT):
                e.wait_ge(L[i % 2].h, l0[i % 2] + 16 * (i // 2 + 1))
                if i >= 1:
                    e.wait_ge(kb.se.h, se_ev(i - 1))
                last = None
                for kc in range(NCH):
                    last = e.matmul(p1[:], w1t[:, kc, :], mv[i % 2][:, kc, :],
                                    start=(kc == 0), stop=(kc == NCH - 1))
                kb.pe.tag(last)
                e.wait_ge(kb.se.h, se_ev(i))
                for dc in range(NCH):
                    if i * 8 + dc >= 2:
                        gprev = i * 8 + dc - 2
                        ip, dp = divmod(gprev, 8)
                        e.wait_ge(kb.se.h, se_sg(ip, dp))
                    kb.pe.tag(e.matmul(p2[(i * 8 + dc) % 2][:],
                                       w2t[:, dc * 128:(dc + 1) * 128],
                                       g1b[:], start=True, stop=True))
        block.tensor(ten)

        def sca(e):
            for i in range(NT):
                e.wait_ge(kb.pe.h, pe_g1(i))
                e.activation(g1b[:], p1[:], AF.Copy, bias=0.0, scale=1.0)
                kb.se.tag(e.drain())
                for dc in range(NCH):
                    e.wait_ge(kb.pe.h, pe_g2(i, dc))
                    if i > 0 and dc == 0:
                        j = (i - 1) % 2
                        e.wait_ge(S[j].h, s0[j] + 16 * ((i - 1) // 2 + 1))
                    e.activation(ob[:, dc, :], p2[(i * 8 + dc) % 2][:],
                                 AF.Sigmoid, bias=0.0, scale=1.0)
                    kb.se.tag(e.drain())
                dst = g_s.ap()[:, i * TT:(i + 1) * TT]
                dst = dst.rearrange("(c p) t -> p c t", p=128)
                S[i % 2].tag(e.dma_start(out=dst, in_=ob.ap()), 16)
        block.scalar(sca)

        kb.barrier(block)


def decay_prep_phase(kb, block, bufs, f_s, b_s, c_s):
    nc, T = kb.nc, kb.T
    NC = T // 128
    S32 = T // 32
    with (
        nc.sbuf_tensor(_u("dp_f0"), [128, T], F32) as f0,
        nc.sbuf_tensor(_u("dp_f1"), [128, T], F32) as f1,
        nc.sbuf_tensor(_u("dp_b1"), [128, T], F32) as b1,
        nc.sbuf_tensor(_u("dp_b2"), [128, T // 2], F32) as b2,
        nc.sbuf_tensor(_u("dp_ub"), [128, T], BF16) as ub,
        nc.sbuf_tensor(_u("dp_ub2"), [128, T], BF16) as ub2,
        nc.sbuf_tensor(_u("dp_ss"), [128, S32], F32) as ss,
        nc.sbuf_tensor(_u("dp_ss2"), [128, S32], F32) as ss2,
        nc.sbuf_tensor(_u("dp_cc"), [128, S32], F32) as cc,
    ):
        fb = [f0, f1]
        L = [kb.ldh, kb.ldh2]
        S = [kb.sth, kb.sth2]
        ve0, pe0, se0, ge0 = kb.ve.v, kb.pe.v, kb.se.v, kb.ge.v
        lds0 = kb.lds.v
        l0 = [L[0].v, L[1].v]
        s0 = [S[0].v, S[1].v]

        def st_bnd(dc):
            return s0[dc % 2] + 16 * (dc // 2 + 1)

        def V(t):
            return t.ap().rearrange("p (s c) -> p s c", c=32)

        def V4(t):
            return t.ap().rearrange("p (n q c) -> p n q c", q=4, c=32)

        def ld(e):
            for dc in range(NCH):
                if dc >= 2:
                    e.wait_ge(L[dc % 2].h,
                              l0[dc % 2] + 16 * ((dc - 2) // 2 + 1))
                    e.wait_ge(kb.se.h, se0 + dc - 1)
                    e.wait_ge(kb.ve.h, ve0 + 3 * (dc - 2) + 1)
                L[dc % 2].tag(e.dma_start(
                    out=fb[dc % 2].ap(),
                    in_=f_s.ap()[dc * 128:(dc + 1) * 128, :]), 16)
        block.sync(ld)

        def sca(e):
            for dc in range(NCH):
                e.wait_ge(L[dc % 2].h, l0[dc % 2] + 16 * (dc // 2 + 1))
                if dc >= 1:
                    e.wait_ge(S[(dc - 1) % 2].h, st_bnd(dc - 1))
                e.activation(b1[:], fb[dc % 2][:], AF.Ln, bias=0.0, scale=1.0)
                kb.se.tag(e.drain())
                e.wait_ge(kb.ve.h, ve0 + 3 * dc + 2)
                S[dc % 2].tag(e.dma_start(
                    out=b_s.ap()[dc * 128:(dc + 1) * 128, :], in_=b1[:]), 16)
        block.scalar(sca)

        def gpc(e):
            for dc in range(NCH):
                if dc >= 1:
                    e.wait_ge(kb.lds.h, lds0 + 16 * dc)
                e.wait_ge(kb.ve.h, ve0 + 3 * dc + 3)
                kb.lds.tag(e.dma_start(
                    out=c_s.ap()[dc * 128:(dc + 1) * 128, :], in_=ub[:]), 16)
            e.wait_ge(kb.lds.h, kb.lds.v)
        block.gpsimd(gpc)

        def hs(e, src, dst, k, op):
            e.tensor_copy(V(dst)[:, :, 0:k], V(src)[:, :, 0:k])
            e.drain()
            e.tensor_tensor(V(dst)[:, :, k:32], V(src)[:, :, k:32],
                            V(src)[:, :, 0:32 - k], op)
            e.drain()

        def vec(e):
            T2 = T // 2
            S2 = T2 // 32
            NC2 = NC // 2

            def Vh(t, h):
                return t.ap()[:, h * T2:(h + 1) * T2].rearrange(
                    "p (s c) -> p s c", c=32)

            def V2(t):
                return t.ap().rearrange("p (s c) -> p s c", c=32)

            def hsh(e, sv, dv, k, op):
                e.tensor_copy(dv[:, :, 0:k], sv[:, :, 0:k])
                e.drain()
                e.tensor_tensor(dv[:, :, k:32], sv[:, :, k:32],
                                sv[:, :, 0:32 - k], op)
                e.drain()

            for dc in range(NCH):
                e.wait_ge(L[dc % 2].h, l0[dc % 2] + 16 * (dc // 2 + 1))
                if dc >= 1:
                    e.wait_ge(kb.lds.h, lds0 + 16 * dc)
                e.tensor_single_scalar(ub[:], fb[dc % 2][:], 0.5, AL.is_equal)
                e.drain()
                e.tensor_scalar(ub[:], ub[:], 2.0, -1.0, AL.mult, AL.add)
                kb.ve.tag(e.drain())
                e.wait_ge(kb.se.h, se0 + dc + 1)
                if dc >= 1:
                    e.wait_ge(S[(dc - 1) % 2].h, st_bnd(dc - 1))
                for h in range(2):
                    bh = Vh(b1, h)
                    b2v = V2(b2)
                    ping = [bh, b2v]
                    for k in (1, 2, 4, 8, 16):
                        hsh(e, ping[0], ping[1], k, AL.add)
                        ping = [ping[1], ping[0]]
                    e.tensor_copy(ss.ap()[:, 0:S2].unsqueeze(2),
                                  b2v[:, :, 31:32])
                    e.drain()
                    sv = ss.ap()[:, 0:S2].rearrange("p (n q) -> p n q", q=4)
                    s2v = ss2.ap()[:, 0:S2].rearrange("p (n q) -> p n q", q=4)
                    e.tensor_copy(s2v[:, :, 0:1], sv[:, :, 0:1])
                    e.drain()
                    e.tensor_tensor(s2v[:, :, 1:4], sv[:, :, 1:4],
                                    sv[:, :, 0:3], AL.add)
                    e.drain()
                    e.tensor_copy(sv[:, :, 0:2], s2v[:, :, 0:2])
                    e.drain()
                    e.tensor_tensor(sv[:, :, 2:4], s2v[:, :, 2:4],
                                    s2v[:, :, 0:2], AL.add)
                    e.drain()
                    b14 = b1.ap()[:, h * T2:(h + 1) * T2].rearrange(
                        "p (n q c) -> p n q c", q=4, c=32)
                    b24 = b2.ap().rearrange("p (n q c) -> p n q c", q=4, c=32)
                    e.tensor_copy(b14[:, :, 0, :], b24[:, :, 0, :])
                    e.drain()
                    e.tensor_tensor(b14[:, :, 1:4, :], b24[:, :, 1:4, :],
                                    sv[:, :, 0:3].unsqueeze(3).to_broadcast(
                                        [128, NC2, 3, 32]), AL.add)
                    e.drain()
                kb.ve.tag(e.drain())
                ping = [ub, ub2]
                for k in (1, 2, 4, 8, 16):
                    hsh(e, V(ping[0]), V(ping[1]), k, AL.mult)
                    ping = [ping[1], ping[0]]
                e.tensor_copy(ss.ap().unsqueeze(2), V(ub2)[:, :, 31:32])
                e.drain()
                e.tensor_tensor_scan(ss2[:], ss[:], ss[:], 1.0, AL.mult,
                                     AL.bypass)
                e.drain()
                e.memset(cc[:, 0:1], 1.0)
                e.drain()
                e.tensor_copy(cc[:, 1:S32], ss2[:, 0:S32 - 1])
                e.drain()
                e.tensor_tensor(V(ub), V(ub2),
                                cc.ap().unsqueeze(2).to_broadcast(
                                    [128, S32, 32]), AL.mult)
                kb.ve.tag(e.drain())
        block.vector(vec)

        kb.barrier(block)


def gla_phase(kb, block, bufs, q_s, f_s, b_s, c_s, g_s, vtok_s, og_s):
    nc, T = kb.nc, kb.T
    NC = T // 128
    with ExitStack() as _stk:
        def sb(name, shape, dt):
            return _stk.enter_context(nc.sbuf_tensor(_u(name), shape, dt))

        def ps(name, shape, dt):
            return _stk.enter_context(nc.psum_tensor(_u(name), shape, dt))

        Fb = [sb("gl_F0", [128, NCH, 128], F32), sb("gl_F1", [128, NCH, 128], F32)]
        bb = [sb("gl_b0", [128, NCH, 128], F32), sb("gl_b1", [128, NCH, 128], F32)]
        Vb = [sb("gl_V0", [128, NCH, 128], BF16), sb("gl_V1", [128, NCH, 128], BF16)]
        gb = [sb("gl_g0", [128, NCH, 128], F32), sb("gl_g1", [128, NCH, 128], F32)]
        og = [sb("gl_og0", [128, NCH, 128], F32), sb("gl_og1", [128, NCH, 128], F32)]
        cb_ = sb("gl_c", [128, NCH, 128], F32)
        Qb = sb("gl_Q", [128, NCH, 128], BF16)
        EA = sb("gl_EA", [128, NCH, 128], F32)
        d1 = sb("gl_d1", [128, NCH, 128], F32)
        kt = sb("gl_kt", [128, NCH, 128], F32)
        KB = sb("gl_KB", [128, NCH, 128], BF16)
        QP = sb("gl_QP", [128, NCH, 128], BF16)
        QH = sb("gl_QH", [128, NCH, 128], BF16)
        KS = sb("gl_KS", [128, NCH, 128], BF16)
        Ktok = sb("gl_Kt", [128, NCH, 128], BF16)
        SBall = sb("gl_SB", [128, NCH, 128], BF16)
        ptb = sb("gl_pt", [128, NCH, 128], BF16)
        S = sb("gl_S", [128, NCH, 128], F32)
        tri = sb("gl_tri", [128, 32], BF16)
        Acol = sb("gl_Ac", [128, NCH, 4], F32)
        dd = sb("gl_dd", [128, NCH, 3], F32)
        eA4 = sb("gl_eA4", [128, NCH, 4], F32)
        dC = sb("gl_dC", [128, NCH, 1], F32)
        eDD = sb("gl_eDD", [128, NCH, 3], F32)
        eC = sb("gl_eC", [128, NCH, 1], F32)
        dec = [sb("gl_dec0", [128, NCH, 1], F32), sb("gl_dec1", [128, NCH, 1], F32)]
        oP = [ps("gl_oP0", [128, NCH, 128], F32), ps("gl_oP1", [128, NCH, 128], F32)]
        Pp = ps("gl_Pp", [128, NCH, 128], F32)
        SAp = ps("gl_SA", [128, NCH, 128], F32)

        L = [kb.ldh, kb.ldh2]
        ST = [kb.sth, kb.sth2]
        SO = [kb.sts, kb.sts2]
        l0 = [L[0].v, L[1].v]
        st0 = [ST[0].v, ST[1].v]
        so0 = [SO[0].v, SO[1].v]
        ve0, pe0, se0, ge0 = kb.ve.v, kb.pe.v, kb.se.v, kb.ge.v

        veV1 = {n: ve0 + 1 + 5 * n + 1 for n in range(NC)}
        veV2 = {n: ve0 + 1 + 5 * n + 2 for n in range(NC)}
        veEV = {n: ve0 + 1 + 5 * n + 3 for n in range(NC)}
        veSU = {n: ve0 + 1 + 5 * n + 4 for n in range(NC)}
        veOG = {n: ve0 + 1 + 5 * n + 5 for n in range(NC)}
        seS1 = {n: se0 + 2 * n + 1 for n in range(NC)}
        seED = {n: se0 + 2 * n + 2 for n in range(NC)}
        geSB = {n: ge0 + 1 + 7 * n + 1 for n in range(NC)}
        geG1 = {n: ge0 + 1 + 7 * n + 2 for n in range(NC)}
        geKD = {n: ge0 + 1 + 7 * n + 3 for n in range(NC)}
        geR = {(n, i): ge0 + 1 + 7 * n + 3 + i for n in range(NC)
               for i in (1, 2, 3)}
        geKS = {n: ge0 + 1 + 7 * n + 7 for n in range(NC)}
        peS = {}
        pePV = {}
        peSA = {}
        v = pe0
        for n in range(NC):
            for i in range(4):
                v += 1
                peS[(n, i)] = v
            for h in range(8):
                v += 1
                pePV[(n, h)] = v
            v += 1
            peSA[n] = v

        def ld_done(n):
            return l0[n % 2] + 96 * (n // 2 + 1)

        def so_done(n):
            return so0[n % 2] + 16 * (n // 2 + 1)

        def tr_done(n):
            return st0[n % 2] + 128 * (n // 2 + 1)

        def init_v(e):
            e.memset(S.ap(), 0.0)
            e.drain()
            e.memset(ptb.ap(), 0.0)
            kb.ve.tag(e.drain())
        block.vector(init_v)

        def init_g(e):
            for q in range(4):
                e.affine_select(tri[q * 32:(q + 1) * 32, :],
                                bufs["ones1k"][q * 32:(q + 1) * 32, 0:32],
                                [[1, 32]], AL.is_ge, 0.0, base=0,
                                channel_multiplier=-1)
                e.drain()
            kb.ge.tag(e.drain())
        block.gpsimd(init_g)

        def ld(e):
            for n in range(NC):
                par = n % 2
                if n >= 2:
                    e.wait_ge(kb.pe.h, peSA[n - 2])
                    e.wait_ge(kb.ve.h, veOG[n - 2])
                    e.wait_ge(kb.se.h, seS1[n - 2])
                    e.wait_ge(kb.ge.h, geKD[n - 2])
                sl = slice(n * 128, (n + 1) * 128)
                rr = lambda apx: apx.rearrange("(c p) t -> p c t", p=128)
                L[par].tag(e.dma_start(out=Fb[par].ap(),
                                       in_=rr(f_s.ap()[:, sl])), 16)
                L[par].tag(e.dma_start(out=bb[par].ap(),
                                       in_=rr(b_s.ap()[:, sl])), 16)
                L[par].tag(e.dma_start(out=gb[par].ap(),
                                       in_=rr(g_s.ap()[:, sl])), 16)
                vsrc = vtok_s.ap()[sl, :]
                L[par].tag(e.dma_start(
                    out=Vb[par].ap(),
                    in_=vsrc.rearrange("p (c f) -> p c f", f=128)), 16)
                if n >= 1:
                    e.wait_ge(kb.ve.h, veV2[n - 1])
                    e.wait_ge(kb.ge.h, geKD[n - 1])
                L[par].tag(e.dma_start(out=cb_.ap(),
                                       in_=rr(c_s.ap()[:, sl])), 16)
                L[par].tag(e.dma_start(out=Qb.ap(),
                                       in_=rr(q_s.ap()[:, sl])), 16)
                if n >= 1:
                    e.wait_ge(kb.ve.h, veOG[n - 1])
                    dst = og_s.ap()[:, (n - 1) * 128:n * 128]
                    SO[(n - 1) % 2].tag(e.dma_start(
                        out=dst.rearrange("(c p) t -> p c t", p=128),
                        in_=og[(n - 1) % 2].ap()), 16)
            e.wait_ge(kb.ve.h, veOG[NC - 1])
            dst = og_s.ap()[:, (NC - 1) * 128:NC * 128]
            SO[(NC - 1) % 2].tag(e.dma_start(
                out=dst.rearrange("(c p) t -> p c t", p=128),
                in_=og[(NC - 1) % 2].ap()), 16)
        block.sync(ld)

        def sca(e):
            for n in range(NC):
                par = n % 2
                e.wait_ge(kb.ve.h, veV1[n])
                if n >= 1:
                    e.wait_ge(kb.ge.h, geKD[n - 1])
                e.activation(eA4.ap(), Acol.ap(), AF.Exp, bias=0.0, scale=1.0)
                e.drain()
                e.activation(eDD.ap(), dd.ap(), AF.Exp, bias=0.0, scale=1.0)
                e.drain()
                e.activation(eC.ap(), dC.ap(), AF.Exp, bias=0.0, scale=1.0)
                e.drain()
                e.activation(dec[par].ap(), bb[par].ap()[:, :, 127:128],
                             AF.Exp, bias=0.0, scale=1.0)
                kb.se.tag(e.drain())
                assert kb.se.v == seS1[n]
                e.wait_ge(kb.ge.h, geG1[n])
                e.activation(EA.ap(), d1.ap(), AF.Exp, bias=0.0, scale=-1.0)
                e.drain()
                e.activation(d1.ap(), d1.ap(), AF.Exp, bias=0.0, scale=1.0)
                kb.se.tag(e.drain())
                assert kb.se.v == seED[n]
                e.wait_ge(kb.ge.h, geKS[n])
                if n >= 1:
                    e.wait_ge(kb.pe.h, peSA[n - 1])
                for h in range(8):
                    ST[par].tag(e.dma_start(out=Ktok[:, h, :],
                                            in_=KS[:, h, :],
                                            transpose=True), 16)
        block.scalar(sca)

        def vec(e):
            for n in range(NC):
                par = n % 2
                e.wait_ge(L[par].h, ld_done(n))
                if n >= 1:
                    e.wait_ge(kb.se.h, seS1[n - 1])
                    e.wait_ge(kb.ge.h, geG1[n - 1])
                e.memset(Acol[:, :, 0:1], 0.0)
                e.drain()
                e.tensor_copy(
                    Acol.ap()[:, :, 1:4].unsqueeze(3),
                    bb[par].ap().rearrange("p h (q s) -> p h q s",
                                           s=32)[:, :, 0:3, 31:32])
                e.drain()
                e.tensor_sub(dd.ap(), Acol.ap()[:, :, 1:4],
                             Acol.ap()[:, :, 0:3])
                e.drain()
                e.tensor_sub(dC.ap(), bb[par].ap()[:, :, 127:128],
                             Acol.ap()[:, :, 3:4])
                kb.ve.tag(e.drain())
                assert kb.ve.v == veV1[n]
                e.wait_ge(kb.se.h, seED[n])
                if n >= 2:
                    e.wait_ge(SO[par].h, so_done(n - 2))
                tmp = og[par]
                e.tensor_mul(tmp.ap(), cb_.ap(), EA.ap())
                e.drain()
                e.scalar_tensor_tensor(QH.ap(), Qb.ap(), 1.0 / 16.0,
                                       tmp.ap(), AL.mult, AL.mult)
                e.drain()
                e.tensor_mul(
                    tmp.ap().rearrange("p h (q s) -> p h q s", s=32),
                    tmp.ap().rearrange("p h (q s) -> p h q s", s=32),
                    eA4.ap().unsqueeze(3).to_broadcast([128, NCH, 4, 32]))
                e.drain()
                e.scalar_tensor_tensor(QP.ap(), Qb.ap(), 1.0 / 16.0,
                                       tmp.ap(), AL.mult, AL.mult)
                kb.ve.tag(e.drain())
                assert kb.ve.v == veV2[n]
                e.wait_ge(kb.pe.h, peS[(n, 3)])
                for q in range(4):
                    pr = slice(q * 32, (q + 1) * 32)
                    e.tensor_mul(ptb.ap()[pr, :, q * 32:(q + 1) * 32],
                                 Pp.ap()[pr, :, q * 32:(q + 1) * 32],
                                 tri.ap()[pr, :].unsqueeze(1).to_broadcast(
                                     [32, NCH, 32]))
                    e.drain()
                for q in range(3):
                    pr = slice(q * 32, (q + 1) * 32)
                    e.tensor_copy(ptb.ap()[pr, :, (q + 1) * 32:128],
                                  Pp.ap()[pr, :, (q + 1) * 32:128])
                    e.drain()
                kb.ve.tag(e.drain())
                assert kb.ve.v == veEV[n]
                e.wait_ge(kb.pe.h, peSA[n])
                e.tensor_mul(S.ap(), S.ap(),
                             dec[par].ap().to_broadcast([128, NCH, 128]))
                e.drain()
                e.tensor_add(S.ap(), S.ap(), SAp.ap())
                kb.ve.tag(e.drain())
                assert kb.ve.v == veSU[n]
                e.tensor_mul(og[par].ap(), oP[par].ap(), gb[par].ap())
                kb.ve.tag(e.drain())
                assert kb.ve.v == veOG[n]
        block.vector(vec)

        def gp(e):
            for n in range(NC):
                par = n % 2
                e.wait_ge(kb.ve.h, veSU[n - 1] if n >= 1 else ve0 + 1)
                if n >= 1:
                    e.wait_ge(kb.pe.h, pePV[(n - 1, 7)])
                e.tensor_copy(SBall.ap(), S.ap())
                kb.ge.tag(e.drain())
                assert kb.ge.v == geSB[n]
                e.wait_ge(kb.ve.h, veV1[n])
                if n >= 1:
                    e.wait_ge(kb.se.h, seED[n - 1])
                e.tensor_sub(
                    d1.ap().rearrange("p h (q s) -> p h q s", s=32),
                    Acol.ap().unsqueeze(3).to_broadcast([128, NCH, 4, 32]),
                    bb[par].ap().rearrange("p h (q s) -> p h q s", s=32))
                kb.ge.tag(e.drain())
                assert kb.ge.v == geG1[n]
                e.tensor_mul(kt.ap(), Fb[par].ap(), cb_.ap())
                e.drain()
                e.tensor_sub(kt.ap(), cb_.ap(), kt.ap())
                e.drain()
                e.wait_ge(kb.se.h, seED[n])
                if n >= 1:
                    e.wait_ge(kb.pe.h, peS[(n - 1, 3)])
                e.tensor_mul(KB.ap(), kt.ap(), d1.ap())
                kb.ge.tag(e.drain())
                assert kb.ge.v == geKD[n]
                for i in (1, 2, 3):
                    e.wait_ge(kb.pe.h, peS[(n, i - 1)])
                    e.tensor_mul(KB.ap()[:, :, 0:i * 32],
                                 KB.ap()[:, :, 0:i * 32],
                                 eDD.ap()[:, :, i - 1:i].to_broadcast(
                                     [128, NCH, i * 32]))
                    kb.ge.tag(e.drain())
                    assert kb.ge.v == geR[(n, i)]
                e.wait_ge(kb.pe.h, peS[(n, 3)])
                e.tensor_mul(KS.ap(), KB.ap(),
                             eC.ap().to_broadcast([128, NCH, 128]))
                kb.ge.tag(e.drain())
                assert kb.ge.v == geKS[n]
        block.gpsimd(gp)

        def ten(e):
            for n in range(NC):
                par = n % 2
                for i in range(4):
                    if i == 0:
                        e.wait_ge(kb.ve.h, veV2[n])
                        e.wait_ge(kb.ge.h, geKD[n])
                        if n >= 1:
                            e.wait_ge(kb.ve.h, veEV[n - 1])
                    else:
                        e.wait_ge(kb.ge.h, geR[(n, i)])
                    last = None
                    for h in range(8):
                        last = e.matmul(Pp[:, h, i * 32:(i + 1) * 32],
                                        KB[:, h, :],
                                        QH[:, h, i * 32:(i + 1) * 32],
                                        start=True, stop=True)
                    kb.pe.tag(last)
                    assert kb.pe.v == peS[(n, i)]
                e.wait_ge(kb.ve.h, veEV[n])
                e.wait_ge(kb.ge.h, geSB[n])
                if n >= 2:
                    e.wait_ge(kb.ve.h, veOG[n - 2])
                e.wait_ge(L[par].h, ld_done(n))
                for h in range(8):
                    e.matmul(oP[par][:, h, :], SBall[:, h, :], QP[:, h, :],
                             start=True, stop=False)
                    last = e.matmul(oP[par][:, h, :], Vb[par][:, h, :],
                                    ptb[:, h, :], start=False, stop=True)
                    kb.pe.tag(last)
                    assert kb.pe.v == pePV[(n, h)]
                e.wait_ge(ST[par].h, tr_done(n))
                if n >= 1:
                    e.wait_ge(kb.ve.h, veSU[n - 1])
                last = None
                for h in range(8):
                    last = e.matmul(SAp[:, h, :], Ktok[:, h, :],
                                    Vb[par][:, h, :], start=True, stop=True)
                kb.pe.tag(last)
                assert kb.pe.v == peSA[n]
        block.tensor(ten)

        kb.barrier(block)


def gn_phase(kb, block, bufs, src_f32, dst_bf, nw_sb):
    nc, T, NT = kb.nc, kb.T, kb.NT
    ones = bufs["ones128"]
    VCH = [0, 1, 2]
    GCH = [3, 4, 5, 6, 7]
    with (
        nc.sbuf_tensor(_u("gn_y0"), [128, NCH, TT], F32) as y0,
        nc.sbuf_tensor(_u("gn_y1"), [128, NCH, TT], F32) as y1,
        nc.sbuf_tensor(_u("gn_yb"), [128, NCH, TT], BF16) as ybf,
        nc.sbuf_tensor(_u("gn_ys"), [128, NCH, TT], BF16) as ysq,
        nc.sbuf_tensor(_u("gn_mu"), [128, TT], F32) as mu,
        nc.sbuf_tensor(_u("gn_es"), [128, TT], F32) as es,
        nc.sbuf_tensor(_u("gn_rstd"), [128, TT], F32) as rstd,
        nc.sbuf_tensor(_u("gn_ob"), [128, NCH, TT], BF16) as obf,
        nc.psum_tensor(_u("gn_psA"), [128, TT], F32) as psA,
        nc.psum_tensor(_u("gn_psB"), [128, TT], F32) as psB,
    ):
        yb = [y0, y1]
        L = [kb.ldh, kb.ldh2]
        S = [kb.sth, kb.sth2]
        ve0, pe0, se0, ge0 = kb.ve.v, kb.pe.v, kb.se.v, kb.ge.v
        l0 = [L[0].v, L[1].v]
        s0 = [S[0].v, S[1].v]

        def ld(e):
            for i in range(NT):
                if i >= 2:
                    e.wait_ge(kb.ve.h, ve0 + 4 * (i - 1))
                    e.wait_ge(kb.ge.h, ge0 + (i - 1))
                src = src_f32.ap()[:, i * TT:(i + 1) * TT]
                L[i % 2].tag(e.dma_start(
                    out=yb[i % 2].ap(),
                    in_=src.rearrange("(c p) t -> p c t", p=128)), 16)
        block.sync(ld)

        def vec(e):
            for i in range(NT):
                e.wait_ge(L[i % 2].h, l0[i % 2] + 16 * (i // 2 + 1))
                e.tensor_copy(ybf.ap(), yb[i % 2].ap())
                e.drain()
                e.tensor_tensor(ysq.ap(), ybf.ap(), ybf.ap(), AL.mult)
                kb.ve.tag(e.drain())
                e.wait_ge(kb.pe.h, pe0 + i + 1)
                if i > 0:
                    e.wait_ge(kb.ge.h, ge0 + i)
                e.tensor_scalar(mu[:], psA[:], 1.0 / D, None, AL.mult)
                e.drain()
                e.tensor_scalar(es[:], psB[:], 1.0 / D, None, AL.mult)
                e.drain()
                e.tensor_tensor(rstd[:], mu[:], mu[:], AL.mult)
                e.drain()
                e.tensor_sub(es[:], es[:], rstd[:])
                kb.ve.tag(e.drain())
                e.wait_ge(kb.se.h, se0 + i + 1)
                e.reciprocal(rstd[:], es[:])
                kb.ve.tag(e.drain())
                if i > 0:
                    j = (i - 1) % 2
                    e.wait_ge(S[j].h, s0[j] + 16 * ((i - 1) // 2 + 1))
                for dc in VCH:
                    y_ = yb[i % 2]
                    e.tensor_sub(y_[:, dc, :], y_[:, dc, :], mu[:])
                    e.drain()
                    e.tensor_mul(y_[:, dc, :], y_[:, dc, :], rstd[:])
                    e.drain()
                    e.tensor_mul(obf[:, dc, :], y_[:, dc, :],
                                 nw_sb[:, dc:dc + 1].to_broadcast([128, TT]))
                    e.drain()
                kb.ve.tag(e.drain())
        block.vector(vec)

        def sca(e):
            for i in range(NT):
                e.wait_ge(kb.ve.h, ve0 + 4 * i + 2)
                e.activation(es[:], es[:], AF.Sqrt, bias=bufs["ceps"][:],
                             scale=1.0)
                kb.se.tag(e.drain())
                e.wait_ge(kb.ve.h, ve0 + 4 * i + 4)
                e.wait_ge(kb.ge.h, ge0 + i + 1)
                dst = dst_bf.ap()[:, i * TT:(i + 1) * TT]
                S[i % 2].tag(e.dma_start(
                    out=dst.rearrange("(c p) t -> p c t", p=128),
                    in_=obf.ap()), 16)
        block.scalar(sca)

        def ten(e):
            for i in range(NT):
                e.wait_ge(kb.ve.h, ve0 + 4 * i + 1)
                for dc in range(NCH):
                    e.matmul(psA[:], ones[:, 0:128], ybf[:, dc, :],
                             start=(dc == 0), stop=(dc == NCH - 1))
                last = None
                for dc in range(NCH):
                    last = e.matmul(psB[:], ones[:, 0:128], ysq[:, dc, :],
                                    start=(dc == 0), stop=(dc == NCH - 1))
                kb.pe.tag(last)
        block.tensor(ten)

        def gp(e):
            for i in range(NT):
                e.wait_ge(kb.ve.h, ve0 + 4 * i + 3)
                for dc in GCH:
                    y_ = yb[i % 2]
                    e.tensor_sub(y_[:, dc, :], y_[:, dc, :], mu[:])
                    e.drain()
                    e.tensor_mul(y_[:, dc, :], y_[:, dc, :], rstd[:])
                    e.drain()
                    e.tensor_mul(obf[:, dc, :], y_[:, dc, :],
                                 nw_sb[:, dc:dc + 1].to_broadcast([128, TT]))
                    e.drain()
                kb.ge.tag(e.drain())
        block.gpsimd(gp)

        kb.barrier(block)


def quant_weights_phase(kb, block, bufs, w1, w2, w1qT_s, w2qT_s, sw1, sw2,
                        cc_in, cc_out, gcol):
    nc = kb.nc
    jobs = ([("w1", r) for r in range(FF1 // 128)] +
            [("w2", r) for r in range(D // 128)])
    units = []
    for j, (w, r) in enumerate(jobs):
        for sub in range(1 if w == "w1" else 4):
            units.append((j, w, r, sub))

    def units_upto(j):
        if j < 32:
            return j + 1
        return 32 + 4 * (j - 31)

    with (
        nc.sbuf_tensor(_u("qw_a0"), [128, FF1], F32) as a0,
        nc.sbuf_tensor(_u("qw_a1"), [128, FF1], F32) as a1,
        nc.sbuf_tensor(_u("qw_q0"), [128, FF1], BF16) as q0,
        nc.sbuf_tensor(_u("qw_q1"), [128, FF1], BF16) as q1,
        nc.sbuf_tensor(_u("qw_tq"), [128, 2, 8, 128], BF16) as tq,
        nc.sbuf_tensor(_u("qw_rc"), [128, 1], F32) as rcol,
        nc.sbuf_tensor(_u("qw_g1"), [1, 1], F32) as gm1,
    ):
        ab = [a0, a1]
        qb = [q0, q1]
        L = [kb.ldh, kb.ldh2]
        S = [kb.sth, kb.sth2]
        TS = [kb.sts, kb.sts2]
        l0 = [L[0].v, L[1].v]
        s0 = [S[0].v, S[1].v]
        t0 = [TS[0].v, TS[1].v]
        ve0, ge0, cc0 = kb.ve.v, kb.ge.v, kb.cc.v

        def srcof(w):
            return (w1 if w == "w1" else w2).ap()

        def emit_unit(e, u):
            j, w, r, sub = units[u]
            if sub == 0:
                e.wait_ge(kb.ve.h, ve0 + 2 * j + 2)
            if u >= 2:
                e.wait_ge(S[u % 2].h, s0[u % 2] + 16 * ((u - 2) // 2 + 1))
            for c in range(8):
                col = (8 * sub + c) * 128
                TS[u % 2].tag(e.dma_start(
                    out=tq[:, u % 2, c, :],
                    in_=qb[j % 2][:, col:col + 128], transpose=True), 16)

        def ld(e):
            eu = 0
            for j, (w, r) in enumerate(jobs):
                if j >= 2:
                    while eu < units_upto(j - 2):
                        emit_unit(e, eu)
                        eu += 1
                    e.wait_ge(kb.ve.h, ve0 + (j - 2) * 2 + 2)
                src = srcof(w)
                C = src.shape[1]
                L[j % 2].tag(e.dma_start(out=ab[j % 2][:, 0:C],
                                         in_=src[r * 128:(r + 1) * 128, :]),
                             16)
            while eu < len(units):
                emit_unit(e, eu)
                eu += 1
        block.sync(ld)

        def st(e):
            for u, (j, w, r, sub) in enumerate(units):
                e.wait_ge(TS[u % 2].h, t0[u % 2] + 128 * (u // 2 + 1))
                if w == "w1":
                    dst = w1qT_s.ap()[:, r * 128:(r + 1) * 128]
                else:
                    dst = w2qT_s.ap()[sub * 1024:(sub + 1) * 1024,
                                      r * 128:(r + 1) * 128]
                S[u % 2].tag(e.dma_start(
                    out=dst.rearrange("(c p) f -> p c f", p=128),
                    in_=tq[:, u % 2, :, :]), 16)
        block.scalar(st)

        def vec(e):
            for j, (w, r) in enumerate(jobs):
                C = srcof(w).shape[1]
                sw = sw1 if w == "w1" else sw2
                e.wait_ge(L[j % 2].h, l0[j % 2] + 16 * (j // 2 + 1))
                a = ab[j % 2]
                e.reduce_max(rcol[:], a[:, 0:C], axis=AX.X,
                             apply_absolute_value=True)
                e.drain()
                e.tensor_scalar(rcol[:], rcol[:], EPS_Q, 1.0 / 127.0, AL.max,
                                AL.mult)
                e.drain()
                e.tensor_copy(sw[:, r:r + 1], rcol[:])
                e.drain()
                e.reciprocal(rcol[:], rcol[:])
                e.drain()
                e.tensor_scalar(a[:, 0:C], a[:, 0:C], rcol[:], None, AL.mult)
                kb.ve.tag(e.drain())
                e.tensor_scalar(a[:, 0:C], a[:, 0:C], RMAGIC, -RMAGIC,
                                AL.add, AL.add)
                e.drain()
                e.tensor_scalar(a[:, 0:C], a[:, 0:C], -128.0, 127.0, AL.max,
                                AL.min)
                e.drain()
                if j >= 2:
                    U = units_upto(j - 2)
                    e.wait_ge(TS[0].h, t0[0] + 128 * ((U + 1) // 2))
                    e.wait_ge(TS[1].h, t0[1] + 128 * (U // 2))
                e.tensor_copy(qb[j % 2][:, 0:C], a[:, 0:C])
                kb.ve.tag(e.drain())
        block.vector(vec)

        def gp(e):
            kb.lds.tag(e.dma_start(out=cc_in[:, :], in_=bufs["mx1"][:, :]), 16)
            e.wait_ge(kb.lds.h, kb.lds.v)
            kb.cc.tag(e.collective_compute(
                "AllReduce", AL.max,
                replica_groups=[list(range(kb.ncores))],
                ins=[cc_in.ap().opt()],
                outs=[cc_out.ap().opt()]))
            e.wait_ge(kb.cc.h, cc0 + 1)
            kb.lds.tag(e.dma_start(out=gcol.ap(),
                                   in_=cc_out.ap().to_broadcast([128, 1])), 16)
            e.wait_ge(kb.lds.h, kb.lds.v)
            e.tensor_scalar(gcol.ap(), gcol.ap(), EPS_Q, None, AL.max)
            kb.ge.tag(e.drain())
        block.gpsimd(gp)

        kb.barrier(block)


def mm1_phase(kb, block, bufs, xn32_s, w1qT_s, y_s, sw1, b1col, gcol,
              cc_in2, cc_out2, g2col):
    nc, T = kb.nc, kb.T
    TH = T // 2
    SUB = min(256, TH)
    NSUB = TH // SUB
    FC = FF1 // 128
    NGH = FC * NSUB
    with ExitStack() as _stk:
        def sb(name, shape, dt):
            return _stk.enter_context(nc.sbuf_tensor(_u(name), shape, dt))
        xq = sb("m1_xq", [128, NCH, TH], BF16)
        xl = [sb("m1_x0", [128, NCH, SUB], F32),
              sb("m1_x1", [128, NCH, SUB], F32)]
        mb_ = sb("m1_m", [128, NCH, SUB], F32)
        wt = [sb("m1_w0", [128, NCH, 128], BF16),
              sb("m1_w1", [128, NCH, 128], BF16)]
        yb = sb("m1_y", [128, 4, SUB], F32)
        esc = sb("m1_es", [128, FC], F32)
        ebi = sb("m1_eb", [128, FC], F32)
        rin = sb("m1_ri", [128, 1], F32)
        hmx = sb("m1_hm", [128, 1], F32)
        hmt = sb("m1_ht", [128, 1], F32)
        hm1 = sb("m1_h1", [1, 1], F32)
        tcol = sb("m1_tc", [128, FC], F32)
        pY = [_stk.enter_context(nc.psum_tensor(_u("m1_p%d" % i),
                                                [128, SUB], F32))
              for i in range(4)]
        L = [kb.ldh, kb.ldh2]
        S = [kb.sth, kb.sth2]
        SO = [kb.sts, kb.sts2]
        l0 = [L[0].v, L[1].v]
        s0 = [S[0].v, S[1].v]
        so0 = [SO[0].v, SO[1].v]
        ve0, pe0, se0, ge0, cc0 = kb.ve.v, kb.pe.v, kb.se.v, kb.ge.v, kb.cc.v

        vq = ve0 + 2
        gq = ge0 + 1

        def veQmid(j):
            half, tq = divmod(j, NSUB)
            return vq + half * (NSUB + NGH) + tq + 1

        def veMax(g):
            half, gg = divmod(g, NGH)
            return vq + half * (NSUB + NGH) + NSUB + gg + 1

        ve_end = vq + 2 * (NSUB + NGH)


        def prol_v(e):
            e.reciprocal(rin[:], gcol[:])
            e.drain()
            e.tensor_scalar(rin[:], rin[:], 255.0, None, AL.mult)
            e.drain()
            e.tensor_scalar(esc.ap(), sw1.ap(), gcol[:], None, AL.mult)
            e.drain()
            e.tensor_scalar(esc.ap(), esc.ap(), 1.0 / 255.0, None, AL.mult)
            e.drain()
            e.reciprocal(tcol.ap(), esc.ap())
            e.drain()
            e.tensor_tensor(tcol.ap(), b1col.ap(), tcol.ap(), AL.mult)
            e.drain()
            e.tensor_scalar(tcol.ap(), tcol.ap(), RMAGIC, -RMAGIC,
                            AL.add, AL.add)
            kb.ve.tag(e.drain())
            e.tensor_tensor(ebi.ap(), tcol.ap(), esc.ap(), AL.mult)
            kb.ve.tag(e.drain())
        block.vector(prol_v)

        def prol_g(e):
            kb.ge.tag(e.engine_nop())
        block.gpsimd(prol_g)

        def emit_store(e, g):
            hh, gg = divmod(g, NGH)
            fcp, tqp = divmod(gg, NSUB)
            e.wait_ge(kb.ve.h, veMax(g))
            if g >= 2:
                e.wait_ge(SO[g % 2].h, so0[g % 2] + 16 * ((g - 2) // 2 + 1))
            t_lo = hh * TH + tqp * SUB
            dst = y_s.ap()[fcp * 128:(fcp + 1) * 128, t_lo:t_lo + SUB]
            SO[g % 2].tag(e.dma_start(out=dst, in_=yb[:, g % 4, :]), 16)

        def ld(e):
            for half in range(2):
                for tq in range(NSUB):
                    j = half * NSUB + tq
                    if j >= 2:
                        e.wait_ge(kb.ve.h, veQmid(j - 2))
                    src = xn32_s.ap()[:, half * TH + tq * SUB:
                                      half * TH + (tq + 1) * SUB]
                    L[j % 2].tag(e.dma_start(
                        out=xl[j % 2].ap(),
                        in_=src.rearrange("(c p) t -> p c t", p=128)), 16)
                for fc in range(FC):
                    gw = half * FC + fc
                    if gw >= 2:
                        e.wait_ge(kb.pe.h, pe0 + (gw - 1) * NSUB)
                    S[gw % 2].tag(e.dma_start(
                        out=wt[gw % 2].ap(),
                        in_=w1qT_s.ap()[:, fc * 128:(fc + 1) * 128].rearrange(
                            "(c p) f -> p c f", p=128)), 16)
                    if fc >= 1:
                        for g in range(half * NGH + (fc - 1) * NSUB,
                                       half * NGH + fc * NSUB):
                            emit_store(e, g)
                for g in range(half * NGH + (FC - 1) * NSUB,
                               half * NGH + FC * NSUB):
                    emit_store(e, g)
        block.sync(ld)

        def quant_ops(e, j):
            a = xl[j % 2]
            half, tq = divmod(j, NSUB)
            e.tensor_scalar(a.ap(), a.ap(), rin[:], None, AL.mult)
            e.drain()
            e.tensor_scalar(a.ap(), a.ap(), RMAGIC, -RMAGIC, AL.add, AL.add)
            e.drain()
            e.tensor_scalar(a.ap(), a.ap(), 0.0, 255.0, AL.max, AL.min)
            e.drain()
            e.tensor_copy(xq[:, :, tq * SUB:(tq + 1) * SUB], a.ap())
            e.drain()

        def vecq(e):
            for half in range(2):
                for tq in range(NSUB):
                    j = half * NSUB + tq
                    e.wait_ge(L[j % 2].h, l0[j % 2] + 16 * (j // 2 + 1))
                    if tq == 0 and half == 1:
                        e.wait_ge(kb.pe.h, pe0 + NGH)
                    quant_ops(e, j)
                    kb.ve.tag(e.drain())
                    assert kb.ve.v == veQmid(j)
                for gg in range(NGH):
                    g = half * NGH + gg
                    e.wait_ge(kb.se.h, se0 + g + 1)
                    e.reduce_max(hmt[:], yb[:, g % 4, :], axis=AX.X)
                    e.drain()
                    if g == 0:
                        e.tensor_copy(hmx[:], hmt[:])
                    else:
                        e.tensor_max(hmx[:], hmx[:], hmt[:])
                    kb.ve.tag(e.drain())
                    assert kb.ve.v == veMax(g)
        block.vector(vecq)

        def gpq(e):
            e.wait_ge(kb.ve.h, ve_end)
            e.tensor_reduce(hm1.ap(), hmx.ap(), AX.C, AL.max)
            e.drain()
            kb.lds.tag(e.dma_start(out=cc_in2[:, :], in_=hm1[:, :]), 16)
            e.wait_ge(kb.lds.h, kb.lds.v)
            kb.cc.tag(e.collective_compute(
                "AllReduce", AL.max,
                replica_groups=[list(range(kb.ncores))],
                ins=[cc_in2.ap().opt()],
                outs=[cc_out2.ap().opt()]))
            e.wait_ge(kb.cc.h, cc0 + 1)
            kb.lds.tag(e.dma_start(out=g2col.ap(),
                                   in_=cc_out2.ap().to_broadcast([128, 1])),
                       16)
            e.wait_ge(kb.lds.h, kb.lds.v)
            e.tensor_scalar(g2col.ap(), g2col.ap(), EPS_Q, None, AL.max)
            kb.ge.tag(e.drain())
        block.gpsimd(gpq)

        def wst_done(gw):
            return s0[gw % 2] + 16 * (gw // 2 + 1)

        def sca(e):
            e.wait_ge(kb.ve.h, vq)
            g = 0
            for half in range(2):
                for fc in range(FC):
                    for tq in range(NSUB):
                        e.wait_ge(kb.pe.h, pe0 + g + 1)
                        if g >= 4:
                            jj = (g - 4) % 2
                            e.wait_ge(SO[jj].h,
                                      so0[jj] + 16 * ((g - 4) // 2 + 1))
                        e.activation(yb[:, g % 4, :], pY[g % 4][:],
                                     AF.Identity, bias=ebi[:, fc:fc + 1],
                                     scale=esc[:, fc:fc + 1])
                        kb.se.tag(e.drain())
                        g += 1
        block.scalar(sca)

        def ten(e):
            g = 0
            for half in range(2):
                for fc in range(FC):
                    gw = half * FC + fc
                    e.wait_ge(S[gw % 2].h, wst_done(gw))
                    for tq in range(NSUB):
                        j = half * NSUB + tq
                        e.wait_ge(kb.ve.h, veQmid(j))
                        if g >= 4:
                            e.wait_ge(kb.se.h, se0 + g - 3)
                        last = None
                        for dc in range(NCH):
                            last = e.matmul(
                                pY[g % 4][:], wt[gw % 2][:, dc, :],
                                xq[:, dc, tq * SUB:(tq + 1) * SUB],
                                start=(dc == 0), stop=(dc == NCH - 1))
                        kb.pe.tag(last)
                        g += 1
        block.tensor(ten)

        kb.barrier(block)


def transpose_dram_phase(kb, block, bufs, src, dst):
    nc = kb.nc
    R, C = src.ap().shape
    jobs = [(r, c) for c in range(C // 128) for r in range(R // 128)]
    assert len(jobs) % 2 == 0
    with (
        nc.sbuf_tensor(_u("tw_bb"), [128, 8, 128], BF16) as bb,
    ):
        L = [kb.ldh, kb.ldh2]
        S = [kb.sth, kb.sth2]
        l0 = [L[0].v, L[1].v]
        s0 = [S[0].v, S[1].v]

        def ld(e):
            for j, (r, c) in enumerate(jobs):
                if j >= 2:
                    e.wait_ge(L[j % 2].h,
                              l0[j % 2] + 16 * ((j - 2) // 2 + 1))
                if j >= 8:
                    p = (j - 8) // 2
                    e.wait_ge(S[p % 2].h, s0[p % 2] + 16 * (p // 2 + 1))
                L[j % 2].tag(e.dma_start(
                    out=bb[:, j % 8, :],
                    in_=src.ap()[r * 128:(r + 1) * 128, c * 128:(c + 1) * 128],
                    transpose=True), 16)
        block.sync(ld)

        def st(e):
            for p in range(len(jobs) // 2):
                r, c = jobs[2 * p]
                if p >= 2:
                    e.wait_ge(S[p % 2].h,
                              s0[p % 2] + 16 * ((p - 2) // 2 + 1))
                e.wait_ge(L[0].h, l0[0] + 16 * (p + 1))
                e.wait_ge(L[1].h, l0[1] + 16 * (p + 1))
                sl = (2 * p) % 8
                S[p % 2].tag(e.dma_start(
                    out=dst.ap()[c * 128:(c + 1) * 128,
                                 r * 128:(r + 2) * 128],
                    in_=bb[:, sl:sl + 2, :]), 16)
        block.scalar(st)

        kb.barrier(block)


def hq_quant_phase(kb, block, bufs, y_s, hq_s, sw2, b2col, g2col, esc2, ebi2):
    nc, T = kb.nc, kb.T
    HW = T // 2
    jobs = [(r, hf) for r in range(FF1 // 128) for hf in range(2)]
    NJ = len(jobs)
    with (
        nc.sbuf_tensor(_u("hq_a0"), [128, HW], F32) as a0,
        nc.sbuf_tensor(_u("hq_a1"), [128, HW], F32) as a1,
        nc.sbuf_tensor(_u("hq_q0"), [128, HW], BF16) as q0,
        nc.sbuf_tensor(_u("hq_q1"), [128, HW], BF16) as q1,
        nc.sbuf_tensor(_u("hq_r2"), [128, 1], F32) as r2,
        nc.sbuf_tensor(_u("hq_tc"), [128, NCH], F32) as tcol,
    ):
        ab = [a0, a1]
        qb = [q0, q1]
        L = [kb.ldh, kb.ldh2]
        S = [kb.sth, kb.sth2]
        l0 = [L[0].v, L[1].v]
        s0 = [S[0].v, S[1].v]
        ve0, ge0 = kb.ve.v, kb.ge.v

        def prol_v(e):
            e.reciprocal(r2[:], g2col[:])
            e.drain()
            e.tensor_scalar(r2[:], r2[:], 255.0, None, AL.mult)
            e.drain()
            e.tensor_scalar(esc2.ap(), sw2.ap(), g2col[:], None, AL.mult)
            e.drain()
            e.tensor_scalar(esc2.ap(), esc2.ap(), 1.0 / 255.0, None, AL.mult)
            e.drain()
            e.reciprocal(tcol.ap(), esc2.ap())
            e.drain()
            e.tensor_tensor(tcol.ap(), b2col.ap(), tcol.ap(), AL.mult)
            e.drain()
            e.tensor_scalar(tcol.ap(), tcol.ap(), RMAGIC, -RMAGIC,
                            AL.add, AL.add)
            kb.ve.tag(e.drain())
            e.tensor_tensor(ebi2.ap(), tcol.ap(), esc2.ap(), AL.mult)
            kb.ve.tag(e.drain())
        block.vector(prol_v)

        def prol_g(e):
            kb.ge.tag(e.engine_nop())
        block.gpsimd(prol_g)

        vq = ve0 + 2
        gq = ge0 + 1

        def jdone(j):
            return kb.ve.h, vq + j + 1

        def ld(e):
            for j, (r, hf) in enumerate(jobs):
                if j >= 2:
                    hh, vv = jdone(j - 2)
                    e.wait_ge(hh, vv)
                L[j % 2].tag(e.dma_start(
                    out=ab[j % 2][:, :],
                    in_=y_s.ap()[r * 128:(r + 1) * 128,
                                 hf * HW:(hf + 1) * HW]), 16)
        block.sync(ld)

        def quant_slab(e, j, tag):
            a = ab[j % 2]
            e.wait_ge(L[j % 2].h, l0[j % 2] + 16 * (j // 2 + 1))
            e.tensor_scalar(a[:], a[:], r2[:], None, AL.mult)
            e.drain()
            e.tensor_scalar(a[:], a[:], RMAGIC, -RMAGIC, AL.add, AL.add)
            e.drain()
            e.tensor_scalar(a[:], a[:], 0.0, 255.0, AL.max, AL.min)
            e.drain()
            if j >= 2:
                jj = (j - 2) % 2
                e.wait_ge(S[jj].h, s0[jj] + 16 * ((j - 2) // 2 + 1))
            e.tensor_copy(qb[j % 2][:], a[:])
            e.drain()
            tag()

        def vecq(e):
            for j in range(NJ):
                quant_slab(e, j, lambda: kb.ve.tag(e.drain()))
        block.vector(vecq)

        def st(e):
            for j, (r, hf) in enumerate(jobs):
                hh, vv = jdone(j)
                e.wait_ge(hh, vv)
                S[j % 2].tag(e.dma_start(
                    out=hq_s.ap()[r * 128:(r + 1) * 128,
                                  hf * HW:(hf + 1) * HW],
                    in_=qb[j % 2][:, :]), 16)
        block.scalar(st)

        kb.barrier(block)



def build_nc(T=4096, ncores=8, upto=99):
    nc = bass.Bass("TRN2", target_bir_lowering=False, debug=False)
    kb = Kb(nc, T, ncores)

    xT = nc.dram_tensor("xT", [D, T], F32, kind="ExternalInput")
    l0_wqT = nc.dram_tensor("l0_wqT", [D, D], F32, kind="ExternalInput")
    l0_wkT = nc.dram_tensor("l0_wkT", [D, D], F32, kind="ExternalInput")
    l0_wvT = nc.dram_tensor("l0_wvT", [D, D], F32, kind="ExternalInput")
    l0_woT = nc.dram_tensor("l0_woT", [D, D], F32, kind="ExternalInput")
    outT = nc.dram_tensor("outT", [D, T], F32, kind="ExternalOutput")

    l0_wgT = nc.dram_tensor("l0_wgT", [D, FF0], I32, kind="ExternalInput")
    l0_wuT = nc.dram_tensor("l0_wuT", [D, FF0], I32, kind="ExternalInput")
    l0_wdT = nc.dram_tensor("l0_wdT", [FF0, D], I32, kind="ExternalInput")
    l1_wq = nc.dram_tensor("l1_wq", [D, D], F32, kind="ExternalInput")
    l1_wk = nc.dram_tensor("l1_wk", [D, D], F32, kind="ExternalInput")
    l1_wv = nc.dram_tensor("l1_wv", [D, D], F32, kind="ExternalInput")
    l1_wo = nc.dram_tensor("l1_wo", [D, D], F32, kind="ExternalInput")
    l1_wg1 = nc.dram_tensor("l1_wg1", [D, D1], F32, kind="ExternalInput")
    l1_wg2 = nc.dram_tensor("l1_wg2", [D1, D], F32, kind="ExternalInput")
    l1_nw = nc.dram_tensor("l1_nw", [128, NCH], F32, kind="ExternalInput")
    l1_w1 = nc.dram_tensor("l1_w1", [FF1, D], F32, kind="ExternalInput")
    l1_b1 = nc.dram_tensor("l1_b1", [128, FF1 // 128], F32, kind="ExternalInput")
    l1_w2 = nc.dram_tensor("l1_w2", [D, FF1], F32, kind="ExternalInput")
    l1_b2 = nc.dram_tensor("l1_b2", [128, NCH], F32, kind="ExternalInput")

    xn_s = nc.dram_tensor("xn_s", [D, T], BF16)
    q_s = nc.dram_tensor("q_s", [D, T], BF16)
    k_s = nc.dram_tensor("k_s", [D, T], BF16)
    v_s = nc.dram_tensor("v_s", [T, 2 * D], BF16)
    o_s = nc.dram_tensor("o_s", [D, T], BF16)
    wg_b = nc.dram_tensor("wg_b", [D, FF0], BF16)
    wu_b = nc.dram_tensor("wu_b", [D, FF0], BF16)
    wd_b = nc.dram_tensor("wd_b", [FF0, D], BF16)
    h_s = nc.dram_tensor("h_s", [FF0, T], BF16)
    f_s = nc.dram_tensor("f_s", [D, T], F32)
    g_s = nc.dram_tensor("g_s", [D, T], F32)
    vtok_s = nc.dram_tensor("vtok_s", [T, D], BF16)
    b_s = nc.dram_tensor("b_s", [D, T], F32)
    c_s = nc.dram_tensor("c_s", [D, T], F32)
    og_s = nc.dram_tensor("og_s", [D, T], F32)
    xn32_s = nc.dram_tensor("xn32_s", [D, T], F32)
    y_s = nc.dram_tensor("y_s", [FF1, T], F32)
    hq_s = nc.dram_tensor("hq_s", [FF1, T], BF16)
    w1qT_s = nc.dram_tensor("w1qT_s", [D, FF1], BF16)
    w2qT_s = nc.dram_tensor("w2qT_s", [FF1, D], BF16)
    cc_in = nc.dram_tensor("cc_in", [1, 1], F32)
    cc_out = nc.dram_tensor("cc_out", [1, 1], F32)
    cc_in2 = nc.dram_tensor("cc_in2", [1, 1], F32)
    cc_out2 = nc.dram_tensor("cc_out2", [1, 1], F32)

    with (
        nc.sbuf_tensor("X", [128, NCH, T], F32) as X,
        nc.sbuf_tensor("ones1k", [128, 1024], BF16) as ones1k,
        nc.sbuf_tensor("ident128", [128, 128], BF16) as ident,
        nc.sbuf_tensor("mx1", [1, 1], F32) as mx1,
        nc.sbuf_tensor("ceps", [128, 1], F32) as ceps,
        nc.sbuf_tensor("nw_sb", [128, NCH], F32) as nw_sb,
        nc.sbuf_tensor("sw1", [128, FF1 // 128], F32) as sw1,
        nc.sbuf_tensor("sw2", [128, NCH], F32) as sw2,
        nc.sbuf_tensor("gcol", [128, 1], F32) as gcol,
        nc.sbuf_tensor("g2col", [128, 1], F32) as g2col,
        nc.sbuf_tensor("b1col", [128, FF1 // 128], F32) as b1col,
        nc.sbuf_tensor("b2col", [128, NCH], F32) as b2col,
        nc.sbuf_tensor("esc2", [128, NCH], F32) as esc2,
        nc.sbuf_tensor("ebi2", [128, NCH], F32) as ebi2,
        nc.Block() as block,
    ):
        ones = ones1k.ap()[:, 0:128]
        bufs = dict(X=X, ones128=ones, ident128=ident, mx1=mx1, ceps=ceps,
                    ones1k=ones1k)

        def ld(e):
            kb.ldh.tag(e.dma_start(
                out=X.ap(),
                in_=xT.ap().rearrange("(c p) t -> p c t", p=128)), 16)
            kb.ldh.tag(e.dma_start(out=nw_sb.ap(), in_=l1_nw.ap()), 16)
            kb.ldh.tag(e.dma_start(out=b1col.ap(), in_=l1_b1.ap()), 16)
            kb.ldh.tag(e.dma_start(out=b2col.ap(), in_=l1_b2.ap()), 16)
        block.sync(ld)

        def init_v(e):
            e.memset(ones1k[:], 1.0)
            e.memset(ceps[:], EPS_LN)
            e.wait_ge(kb.ldh.h, kb.ldh.v)
            kb.ve.tag(e.drain())
        block.vector(init_v)

        def init_g(e):
            e.wait_ge(kb.ve.h, kb.ve.v)
            e.affine_select(ident[:], ones[:, 0:128], [[1, 128]], AL.is_equal,
                            0.0, base=0, channel_multiplier=-1)
            kb.ge.tag(e.drain())
        block.gpsimd(init_g)

        def init_vpad(e):
            e.wait_ge(kb.ve.h, kb.ve.v)
            for i in range(T // 128):
                dst = v_s.ap()[i * 128:(i + 1) * 128, :]
                dst = dst.rearrange("p (h c) -> p h c", c=128)[:, :, 64:128]
                kb.sth.tag(e.dma_start(
                    out=dst,
                    in_=ones1k.ap().rearrange("p (h c) -> p h c", c=64)), 16)
        block.scalar(init_vpad)

        kb.barrier(block)

        ln_pre_phase(kb, block, bufs, dst_bf=xn_s)
        if upto >= 2:
            simple_proj_phase(kb, block, bufs, None, xn_s, None, None,
                              None, None, jobs=[
                                  (l0_wqT, q_s, AF.Copy, 1.0, BF16),
                                  (l0_wkT, k_s, AF.Copy, 1.0, BF16)])
            v_proj_phase(kb, block, bufs, l0_wvT, xn_s, v_s, padded=True)
        if upto >= 3:
            attn0_phase(kb, block, bufs, q_s, k_s, v_s, o_s)
        if upto >= 4:
            proj_postln_phase(kb, block, bufs, o_s, NCH, w_dram=l0_woT)
        if upto >= 5:
            cast_weights_phase(kb, block, bufs, [(l0_wgT, wg_b), (l0_wuT, wu_b),
                                                 (l0_wdT, wd_b)])
            ln_pre_phase(kb, block, bufs, dst_bf=xn_s)
            swiglu_phase(kb, block, bufs, wg_b, wu_b, xn_s, h_s)
            proj_postln_phase(kb, block, bufs, h_s, FF0 // 128,
                              w_bf_dram=wd_b, evac_scale=INV127, tt=256)
        if upto >= 5.1:
            ln_pre_phase(kb, block, bufs, dst_bf=xn_s)
            simple_proj_phase(kb, block, bufs, None, xn_s, None, None,
                              None, None, jobs=[
                                  (l1_wq, q_s, AF.Silu, 1.0, BF16),
                                  (l1_wk, f_s, AF.Sigmoid, -1.0, F32)])
        if upto >= 5.2:
            pass
        if upto >= 5.3:
            v_proj_phase(kb, block, bufs, l1_wv, xn_s, vtok_s, padded=False)
        if upto >= 5.4:
            gate_phase(kb, block, bufs, l1_wg1, l1_wg2, xn_s, g_s)
        if upto >= 5.5:
            decay_prep_phase(kb, block, bufs, f_s, b_s, c_s)
        if upto >= 5.6:
            gla_phase(kb, block, bufs, q_s, f_s, b_s, c_s, g_s, vtok_s, og_s)
        if upto >= 5.7:
            gn_phase(kb, block, bufs, og_s, o_s, nw_sb)
        if upto >= 6:
            proj_postln_phase(kb, block, bufs, o_s, NCH, w_dram=l1_wo)
        if upto >= 6.1:
            ln_pre_phase(kb, block, bufs, dst_f32=xn32_s, want_max=True)
        if upto >= 6.2:
            quant_weights_phase(kb, block, bufs, l1_w1, l1_w2, w1qT_s,
                                w2qT_s, sw1, sw2, cc_in, cc_out, gcol)
        if upto >= 6.3:
            pass
        if upto >= 6.4:
            mm1_phase(kb, block, bufs, xn32_s, w1qT_s, y_s, sw1, b1col, gcol,
                      cc_in2, cc_out2, g2col)
        if upto >= 6.5:
            hq_quant_phase(kb, block, bufs, y_s, hq_s, sw2, b2col, g2col,
                           esc2, ebi2)
        if upto >= 7:
            proj_postln_phase(kb, block, bufs, hq_s, FF1 // 128,
                              w_bf_dram=w2qT_s, evac_scale_col=esc2,
                              evac_bias_col=ebi2, tt=256)

        def st(e):
            last = e.dma_start(out=outT.ap().rearrange("(c p) t -> p c t", p=128),
                               in_=X.ap())
            kb.ldh.tag(last, 16)
            e.wait_ge(kb.ldh.h, kb.ldh.v)
            e.wait_ge(kb.sth.h, kb.sth.v)
            e.wait_ge(kb.lds.h, kb.lds.v)
        block.sync(st)

    return nc


_NC_CACHE = {}


def kernel(**inputs):
    B, Tfull = 8, 4096

    def A(x, dt=np.float32):
        return np.ascontiguousarray(np.asarray(x), dtype=dt)

    def T_(x, dt=np.float32):
        return np.ascontiguousarray(np.asarray(x).T, dtype=dt)

    if "nc" not in _NC_CACHE:
        _NC_CACHE["nc"] = build_nc(T=Tfull, ncores=B)
    nc = _NC_CACHE["nc"]
    x = np.asarray(inputs["L0_attn_x"], dtype=np.float32)
    xT = np.ascontiguousarray(x.transpose(0, 2, 1))
    common = {
        "l0_wqT": T_(inputs["L0_attn_Wq"]),
        "l0_wkT": T_(inputs["L0_attn_Wk"]),
        "l0_wvT": T_(inputs["L0_attn_Wv"]),
        "l0_woT": T_(inputs["L0_attn_Wo"]),
        "l0_wgT": T_(inputs["L0_ffn_w_gate"], np.int32),
        "l0_wuT": T_(inputs["L0_ffn_w_up"], np.int32),
        "l0_wdT": T_(inputs["L0_ffn_w_down"], np.int32),
        "l1_wq": A(inputs["L1_attn_wq"]),
        "l1_wk": A(inputs["L1_attn_wk"]),
        "l1_wv": A(inputs["L1_attn_wv"]),
        "l1_wo": A(inputs["L1_attn_wo"]),
        "l1_wg1": A(inputs["L1_attn_wg1"]),
        "l1_wg2": A(inputs["L1_attn_wg2"]),
        "l1_nw": np.ascontiguousarray(
            np.asarray(inputs["L1_attn_norm_w"], dtype=np.float32)
            .reshape(NCH, 128).T),
        "l1_w1": A(inputs["L1_ffn_w1"]),
        "l1_b1": np.ascontiguousarray(
            np.asarray(inputs["L1_ffn_b1"], dtype=np.float32)
            .reshape(FF1 // 128, 128).T),
        "l1_w2": A(inputs["L1_ffn_w2"]),
        "l1_b2": np.ascontiguousarray(
            np.asarray(inputs["L1_ffn_b2"], dtype=np.float32)
            .reshape(NCH, 128).T),
    }
    in_maps = [{"xT": np.ascontiguousarray(xT[i]), **common} for i in range(B)]
    res = run_bass_kernel_spmd(nc, in_maps, core_ids=list(range(B)))
    out = np.stack([np.asarray(r["outT"], dtype=np.float32).T
                    for r in res.results])
    return np.ascontiguousarray(out)


# version 21 (final)
# speedup vs baseline: 1.0229x; 1.0126x over previous
from contextlib import ExitStack

import numpy as np
import concourse.bass as bass
import concourse.mybir as mybir
from concourse.bass_utils import run_bass_kernel_spmd

F32 = mybir.dt.float32
BF16 = mybir.dt.bfloat16
I32 = mybir.dt.int32
AF = mybir.ActivationFunctionType
AL = mybir.AluOpType
AX = mybir.AxisListType

D = 1024
NCH = 8
H0 = 16
HD0 = 64
WIN = 512
H1 = 8
D1 = 128
FF0 = 3072
FF1 = 4096
EPS_LN = 1e-5
EPS_GN = 1e-5
EPS_Q = 1e-8
INV127 = 1.0 / 127.0
TT = 512
RMAGIC = 12582912.0


_UID = [0]


def _u(name):
    _UID[0] += 1
    return f"{name}_{_UID[0]}"


class Sem:
    def __init__(self, nc, name):
        self.h = nc.semaphore(name).__enter__()
        self.v = 0

    def tag(self, inst, n=1):
        inst.then_inc(self.h, n)
        self.v += n
        return self.v


class Kb:

    def __init__(self, nc, T, ncores):
        self.nc = nc
        self.T = T
        self.NT = T // TT
        self.ncores = ncores
        self.ldh = Sem(nc, "ldh")
        self.ldh2 = Sem(nc, "ldh2")
        self.sth = Sem(nc, "sth")
        self.sth2 = Sem(nc, "sth2")
        self.lds = Sem(nc, "lds")
        self.lds2 = Sem(nc, "lds2")
        self.sts = Sem(nc, "sts")
        self.sts2 = Sem(nc, "sts2")
        self.pe = Sem(nc, "pe")
        self.ve = Sem(nc, "ve")
        self.se = Sem(nc, "se")
        self.ge = Sem(nc, "ge")
        self.cc = Sem(nc, "cc")
        self.ph = Sem(nc, "ph")
        self.nbar = 0

    def barrier(self, block):
        self.nbar += 1
        target = 5 * self.nbar
        h = self.ph.h
        self.ph.v = target
        pend = {"sync": [(self.ldh.h, self.ldh.v), (self.ldh2.h, self.ldh2.v),
                         (self.sts.h, self.sts.v), (self.sts2.h, self.sts2.v)],
                "scalar": [(self.sth.h, self.sth.v), (self.sth2.h, self.sth2.v)],
                "gpsimd": [(self.lds.h, self.lds.v), (self.lds2.h, self.lds2.v)]}
        for name, reg in (("sync", block.sync), ("scalar", block.scalar),
                          ("vector", block.vector), ("gpsimd", block.gpsimd),
                          ("tensor", block.tensor)):
            def body(e, h=h, target=target, p=pend.get(name)):
                for ph_, pv_ in (p or []):
                    e.wait_ge(ph_, pv_)
                e.sem_inc(h, 1)
                e.wait_ge(h, target)
            reg(body)


def _apply_ln_batched(e, X, src, mu, rstd, o32, obf, sl, th, residual,
                      tt_=TT):
    hw = tt_ // 2
    hs = slice(th * hw, (th + 1) * hw)
    mu_b = mu[:, hs].unsqueeze(1).to_broadcast([128, NCH, hw])
    rs_b = rstd[:, hs].unsqueeze(1).to_broadcast([128, NCH, hw])
    if not residual:
        xin = src[:, :, sl][:, :, hs]
        e.tensor_sub(o32[:, :, hs], xin, mu_b)
        e.drain()
        e.tensor_mul(o32[:, :, hs], o32[:, :, hs], rs_b)
        e.drain()
        if obf is not None:
            e.tensor_copy(obf[:, :, hs], o32[:, :, hs])
            e.drain()
    else:
        e.tensor_sub(o32[:, :, hs], src[:, :, hs], mu_b)
        e.drain()
        e.tensor_mul(o32[:, :, hs], o32[:, :, hs], rs_b)
        e.drain()
        Xs = X[:, :, sl]
        e.tensor_add(Xs[:, :, hs], Xs[:, :, hs], o32[:, :, hs])
        e.drain()


def ln_pre_phase(kb, block, bufs, dst_bf=None, dst_f32=None, want_max=False):
    nc, T, NT = kb.nc, kb.T, kb.NT
    X, ones = bufs["X"], bufs["ones128"]
    VCH = [0, 1, 2]
    GCH = [3, 4, 5, 6, 7]

    with (
        nc.sbuf_tensor(_u("ln_xb"), [128, NCH, TT], BF16) as xb,
        nc.sbuf_tensor(_u("ln_xsq"), [128, NCH, TT], BF16) as xsq,
        nc.sbuf_tensor(_u("ln_mu"), [128, TT], F32) as mu,
        nc.sbuf_tensor(_u("ln_es"), [128, TT], F32) as es,
        nc.sbuf_tensor(_u("ln_rstd"), [128, TT], F32) as rstd,
        nc.sbuf_tensor(_u("ln_o32"), [128, NCH, TT], F32) as o32,
        nc.sbuf_tensor(_u("ln_obf"), [128, NCH, TT], BF16) as obf,
        nc.sbuf_tensor(_u("ln_mxc"), [128, 1], F32) as mxc,
        nc.sbuf_tensor(_u("ln_mxt"), [128, 1], F32) as mxt,
        nc.psum_tensor(_u("ln_psA"), [128, TT], F32) as psA,
        nc.psum_tensor(_u("ln_psB"), [128, TT], F32) as psB,
    ):
        ve0, pe0, se0, ge0, ldh0 = kb.ve.v, kb.pe.v, kb.se.v, kb.ge.v, kb.ldh.v
        nst = (1 if dst_bf is not None else 0) + (1 if dst_f32 is not None else 0)
        VT = 5 if want_max else 4

        def st_done(i):
            return ldh0 + 16 * nst * (i + 1)

        def vec(e):
            for i in range(NT):
                sl = slice(i * TT, (i + 1) * TT)
                e.tensor_copy(xb.ap(), X[:, :, sl])
                e.drain()
                e.tensor_tensor(xsq.ap(), xb.ap(), xb.ap(), AL.mult)
                kb.ve.tag(e.drain())
                e.wait_ge(kb.pe.h, pe0 + i + 1)
                if i > 0:
                    e.wait_ge(kb.ge.h, ge0 + i)
                e.tensor_scalar(mu[:], psA[:], 1.0 / D, None, AL.mult)
                e.drain()
                e.tensor_scalar(es[:], psB[:], 1.0 / D, None, AL.mult)
                e.drain()
                e.tensor_tensor(rstd[:], mu[:], mu[:], AL.mult)
                e.drain()
                e.tensor_sub(es[:], es[:], rstd[:])
                kb.ve.tag(e.drain())
                e.wait_ge(kb.se.h, se0 + i + 1)
                e.reciprocal(rstd[:], es[:])
                kb.ve.tag(e.drain())
                if i > 0 and nst:
                    e.wait_ge(kb.ldh.h, st_done(i - 1))
                _apply_ln_batched(e, None, X, mu, rstd, o32, obf, sl, 0,
                                  residual=False)
                kb.ve.tag(e.drain())
                if want_max:
                    e.wait_ge(kb.ge.h, ge0 + i + 1)
                    for dc in range(NCH):
                        e.reduce_max(mxt[:], o32[:, dc, :], axis=AX.X)
                        e.drain()
                        if i == 0 and dc == 0:
                            e.tensor_copy(mxc[:], mxt[:])
                        else:
                            e.tensor_max(mxc[:], mxc[:], mxt[:])
                        e.drain()
                    kb.ve.tag(e.drain())

        def sca(e):
            for i in range(NT):
                e.wait_ge(kb.ve.h, ve0 + VT * i + 2)
                e.activation(es[:], es[:], AF.Sqrt, bias=bufs["ceps"][:], scale=1.0)
                kb.se.tag(e.drain())

        def ten(e):
            for i in range(NT):
                e.wait_ge(kb.ve.h, ve0 + VT * i + 1)
                for dc in range(NCH):
                    e.matmul(psA[:], ones[:], xb[:, dc, :], start=(dc == 0),
                             stop=(dc == NCH - 1))
                last = None
                for dc in range(NCH):
                    last = e.matmul(psB[:], ones[:], xsq[:, dc, :],
                                    start=(dc == 0), stop=(dc == NCH - 1))
                kb.pe.tag(last)

        def gp(e):
            for i in range(NT):
                sl = slice(i * TT, (i + 1) * TT)
                e.wait_ge(kb.ve.h, ve0 + VT * i + 3)
                if i > 0 and nst:
                    e.wait_ge(kb.ldh.h, st_done(i - 1))
                if i > 0 and want_max:
                    e.wait_ge(kb.ve.h, ve0 + VT * i)
                _apply_ln_batched(e, None, X, mu, rstd, o32, obf, sl, 1,
                                  residual=False)
                kb.ge.tag(e.drain())

        def st(e):
            for i in range(NT):
                sl_lo = i * TT
                e.wait_ge(kb.ve.h, ve0 + VT * i + 4)
                e.wait_ge(kb.ge.h, ge0 + i + 1)
                if dst_bf is not None:
                    dst = dst_bf.ap()[:, sl_lo:sl_lo + TT]
                    dst = dst.rearrange("(c p) t -> p c t", p=128)
                    kb.ldh.tag(e.dma_start(out=dst, in_=obf.ap()), 16)
                if dst_f32 is not None:
                    dst = dst_f32.ap()[:, sl_lo:sl_lo + TT]
                    dst = dst.rearrange("(c p) t -> p c t", p=128)
                    kb.ldh.tag(e.dma_start(out=dst, in_=o32.ap()), 16)
            if nst:
                e.wait_ge(kb.ldh.h, kb.ldh.v)
            assert kb.ldh.v == st_done(NT - 1), (kb.ldh.v, st_done(NT - 1))

        block.vector(vec)
        block.scalar(sca)
        block.tensor(ten)
        block.gpsimd(gp)
        block.sync(st)

        if want_max:
            def gmax(e):
                e.wait_ge(kb.ve.h, ve0 + VT * NT)
                e.tensor_reduce(bufs["mx1"][:], mxc[:], AX.C, AL.max)
                kb.ge.tag(e.drain())
            block.gpsimd(gmax)

        kb.barrier(block)


def simple_proj_phase(kb, block, bufs, w_dram, src_dram, dst_dram, func,
                      scale, out_dtype, w_rows=D, jobs=None):
    nc, T, NT = kb.nc, kb.T, kb.NT
    KC = w_rows // 128
    FC = D // 128
    if jobs is None:
        jobs = [(w_dram, dst_dram, func, scale, out_dtype)]
    NJ = len(jobs)
    with ExitStack() as ctx:
        wt = [ctx.enter_context(
            nc.sbuf_tensor(_u("pj_wt"), [128, KC, D], BF16)) for _ in jobs]
        ob = [ctx.enter_context(
            nc.sbuf_tensor(_u("pj_ob"), [128, FC, TT], j[4])) for j in jobs]
        mv0 = ctx.enter_context(
            nc.sbuf_tensor(_u("pj_mv0"), [128, KC, TT], BF16))
        mv1 = ctx.enter_context(
            nc.sbuf_tensor(_u("pj_mv1"), [128, KC, TT], BF16))
        psA = ctx.enter_context(nc.psum_tensor(_u("pj_psA"), [128, TT], F32))
        psB = ctx.enter_context(nc.psum_tensor(_u("pj_psB"), [128, TT], F32))

        mv = [mv0, mv1]
        ps = [psA, psB]
        L = [kb.ldh, kb.ldh2]
        S = [kb.sth, kb.sth2]
        ve0, pe0, se0, ge0, lds0 = (kb.ve.v, kb.pe.v, kb.se.v, kb.ge.v,
                                    kb.lds.v)
        l0 = [L[0].v, L[1].v]
        s0 = [S[0].v, S[1].v]

        def gld(e):
            for j, jb in enumerate(jobs):
                kb.lds.tag(e.dma_start(
                    out=wt[j].ap(),
                    in_=jb[0].ap().rearrange("(c p) f -> p c f", p=128)), 16)
        block.gpsimd(gld)

        def ld(e):
            for i in range(NT):
                if i >= 2:
                    e.wait_ge(kb.pe.h, pe0 + (i - 1) * FC * NJ)
                src = src_dram.ap()[:, i * TT:(i + 1) * TT]
                L[i % 2].tag(e.dma_start(
                    out=mv[i % 2].ap(),
                    in_=src.rearrange("(c p) t -> p c t", p=128)), 16)
        block.sync(ld)

        def ten(e):
            e.wait_ge(kb.lds.h, lds0 + 16 * NJ)
            g = 0
            for i in range(NT):
                e.wait_ge(L[i % 2].h, l0[i % 2] + 16 * (i // 2 + 1))
                for j in range(NJ):
                    for fc in range(FC):
                        if g >= 2:
                            e.wait_ge(kb.se.h, se0 + g - 1)
                        last = None
                        for kc in range(KC):
                            last = e.matmul(ps[g % 2][:],
                                            wt[j][:, kc, fc * 128:(fc + 1) * 128],
                                            mv[i % 2][:, kc, :],
                                            start=(kc == 0), stop=(kc == KC - 1))
                        kb.pe.tag(last)
                        g += 1
        block.tensor(ten)

        def sca(e):
            g = 0
            for i in range(NT):
                for j, jb in enumerate(jobs):
                    for fc in range(FC):
                        e.wait_ge(kb.pe.h, pe0 + g + 1)
                        if i > 0 and j == 0 and fc == 0:
                            p = (i - 1) % 2
                            e.wait_ge(S[p].h,
                                      s0[p] + 16 * NJ * ((i - 1) // 2 + 1))
                        e.activation(ob[j][:, fc, :], ps[g % 2][:], jb[2],
                                     bias=0.0, scale=jb[3])
                        kb.se.tag(e.drain())
                        g += 1
                for j, jb in enumerate(jobs):
                    dst = jb[1].ap()[:, i * TT:(i + 1) * TT]
                    dst = dst.rearrange("(c p) t -> p c t", p=128)
                    S[i % 2].tag(e.dma_start(out=dst, in_=ob[j].ap()), 16)
        block.scalar(sca)

        kb.barrier(block)


def v_proj_phase(kb, block, bufs, w_dram, src_dram, dst_dram, padded):
    nc, T = kb.nc, kb.T
    NTC = T // 128
    with (
        nc.sbuf_tensor(_u("vp_wt"), [128, NCH, D], BF16) as wt,
        nc.sbuf_tensor(_u("vp_mv0"), [128, NCH, 128], BF16) as mv0,
        nc.sbuf_tensor(_u("vp_mv1"), [128, NCH, 128], BF16) as mv1,
        nc.sbuf_tensor(_u("vp_ob0"), [128, 2, 512], BF16) as ob0,
        nc.sbuf_tensor(_u("vp_ob1"), [128, 2, 512], BF16) as ob1,
        nc.psum_tensor(_u("vp_psA"), [128, 512], F32) as psA,
        nc.psum_tensor(_u("vp_psB"), [128, 512], F32) as psB,
    ):
        mv = [mv0, mv1]
        ob = [ob0, ob1]
        ps = [psA, psB]
        L = [kb.ldh, kb.ldh2]
        S = [kb.sth, kb.sth2]
        ve0, pe0, se0, lds0 = kb.ve.v, kb.pe.v, kb.se.v, kb.lds.v
        l0 = [L[0].v, L[1].v]
        s0 = [S[0].v, S[1].v]

        def gld(e):
            kb.lds.tag(e.dma_start(
                out=wt.ap(),
                in_=w_dram.ap().rearrange("(c p) f -> p c f", p=128)), 16)
        block.gpsimd(gld)

        def ld(e):
            for i in range(NTC):
                if i >= 2:
                    e.wait_ge(kb.pe.h, pe0 + 2 * (i - 1))
                src = src_dram.ap()[:, i * 128:(i + 1) * 128]
                L[i % 2].tag(e.dma_start(
                    out=mv[i % 2].ap(),
                    in_=src.rearrange("(c p) t -> p c t", p=128)), 16)
        block.sync(ld)

        def ten(e):
            e.wait_ge(kb.lds.h, lds0 + 16)
            for i in range(NTC):
                e.wait_ge(L[i % 2].h, l0[i % 2] + 16 * (i // 2 + 1))
                if i >= 1:
                    e.wait_ge(kb.se.h, se0 + 2 * i)
                for fh in range(2):
                    last = None
                    for kc in range(NCH):
                        last = e.matmul(ps[fh][:],
                                        mv[i % 2][:, kc, :],
                                        wt[:, kc, fh * 512:(fh + 1) * 512],
                                        start=(kc == 0), stop=(kc == NCH - 1))
                    kb.pe.tag(last)
        block.tensor(ten)

        def sca(e):
            for i in range(NTC):
                if i >= 2:
                    j = (i - 2) % 2
                    e.wait_ge(S[j].h, s0[j] + 16 * ((i - 2) // 2 + 1))
                for fh in range(2):
                    e.wait_ge(kb.pe.h, pe0 + 2 * i + fh + 1)
                    e.activation(ob[i % 2][:, fh, :], ps[fh][:], AF.Copy,
                                 bias=0.0, scale=1.0)
                    kb.se.tag(e.drain())
                if padded:
                    dst = dst_dram.ap()[i * 128:(i + 1) * 128, :]
                    dst = dst.rearrange("p (h c) -> p h c", c=128)[:, :, 0:64]
                    srcap = ob[i % 2].ap().rearrange(
                        "p a t -> p (a t)").rearrange("p (h c) -> p h c", c=64)
                    S[i % 2].tag(e.dma_start(out=dst, in_=srcap), 16)
                else:
                    dst = dst_dram.ap()[i * 128:(i + 1) * 128, :]
                    S[i % 2].tag(e.dma_start(
                        out=dst, in_=ob[i % 2].ap().rearrange("p a t -> p (a t)")),
                        16)
        block.scalar(sca)

        kb.barrier(block)


def attn0_phase(kb, block, bufs, q_s, k_s, v_s, o_s):
    nc, T = kb.nc, kb.T
    NSEG = T // TT
    with (
        nc.sbuf_tensor(_u("at_qb0"), [128, TT], BF16) as qb0,
        nc.sbuf_tensor(_u("at_qb1"), [128, TT], BF16) as qb1,
        nc.sbuf_tensor(_u("at_kw0"), [128, 1024], BF16) as kw0,
        nc.sbuf_tensor(_u("at_kw1"), [128, 1024], BF16) as kw1,
        nc.sbuf_tensor(_u("at_vw0"), [128, 8, 256], BF16) as vw0,
        nc.sbuf_tensor(_u("at_vw1"), [128, 8, 256], BF16) as vw1,
        nc.sbuf_tensor(_u("at_pb0"), [128, 8, 2, TT], BF16) as pb0,
        nc.sbuf_tensor(_u("at_pb1"), [128, 8, 2, TT], BF16) as pb1,
        nc.sbuf_tensor(_u("at_msk"), [128, 8, TT], BF16) as msk,
        nc.sbuf_tensor(_u("at_ob"), [128, 8, TT], BF16) as obuf,
        nc.sbuf_tensor(_u("at_rc"), [128, TT], F32) as rc,
        nc.sbuf_tensor(_u("at_rc2"), [128, TT], F32) as rc2,
        nc.psum_tensor(_u("at_sc0"), [128, 2, TT], F32) as sc0,
        nc.psum_tensor(_u("at_sc1"), [128, 2, TT], F32) as sc1,
        nc.psum_tensor(_u("at_pOA"), [128, TT], F32) as pOA,
        nc.psum_tensor(_u("at_pOB"), [128, TT], F32) as pOB,
        nc.psum_tensor(_u("at_pSA"), [128, TT], F32) as pSA,
        nc.psum_tensor(_u("at_pSB"), [128, TT], F32) as pSB,
    ):
        qb = [qb0, qb1]
        kw = [kw0, kw1]
        vw = [vw0, vw1]
        pb = [pb0, pb1]
        sc = [sc0, sc1]
        L = [kb.ldh, kb.ldh2]
        S = [kb.sth, kb.sth2]
        ones = bufs["ones128"]

        def kcs_of(s):
            return list(range(4, 8)) if s == 0 else list(range(8))

        DVEI = (1, 3, 5, 7)

        def dve_idx(s):
            return [i for i in range(len(kcs_of(s))) if i in DVEI]

        def pool_idx(s):
            return [i for i in range(len(kcs_of(s))) if i not in DVEI]

        groups = [(s, hp) for s in range(NSEG) for hp in range(8)]
        NG = len(groups)
        ve0, pe0, se0, ge0 = kb.ve.v, kb.pe.v, kb.se.v, kb.ge.v
        l0 = [L[0].v, L[1].v]
        s0 = [S[0].v, S[1].v]

        def mk(e):
            for kc in range(8):
                e.affine_select(msk[:, kc, :], ones[:, 0:1].to_broadcast(
                    [128, TT]), [[1, TT]], AL.is_ge, 0.0,
                    base=512 - kc * 128, channel_multiplier=-1)
                e.drain()
                e.affine_select(msk[:, kc, :], msk[:, kc, :], [[-1, TT]],
                                AL.is_ge, 0.0, base=kc * 128 - 1,
                                channel_multiplier=1)
                e.drain()
            kb.ge.tag(e.drain())
        block.gpsimd(mk)
        ge0 += 1

        pe_sc = {}
        pe_pv = {}
        v = pe0
        for g, (s, hp) in enumerate(groups):
            for kci in range(len(kcs_of(s))):
                v += 1
                pe_sc[(g, kci)] = v
                if kci == 1 and g >= 1:
                    v += 1
                    pe_pv[g - 1] = v
        v += 1
        pe_pv[NG - 1] = v
        se_t = {}
        se_sg = {}
        v = se0
        for g, (s, hp) in enumerate(groups):
            for kci in range(len(kcs_of(s))):
                v += 1
                se_t[(g, kci)] = v
                if kci == 1 and g >= 1:
                    v += 1
                    se_sg[g - 1] = v
        v += 1
        se_sg[NG - 1] = v
        ge_m = {}
        v = ge0
        for g, (s, hp) in enumerate(groups):
            for kci in pool_idx(s):
                v += 1
                ge_m[(g, kci)] = v
        ve_m = {}
        ve_r = {}
        v = ve0
        for g, (s, hp) in enumerate(groups):
            for kci in dve_idx(s):
                v += 1
                ve_m[(g, kci)] = v
            v += 1
            ve_r[g] = v

        def last_same_parity(g, kci):
            n = len(kcs_of(groups[g][0]))
            j = n - 1
            while j % 2 != kci % 2:
                j -= 1
            return j

        def ld_done(g):
            return l0[g % 2] + 48 * (g // 2 + 1)
        def st_done(s):
            return s0[s % 2] + 16 * (s // 2 + 1)

        def ld2(e):
            for g, (s, hp) in enumerate(groups):
                if g >= 2:
                    e.wait_ge(kb.pe.h, pe_pv[g - 2])
                Q0 = s * TT
                kl = Q0 - 512
                L[g % 2].tag(e.dma_start(
                    out=qb[g % 2][:, :],
                    in_=q_s.ap()[hp * 128:(hp + 1) * 128, Q0:Q0 + TT]), 16)
                if s == 0:
                    L[g % 2].tag(e.dma_start(
                        out=kw[g % 2][:, 512:1024],
                        in_=k_s.ap()[hp * 128:(hp + 1) * 128, 0:512]), 16)
                    vsrc = v_s.ap()[0:512, hp * 256:(hp + 1) * 256]
                    L[g % 2].tag(e.dma_start(
                        out=vw[g % 2][:, 4:8, :],
                        in_=vsrc.rearrange("(c p) f -> p c f", p=128)), 16)
                else:
                    L[g % 2].tag(e.dma_start(
                        out=kw[g % 2][:, :],
                        in_=k_s.ap()[hp * 128:(hp + 1) * 128, kl:kl + 1024]),
                        16)
                    vsrc = v_s.ap()[kl:kl + 1024, hp * 256:(hp + 1) * 256]
                    L[g % 2].tag(e.dma_start(
                        out=vw[g % 2][:, :, :],
                        in_=vsrc.rearrange("(c p) f -> p c f", p=128)), 16)
        block.sync(ld2)

        def emit_pv(e, g):
            s, hp = groups[g]
            kcs = kcs_of(s)
            e.wait_ge(kb.ge.h, ge_m[(g, pool_idx(s)[-1])])
            e.wait_ge(kb.ve.h, ve_m[(g, dve_idx(s)[-1])])
            if g >= 1:
                e.wait_ge(kb.ve.h, ve_r[g - 1])
                e.wait_ge(kb.se.h, se_sg[g - 1])
            for kci, kc in enumerate(kcs):
                first, last_f = (kci == 0), (kci == len(kcs) - 1)
                e.matmul(pOA[0:64, :], vw[g % 2][:, kc, 0:64],
                         pb[g % 2][:, kc, 0, :], start=first,
                         stop=last_f, tile_position=(0, 0))
                e.matmul(pOB[64:128, :], vw[g % 2][:, kc, 128:192],
                         pb[g % 2][:, kc, 1, :], start=first,
                         stop=last_f, tile_position=(0, 64))
                e.matmul(pSA[0:64, :], ones[:, 0:64],
                         pb[g % 2][:, kc, 0, :], start=first,
                         stop=last_f, tile_position=(0, 0))
                lastmm = e.matmul(pSB[64:128, :], ones[:, 0:64],
                                  pb[g % 2][:, kc, 1, :], start=first,
                                  stop=last_f, tile_position=(0, 64))
            kb.pe.tag(lastmm)
            assert kb.pe.v == pe_pv[g]

        def ten(e):
            for g, (s, hp) in enumerate(groups):
                kcs = kcs_of(s)
                e.wait_ge(L[g % 2].h, ld_done(g))
                for kci, kc in enumerate(kcs):
                    if kci >= 2:
                        e.wait_ge(kb.se.h, se_t[(g, kci - 2)])
                    elif g >= 1:
                        e.wait_ge(kb.se.h,
                                  se_t[(g - 1, last_same_parity(g - 1, kci))])
                    st_a = kw[g % 2][0:64, kc * 128:(kc + 1) * 128]
                    st_b = kw[g % 2][64:128, kc * 128:(kc + 1) * 128]
                    e.matmul(sc[kci % 2][:, 0, :], st_a, qb[g % 2][0:64, :],
                             start=True, stop=True, tile_position=(0, 0))
                    last = e.matmul(sc[kci % 2][:, 1, :], st_b,
                                    qb[g % 2][64:128, :], start=True,
                                    stop=True, tile_position=(64, 0))
                    kb.pe.tag(last)
                    assert kb.pe.v == pe_sc[(g, kci)]
                    if kci == 1 and g >= 1:
                        emit_pv(e, g - 1)
            emit_pv(e, NG - 1)
        block.tensor(ten)

        def emit_sigma(e, g):
            s, hp = groups[g]
            e.wait_ge(kb.pe.h, pe_pv[g])
            if g >= 1:
                e.wait_ge(kb.ve.h, ve_r[g - 1])
            e.activation(rc[0:64, :], pSA[0:64, :], AF.Ln, bias=0.0,
                         scale=1.0)
            e.drain()
            e.activation(rc[64:128, :], pSB[64:128, :], AF.Ln, bias=0.0,
                         scale=1.0)
            e.drain()
            e.activation(rc2[0:64, :], rc[0:64, :], AF.Exp, bias=0.0,
                         scale=-1.0)
            e.drain()
            e.activation(rc2[64:128, :], rc[64:128, :], AF.Exp, bias=0.0,
                         scale=-1.0)
            kb.se.tag(e.drain())
            assert kb.se.v == se_sg[g]
            if hp == 7:
                e.wait_ge(kb.ve.h, ve_r[g])
                dst = o_s.ap()[:, s * TT:(s + 1) * TT]
                dst = dst.rearrange("(c p) t -> p c t", p=128)
                S[s % 2].tag(e.dma_start(out=dst, in_=obuf.ap()), 16)

        def sca(e):
            for g, (s, hp) in enumerate(groups):
                kcs = kcs_of(s)
                if g >= 2:
                    e.wait_ge(kb.pe.h, pe_pv[g - 2])
                for kci, kc in enumerate(kcs):
                    e.wait_ge(kb.pe.h, pe_sc[(g, kci)])
                    e.activation(pb[g % 2][:, kc, :, :], sc[kci % 2].ap(),
                                 AF.Exp, bias=0.0, scale=0.125)
                    kb.se.tag(e.drain())
                    assert kb.se.v == se_t[(g, kci)]
                    if kci == 1 and g >= 1:
                        emit_sigma(e, g - 1)
            emit_sigma(e, NG - 1)
        block.scalar(sca)

        def gp(e):
            e.wait_ge(kb.ge.h, ge0)
            for g, (s, hp) in enumerate(groups):
                kcs = kcs_of(s)
                for kci in pool_idx(s):
                    kc = kcs[kci]
                    e.wait_ge(kb.se.h, se_t[(g, kci)])
                    e.tensor_mul(pb[g % 2][:, kc, :, :],
                                 pb[g % 2][:, kc, :, :],
                                 msk[:, kc, :].unsqueeze(1).to_broadcast(
                                     [128, 2, TT]))
                    kb.ge.tag(e.drain())
                    assert kb.ge.v == ge_m[(g, kci)]
        block.gpsimd(gp)

        def vec(e):
            e.wait_ge(kb.ge.h, ge0)
            for g, (s, hp) in enumerate(groups):
                kcs = kcs_of(s)
                for kci in dve_idx(s):
                    kc = kcs[kci]
                    e.wait_ge(kb.se.h, se_t[(g, kci)])
                    e.tensor_mul(pb[g % 2][:, kc, :, :],
                                 pb[g % 2][:, kc, :, :],
                                 msk[:, kc, :].unsqueeze(1).to_broadcast(
                                     [128, 2, TT]))
                    kb.ve.tag(e.drain())
                    assert kb.ve.v == ve_m[(g, kci)]
                e.wait_ge(kb.se.h, se_sg[g])
                if hp == 0 and s >= 1:
                    e.wait_ge(S[(s - 1) % 2].h, st_done(s - 1))
                e.tensor_mul(obuf[0:64, hp, :], pOA[0:64, :], rc2[0:64, :])
                e.drain()
                e.tensor_mul(obuf[64:128, hp, :], pOB[64:128, :],
                             rc2[64:128, :])
                kb.ve.tag(e.drain())
                assert kb.ve.v == ve_r[g]
        block.vector(vec)

        kb.barrier(block)


def proj_postln_phase(kb, block, bufs, src_dram, KC, w_dram=None,
                      w_bf_dram=None, evac_scale=1.0, evac_scale_col=None,
                      evac_bias_col=None, w_tiles_dram=None, tt=TT):
    nc, T = kb.nc, kb.T
    NT = T // tt
    X, ones = bufs["X"], bufs["ones128"]
    FC = NCH
    VCH = [0, 1, 2]
    GCH = [3, 4, 5, 6, 7]
    resident = w_dram is not None
    with (
        nc.sbuf_tensor(_u("pp_wt"), [128, KC, D if resident else 256], BF16) as wt,
        nc.sbuf_tensor(_u("pp_mv0"), [128, KC, tt], BF16) as mv0,
        nc.sbuf_tensor(_u("pp_mv1"), [128, KC, tt], BF16) as mv1,
        nc.sbuf_tensor(_u("pp_y"), [128, NCH, tt], F32) as y32,
        nc.sbuf_tensor(_u("pp_yb"), [128, NCH, tt], BF16) as ybf,
        nc.sbuf_tensor(_u("pp_ys"), [128, NCH, tt], BF16) as ysq,
        nc.sbuf_tensor(_u("pp_mu"), [128, tt], F32) as mu,
        nc.sbuf_tensor(_u("pp_es"), [128, tt], F32) as es,
        nc.sbuf_tensor(_u("pp_rstd"), [128, tt], F32) as rstd,
        nc.psum_tensor(_u("pp_pY0"), [128, tt], F32) as pY0,
        nc.psum_tensor(_u("pp_pY1"), [128, tt], F32) as pY1,
        nc.psum_tensor(_u("pp_psA"), [128, tt], F32) as psA,
        nc.psum_tensor(_u("pp_psB"), [128, tt], F32) as psB,
    ):
        mv = [mv0, mv1]
        pY = [pY0, pY1]
        L = [kb.ldh, kb.ldh2]
        S = [kb.sth, kb.sth2]
        ve0, pe0, se0, ge0, lds0 = (kb.ve.v, kb.pe.v, kb.se.v, kb.ge.v,
                                    kb.lds.v)
        l0 = [L[0].v, L[1].v]
        s0 = [S[0].v, S[1].v]

        if resident:
            def gld(e):
                kb.lds.tag(e.dma_start(
                    out=wt.ap()[:, :, 0:D],
                    in_=w_dram.ap().rearrange("(c p) f -> p c f", p=128)), 16)
            block.gpsimd(gld)

        def gidx(i, fc):
            return i * FC + fc

        def pe_proj(i, fc):
            return pe0 + i * (FC + 1) + fc + 1

        def pe_stats(i):
            return pe0 + (i + 1) * (FC + 1)

        def wst_done(g):
            return s0[g % 2] + 16 * (g // 2 + 1)

        def emit_wload(e, g):
            ii, ff = divmod(g, FC)
            if g >= 2:
                e.wait_ge(kb.pe.h, pe0 + ((g - 2) // FC) * (FC + 1)
                          + ((g - 2) % FC) + 1)
            srcw = w_bf_dram.ap()[:, ff * 128:(ff + 1) * 128]
            srcw = srcw.rearrange("(c p) f -> p c f", p=128)
            S[g % 2].tag(e.dma_start(
                out=wt[:, :, (g % 2) * 128:(g % 2) * 128 + 128], in_=srcw), 16)

        def ld(e):
            wp = 0
            for i in range(NT):
                if i >= 2:
                    e.wait_ge(kb.pe.h, pe0 + (i - 2) * (FC + 1) + FC)
                src = src_dram.ap()[:, i * tt:(i + 1) * tt]
                L[i % 2].tag(e.dma_start(
                    out=mv[i % 2].ap(),
                    in_=src.rearrange("(c p) t -> p c t", p=128)), 16)
                if not resident:
                    while wp < min((i + 1) * FC + 2, NT * FC):
                        if wp % 2 == 0:
                            emit_wload(e, wp)
                        wp += 1
            if not resident:
                while wp < NT * FC:
                    if wp % 2 == 0:
                        emit_wload(e, wp)
                    wp += 1
        block.sync(ld)

        def sca(e):
            if not resident:
                emit_wload(e, 1)
            for i in range(NT):
                for fc in range(FC):
                    g = gidx(i, fc)
                    e.wait_ge(kb.pe.h, pe_proj(i, fc))
                    if fc == 0 and i > 0:
                        e.wait_ge(kb.ve.h, ve0 + 4 * i)
                        e.wait_ge(kb.ge.h, ge0 + i)
                    if evac_scale_col is not None:
                        e.activation(y32[:, fc, :], pY[g % 2][:],
                                     AF.Identity, bias=evac_bias_col[:, fc:fc + 1],
                                     scale=evac_scale_col[:, fc:fc + 1])
                    else:
                        e.activation(y32[:, fc, :], pY[g % 2][:],
                                     AF.Identity, bias=0.0, scale=evac_scale)
                    kb.se.tag(e.drain())
                    if (not resident) and g + 2 < NT * FC and (g % 2) == 1:
                        emit_wload(e, g + 2)
                e.wait_ge(kb.ve.h, ve0 + 4 * i + 2)
                e.activation(es[:], es[:], AF.Sqrt, bias=bufs["ceps"][:],
                             scale=1.0)
                kb.se.tag(e.drain())
        block.scalar(sca)

        def ten(e):
            if resident:
                e.wait_ge(kb.lds.h, lds0 + 16)
            for i in range(NT):
                e.wait_ge(L[i % 2].h, l0[i % 2] + 16 * (i // 2 + 1))
                for fc in range(FC):
                    g = gidx(i, fc)
                    if not resident:
                        e.wait_ge(S[g % 2].h, wst_done(g))
                    if g >= 2:
                        gp_ = g - 2
                        tgt = se0 + (gp_ // FC) * (FC + 1) + (gp_ % FC) + 1
                        e.wait_ge(kb.se.h, tgt)
                    last = None
                    for kc in range(KC):
                        stat = (wt[:, kc, fc * 128:(fc + 1) * 128] if resident
                                else wt[:, kc, (g % 2) * 128:(g % 2) * 128 + 128])
                        last = e.matmul(pY[g % 2][:], stat, mv[i % 2][:, kc, :],
                                        start=(kc == 0), stop=(kc == KC - 1))
                    kb.pe.tag(last)
                    assert kb.pe.v == pe_proj(i, fc)
                e.wait_ge(kb.ve.h, ve0 + 4 * i + 1)
                for dc in range(NCH):
                    e.matmul(psA[:], ones[:, 0:128], ybf[:, dc, :],
                             start=(dc == 0), stop=(dc == NCH - 1))
                last = None
                for dc in range(NCH):
                    last = e.matmul(psB[:], ones[:, 0:128], ysq[:, dc, :],
                                    start=(dc == 0), stop=(dc == NCH - 1))
                kb.pe.tag(last)
                assert kb.pe.v == pe_stats(i)
        block.tensor(ten)

        def vec(e):
            for i in range(NT):
                sl = slice(i * tt, (i + 1) * tt)
                e.wait_ge(kb.se.h, se0 + i * (FC + 1) + FC)
                e.tensor_copy(ybf.ap(), y32.ap())
                e.drain()
                e.tensor_tensor(ysq.ap(), ybf.ap(), ybf.ap(), AL.mult)
                kb.ve.tag(e.drain())
                e.wait_ge(kb.pe.h, pe_stats(i))
                if i > 0:
                    e.wait_ge(kb.ge.h, ge0 + i)
                e.tensor_scalar(mu[:], psA[:], 1.0 / D, None, AL.mult)
                e.drain()
                e.tensor_scalar(es[:], psB[:], 1.0 / D, None, AL.mult)
                e.drain()
                e.tensor_tensor(rstd[:], mu[:], mu[:], AL.mult)
                e.drain()
                e.tensor_sub(es[:], es[:], rstd[:])
                kb.ve.tag(e.drain())
                e.wait_ge(kb.se.h, se0 + i * (FC + 1) + FC + 1)
                e.reciprocal(rstd[:], es[:])
                kb.ve.tag(e.drain())
                _apply_ln_batched(e, X, y32, mu, rstd, y32, None, sl, 0,
                                  residual=True, tt_=tt)
                kb.ve.tag(e.drain())

        block.vector(vec)

        def gp(e):
            for i in range(NT):
                sl = slice(i * tt, (i + 1) * tt)
                e.wait_ge(kb.ve.h, ve0 + 4 * i + 3)
                _apply_ln_batched(e, X, y32, mu, rstd, y32, None, sl, 1,
                                  residual=True, tt_=tt)
                kb.ge.tag(e.drain())
        block.gpsimd(gp)

        kb.barrier(block)


def cast_weights_phase(kb, block, bufs, pairs):
    nc = kb.nc
    jobs = []
    for src, dst in pairs:
        R, C = src.ap().shape
        for r in range(R // 128):
            jobs.append((src, dst, r, C))
    CMAX = max(j[3] for j in jobs)
    with (
        nc.sbuf_tensor(_u("cw_b0"), [128, CMAX], BF16) as b0,
        nc.sbuf_tensor(_u("cw_b1"), [128, CMAX], BF16) as b1,
    ):
        bb = [b0, b1]
        Ls = [kb.lds, kb.lds2]
        Ss = [kb.sth, kb.sth2]
        l0 = [Ls[0].v, Ls[1].v]
        s0 = [Ss[0].v, Ss[1].v]

        def gld(e):
            for j, (src, dst, r, C) in enumerate(jobs):
                if j >= 2:
                    e.wait_ge(Ss[j % 2].h, s0[j % 2] + 16 * ((j - 2) // 2 + 1))
                Ls[j % 2].tag(e.dma_start(
                    out=bb[j % 2][:, 0:C],
                    in_=src.ap()[r * 128:(r + 1) * 128, :]), 16)
        block.gpsimd(gld)

        def st(e):
            for j, (src, dst, r, C) in enumerate(jobs):
                e.wait_ge(Ls[j % 2].h, l0[j % 2] + 16 * (j // 2 + 1))
                Ss[j % 2].tag(e.dma_start(
                    out=dst.ap()[r * 128:(r + 1) * 128, :],
                    in_=bb[j % 2][:, 0:C]), 16)
        block.scalar(st)

        kb.barrier(block)


def swiglu_phase(kb, block, bufs, wg_b, wu_b, xn_s, h_s):
    nc, T = kb.nc, kb.T
    TH = min(T // 2, 1024)
    NPASS = T // TH
    SUB = min(512, TH)
    NSUB = TH // SUB
    HC = FF0 // 128
    with (
        nc.sbuf_tensor(_u("sw_xn0"), [128, NCH, TH], BF16) as xn0,
        nc.sbuf_tensor(_u("sw_xn1"), [128, NCH, TH], BF16) as xn1,
        nc.sbuf_tensor(_u("sw_wg0"), [128, NCH, 128], BF16) as wg0,
        nc.sbuf_tensor(_u("sw_wg1"), [128, NCH, 128], BF16) as wg1,
        nc.sbuf_tensor(_u("sw_wu0"), [128, NCH, 128], BF16) as wu0,
        nc.sbuf_tensor(_u("sw_wu1"), [128, NCH, 128], BF16) as wu1,
        nc.sbuf_tensor(_u("sw_t10"), [128, SUB], F32) as t10,
        nc.sbuf_tensor(_u("sw_t11"), [128, SUB], F32) as t11,
        nc.sbuf_tensor(_u("sw_hb0"), [128, SUB], BF16) as hb0,
        nc.sbuf_tensor(_u("sw_hb1"), [128, SUB], BF16) as hb1,
        nc.psum_tensor(_u("sw_pG0"), [128, SUB], F32) as pG0,
        nc.psum_tensor(_u("sw_pG1"), [128, SUB], F32) as pG1,
        nc.psum_tensor(_u("sw_pU0"), [128, SUB], F32) as pU0,
        nc.psum_tensor(_u("sw_pU1"), [128, SUB], F32) as pU1,
    ):
        xnb = [xn0, xn1]
        wg = [wg0, wg1]
        wu = [wu0, wu1]
        t1 = [t10, t11]
        hb = [hb0, hb1]
        pG = [pG0, pG1]
        pU = [pU0, pU1]
        L = [kb.ldh, kb.ldh2]
        S = [kb.sth, kb.sth2]
        ve0, pe0, se0, ge0 = kb.ve.v, kb.pe.v, kb.se.v, kb.ge.v
        l0 = [L[0].v, L[1].v]
        s0 = [S[0].v, S[1].v]

        ld_tgt = {}
        lv = [l0[0], l0[1]]
        for h in range(NPASS):
            p = (h * (HC + 1)) % 2
            lv[p] += 16
            ld_tgt[("xn", h)] = (p, lv[p])
            for hc in range(HC):
                p2 = (h * (HC + 1) + hc + 1) % 2
                lv[p2] += 32
                ld_tgt[("w", h, hc)] = (p2, lv[p2])

        NG = NPASS * HC * NSUB

        def pe_t(g):
            return pe0 + g + 1

        def ld(e):
            for h in range(NPASS):
                p, tgt = ld_tgt[("xn", h)]
                if h >= 1:
                    e.wait_ge(kb.pe.h, pe_t((h * HC - 1) * NSUB))
                if h >= 2:
                    e.wait_ge(kb.pe.h, pe_t(((h - 1) * HC) * NSUB - 1 + 1))
                src = xn_s.ap()[:, h * TH:(h + 1) * TH]
                L[p].tag(e.dma_start(
                    out=xnb[h % 2].ap(),
                    in_=src.rearrange("(c p) t -> p c t", p=128)), 16)
                for hc in range(HC):
                    gslab = h * HC + hc
                    if gslab < 2:
                        e.wait_ge(L[p].h, tgt)
                    if gslab >= 2:
                        e.wait_ge(kb.pe.h, pe_t((gslab - 1) * NSUB))
                    p2, tgt2 = ld_tgt[("w", h, hc)]
                    L[p2].tag(e.dma_start(
                        out=wg[gslab % 2].ap(),
                        in_=wg_b.ap()[:, hc * 128:(hc + 1) * 128].rearrange(
                            "(c p) f -> p c f", p=128)), 16)
                    L[p2].tag(e.dma_start(
                        out=wu[gslab % 2].ap(),
                        in_=wu_b.ap()[:, hc * 128:(hc + 1) * 128].rearrange(
                            "(c p) f -> p c f", p=128)), 16)
        block.sync(ld)

        def ten(e):
            for h in range(NPASS):
                p, tgt = ld_tgt[("xn", h)]
                e.wait_ge(L[p].h, tgt)
                for hc in range(HC):
                    gslab = h * HC + hc
                    p2, tgt2 = ld_tgt[("w", h, hc)]
                    e.wait_ge(L[p2].h, tgt2)
                    for ts in range(NSUB):
                        g = gslab * NSUB + ts
                        if g >= 2:
                            e.wait_ge(kb.se.h, se0 + g - 1)
                            e.wait_ge(kb.ve.h, ve0 + g - 1)
                        sl = slice(ts * SUB, (ts + 1) * SUB)
                        for kc in range(NCH):
                            e.matmul(pG[g % 2][:], wg[gslab % 2][:, kc, :],
                                     xnb[h % 2][:, kc, sl], start=(kc == 0),
                                     stop=(kc == NCH - 1))
                        last = None
                        for kc in range(NCH):
                            last = e.matmul(pU[g % 2][:], wu[gslab % 2][:, kc, :],
                                            xnb[h % 2][:, kc, sl], start=(kc == 0),
                                            stop=(kc == NCH - 1))
                        kb.pe.tag(last)
                        assert kb.pe.v == pe_t(g)
        block.tensor(ten)

        def emit_store(e, g):
            gslab, ts = divmod(g, NSUB)
            h, hc = divmod(gslab, HC)
            e.wait_ge(kb.ve.h, ve0 + g + 1)
            t_lo = h * TH + ts * SUB
            S[g % 2].tag(e.dma_start(
                out=h_s.ap()[hc * 128:(hc + 1) * 128, t_lo:t_lo + SUB],
                in_=hb[g % 2][:, :]), 16)

        def sca(e):
            for g in range(NG):
                e.wait_ge(kb.pe.h, pe_t(g))
                if g >= 2:
                    e.wait_ge(kb.ve.h, ve0 + g - 1)
                e.activation(t1[g % 2][:], pG[g % 2][:], AF.Silu, bias=0.0,
                             scale=INV127)
                kb.se.tag(e.drain())
                if g >= 1:
                    emit_store(e, g - 1)
            emit_store(e, NG - 1)
        block.scalar(sca)

        def vec(e):
            for g in range(NG):
                e.wait_ge(kb.se.h, se0 + g + 1)
                if g >= 2:
                    j = (g - 2) % 2
                    e.wait_ge(S[j].h, s0[j] + 16 * ((g - 2) // 2 + 1))
                e.scalar_tensor_tensor(hb[g % 2][:], pU[g % 2][:], INV127,
                                       t1[g % 2][:], AL.mult, AL.mult)
                kb.ve.tag(e.drain())
        block.vector(vec)

        kb.barrier(block)


def gate_phase(kb, block, bufs, wg1, wg2, xn_s, g_s):
    nc, T, NT = kb.nc, kb.T, kb.NT
    with (
        nc.sbuf_tensor(_u("gt_w1"), [128, NCH, 128], BF16) as w1t,
        nc.sbuf_tensor(_u("gt_w2"), [128, D], BF16) as w2t,
        nc.sbuf_tensor(_u("gt_mv0"), [128, NCH, TT], BF16) as mv0,
        nc.sbuf_tensor(_u("gt_mv1"), [128, NCH, TT], BF16) as mv1,
        nc.sbuf_tensor(_u("gt_g1"), [128, TT], BF16) as g1b,
        nc.sbuf_tensor(_u("gt_ob"), [128, NCH, TT], F32) as ob,
        nc.psum_tensor(_u("gt_p1"), [128, TT], F32) as p1,
        nc.psum_tensor(_u("gt_p20"), [128, TT], F32) as p20,
        nc.psum_tensor(_u("gt_p21"), [128, TT], F32) as p21,
    ):
        mv = [mv0, mv1]
        p2 = [p20, p21]
        L = [kb.ldh, kb.ldh2]
        S = [kb.sth, kb.sth2]
        ve0, pe0, se0, lds0 = kb.ve.v, kb.pe.v, kb.se.v, kb.lds.v
        l0 = [L[0].v, L[1].v]
        s0 = [S[0].v, S[1].v]

        def gld(e):
            kb.lds.tag(e.dma_start(
                out=w1t.ap(),
                in_=wg1.ap().rearrange("(c p) f -> p c f", p=128)), 16)
            kb.lds.tag(e.dma_start(out=w2t.ap(), in_=wg2.ap()), 16)
        block.gpsimd(gld)

        def pe_g1(i):
            return pe0 + i * 9 + 1

        def pe_g2(i, dc):
            return pe0 + i * 9 + 1 + dc + 1

        def se_ev(i):
            return se0 + i * 9 + 1

        def se_sg(i, dc):
            return se0 + i * 9 + 1 + dc + 1

        def ld(e):
            for i in range(NT):
                if i >= 2:
                    e.wait_ge(kb.pe.h, pe_g1(i - 1))
                src = xn_s.ap()[:, i * TT:(i + 1) * TT]
                L[i % 2].tag(e.dma_start(
                    out=mv[i % 2].ap(),
                    in_=src.rearrange("(c p) t -> p c t", p=128)), 16)
        block.sync(ld)

        def ten(e):
            e.wait_ge(kb.lds.h, lds0 + 32)
            for i in range(NT):
                e.wait_ge(L[i % 2].h, l0[i % 2] + 16 * (i // 2 + 1))
                if i >= 1:
                    e.wait_ge(kb.se.h, se_ev(i - 1))
                last = None
                for kc in range(NCH):
                    last = e.matmul(p1[:], w1t[:, kc, :], mv[i % 2][:, kc, :],
                                    start=(kc == 0), stop=(kc == NCH - 1))
                kb.pe.tag(last)
                e.wait_ge(kb.se.h, se_ev(i))
                for dc in range(NCH):
                    if i * 8 + dc >= 2:
                        gprev = i * 8 + dc - 2
                        ip, dp = divmod(gprev, 8)
                        e.wait_ge(kb.se.h, se_sg(ip, dp))
                    kb.pe.tag(e.matmul(p2[(i * 8 + dc) % 2][:],
                                       w2t[:, dc * 128:(dc + 1) * 128],
                                       g1b[:], start=True, stop=True))
        block.tensor(ten)

        def sca(e):
            for i in range(NT):
                e.wait_ge(kb.pe.h, pe_g1(i))
                e.activation(g1b[:], p1[:], AF.Copy, bias=0.0, scale=1.0)
                kb.se.tag(e.drain())
                for dc in range(NCH):
                    e.wait_ge(kb.pe.h, pe_g2(i, dc))
                    if i > 0 and dc == 0:
                        j = (i - 1) % 2
                        e.wait_ge(S[j].h, s0[j] + 16 * ((i - 1) // 2 + 1))
                    e.activation(ob[:, dc, :], p2[(i * 8 + dc) % 2][:],
                                 AF.Sigmoid, bias=0.0, scale=1.0)
                    kb.se.tag(e.drain())
                dst = g_s.ap()[:, i * TT:(i + 1) * TT]
                dst = dst.rearrange("(c p) t -> p c t", p=128)
                S[i % 2].tag(e.dma_start(out=dst, in_=ob.ap()), 16)
        block.scalar(sca)

        kb.barrier(block)


def decay_prep_phase(kb, block, bufs, f_s, b_s, c_s):
    nc, T = kb.nc, kb.T
    NC = T // 128
    S32 = T // 32
    with (
        nc.sbuf_tensor(_u("dp_f0"), [128, T], F32) as f0,
        nc.sbuf_tensor(_u("dp_f1"), [128, T], F32) as f1,
        nc.sbuf_tensor(_u("dp_b1"), [128, T], F32) as b1,
        nc.sbuf_tensor(_u("dp_b2"), [128, T // 2], F32) as b2,
        nc.sbuf_tensor(_u("dp_ub"), [128, T], BF16) as ub,
        nc.sbuf_tensor(_u("dp_ub2"), [128, T], BF16) as ub2,
        nc.sbuf_tensor(_u("dp_ss"), [128, S32], F32) as ss,
        nc.sbuf_tensor(_u("dp_ss2"), [128, S32], F32) as ss2,
        nc.sbuf_tensor(_u("dp_cc"), [128, S32], F32) as cc,
    ):
        fb = [f0, f1]
        L = [kb.ldh, kb.ldh2]
        S = [kb.sth, kb.sth2]
        ve0, pe0, se0, ge0 = kb.ve.v, kb.pe.v, kb.se.v, kb.ge.v
        lds0 = kb.lds.v
        l0 = [L[0].v, L[1].v]
        s0 = [S[0].v, S[1].v]

        def st_bnd(dc):
            return s0[dc % 2] + 16 * (dc // 2 + 1)

        def V(t):
            return t.ap().rearrange("p (s c) -> p s c", c=32)

        def V4(t):
            return t.ap().rearrange("p (n q c) -> p n q c", q=4, c=32)

        def ld(e):
            for dc in range(NCH):
                if dc >= 2:
                    e.wait_ge(L[dc % 2].h,
                              l0[dc % 2] + 16 * ((dc - 2) // 2 + 1))
                    e.wait_ge(kb.se.h, se0 + dc - 1)
                    e.wait_ge(kb.ve.h, ve0 + 3 * (dc - 2) + 1)
                L[dc % 2].tag(e.dma_start(
                    out=fb[dc % 2].ap(),
                    in_=f_s.ap()[dc * 128:(dc + 1) * 128, :]), 16)
        block.sync(ld)

        def sca(e):
            for dc in range(NCH):
                e.wait_ge(L[dc % 2].h, l0[dc % 2] + 16 * (dc // 2 + 1))
                if dc >= 1:
                    e.wait_ge(S[(dc - 1) % 2].h, st_bnd(dc - 1))
                e.activation(b1[:], fb[dc % 2][:], AF.Ln, bias=0.0, scale=1.0)
                kb.se.tag(e.drain())
                e.wait_ge(kb.ve.h, ve0 + 3 * dc + 2)
                S[dc % 2].tag(e.dma_start(
                    out=b_s.ap()[dc * 128:(dc + 1) * 128, :], in_=b1[:]), 16)
        block.scalar(sca)

        def gpc(e):
            for dc in range(NCH):
                if dc >= 1:
                    e.wait_ge(kb.lds.h, lds0 + 16 * dc)
                e.wait_ge(kb.ve.h, ve0 + 3 * dc + 3)
                kb.lds.tag(e.dma_start(
                    out=c_s.ap()[dc * 128:(dc + 1) * 128, :], in_=ub[:]), 16)
            e.wait_ge(kb.lds.h, kb.lds.v)
        block.gpsimd(gpc)

        def hs(e, src, dst, k, op):
            e.tensor_copy(V(dst)[:, :, 0:k], V(src)[:, :, 0:k])
            e.drain()
            e.tensor_tensor(V(dst)[:, :, k:32], V(src)[:, :, k:32],
                            V(src)[:, :, 0:32 - k], op)
            e.drain()

        def vec(e):
            T2 = T // 2
            S2 = T2 // 32
            NC2 = NC // 2

            def Vh(t, h):
                return t.ap()[:, h * T2:(h + 1) * T2].rearrange(
                    "p (s c) -> p s c", c=32)

            def V2(t):
                return t.ap().rearrange("p (s c) -> p s c", c=32)

            def hsh(e, sv, dv, k, op):
                e.tensor_copy(dv[:, :, 0:k], sv[:, :, 0:k])
                e.drain()
                e.tensor_tensor(dv[:, :, k:32], sv[:, :, k:32],
                                sv[:, :, 0:32 - k], op)
                e.drain()

            for dc in range(NCH):
                e.wait_ge(L[dc % 2].h, l0[dc % 2] + 16 * (dc // 2 + 1))
                if dc >= 1:
                    e.wait_ge(kb.lds.h, lds0 + 16 * dc)
                e.tensor_single_scalar(ub[:], fb[dc % 2][:], 0.5, AL.is_equal)
                e.drain()
                e.tensor_scalar(ub[:], ub[:], 2.0, -1.0, AL.mult, AL.add)
                kb.ve.tag(e.drain())
                e.wait_ge(kb.se.h, se0 + dc + 1)
                if dc >= 1:
                    e.wait_ge(S[(dc - 1) % 2].h, st_bnd(dc - 1))
                for h in range(2):
                    bh = Vh(b1, h)
                    b2v = V2(b2)
                    ping = [bh, b2v]
                    for k in (1, 2, 4, 8, 16):
                        hsh(e, ping[0], ping[1], k, AL.add)
                        ping = [ping[1], ping[0]]
                    e.tensor_copy(ss.ap()[:, 0:S2].unsqueeze(2),
                                  b2v[:, :, 31:32])
                    e.drain()
                    sv = ss.ap()[:, 0:S2].rearrange("p (n q) -> p n q", q=4)
                    s2v = ss2.ap()[:, 0:S2].rearrange("p (n q) -> p n q", q=4)
                    e.tensor_copy(s2v[:, :, 0:1], sv[:, :, 0:1])
                    e.drain()
                    e.tensor_tensor(s2v[:, :, 1:4], sv[:, :, 1:4],
                                    sv[:, :, 0:3], AL.add)
                    e.drain()
                    e.tensor_copy(sv[:, :, 0:2], s2v[:, :, 0:2])
                    e.drain()
                    e.tensor_tensor(sv[:, :, 2:4], s2v[:, :, 2:4],
                                    s2v[:, :, 0:2], AL.add)
                    e.drain()
                    b14 = b1.ap()[:, h * T2:(h + 1) * T2].rearrange(
                        "p (n q c) -> p n q c", q=4, c=32)
                    b24 = b2.ap().rearrange("p (n q c) -> p n q c", q=4, c=32)
                    e.tensor_copy(b14[:, :, 0, :], b24[:, :, 0, :])
                    e.drain()
                    e.tensor_tensor(b14[:, :, 1:4, :], b24[:, :, 1:4, :],
                                    sv[:, :, 0:3].unsqueeze(3).to_broadcast(
                                        [128, NC2, 3, 32]), AL.add)
                    e.drain()
                kb.ve.tag(e.drain())
                ping = [ub, ub2]
                for k in (1, 2, 4, 8, 16):
                    hsh(e, V(ping[0]), V(ping[1]), k, AL.mult)
                    ping = [ping[1], ping[0]]
                e.tensor_copy(ss.ap().unsqueeze(2), V(ub2)[:, :, 31:32])
                e.drain()
                e.tensor_tensor_scan(ss2[:], ss[:], ss[:], 1.0, AL.mult,
                                     AL.bypass)
                e.drain()
                e.memset(cc[:, 0:1], 1.0)
                e.drain()
                e.tensor_copy(cc[:, 1:S32], ss2[:, 0:S32 - 1])
                e.drain()
                e.tensor_tensor(V(ub), V(ub2),
                                cc.ap().unsqueeze(2).to_broadcast(
                                    [128, S32, 32]), AL.mult)
                kb.ve.tag(e.drain())
        block.vector(vec)

        kb.barrier(block)


def gla_phase(kb, block, bufs, q_s, f_s, b_s, c_s, g_s, vtok_s, og_s):
    nc, T = kb.nc, kb.T
    NC = T // 128
    with ExitStack() as _stk:
        def sb(name, shape, dt):
            return _stk.enter_context(nc.sbuf_tensor(_u(name), shape, dt))

        def ps(name, shape, dt):
            return _stk.enter_context(nc.psum_tensor(_u(name), shape, dt))

        Fb = [sb("gl_F0", [128, NCH, 128], F32), sb("gl_F1", [128, NCH, 128], F32)]
        bb = [sb("gl_b0", [128, NCH, 128], F32), sb("gl_b1", [128, NCH, 128], F32)]
        Vb = [sb("gl_V0", [128, NCH, 128], BF16), sb("gl_V1", [128, NCH, 128], BF16)]
        gb = [sb("gl_g0", [128, NCH, 128], F32), sb("gl_g1", [128, NCH, 128], F32)]
        og = [sb("gl_og0", [128, NCH, 128], F32), sb("gl_og1", [128, NCH, 128], F32)]
        cb_ = sb("gl_c", [128, NCH, 128], F32)
        Qb = sb("gl_Q", [128, NCH, 128], BF16)
        EA = sb("gl_EA", [128, NCH, 128], F32)
        d1 = sb("gl_d1", [128, NCH, 128], F32)
        kt = sb("gl_kt", [128, NCH, 128], F32)
        KB = sb("gl_KB", [128, NCH, 128], BF16)
        QP = sb("gl_QP", [128, NCH, 128], BF16)
        QH = sb("gl_QH", [128, NCH, 128], BF16)
        KS = sb("gl_KS", [128, NCH, 128], BF16)
        Ktok = sb("gl_Kt", [128, NCH, 128], BF16)
        SBall = sb("gl_SB", [128, NCH, 128], BF16)
        ptb = sb("gl_pt", [128, NCH, 128], BF16)
        S = sb("gl_S", [128, NCH, 128], F32)
        tri = sb("gl_tri", [128, 32], BF16)
        Acol = sb("gl_Ac", [128, NCH, 4], F32)
        dd = sb("gl_dd", [128, NCH, 3], F32)
        eA4 = sb("gl_eA4", [128, NCH, 4], F32)
        dC = sb("gl_dC", [128, NCH, 1], F32)
        eDD = sb("gl_eDD", [128, NCH, 3], F32)
        eC = sb("gl_eC", [128, NCH, 1], F32)
        dec = [sb("gl_dec0", [128, NCH, 1], F32), sb("gl_dec1", [128, NCH, 1], F32)]
        oP = [ps("gl_oP0", [128, NCH, 128], F32), ps("gl_oP1", [128, NCH, 128], F32)]
        Pp = ps("gl_Pp", [128, NCH, 128], F32)
        SAp = ps("gl_SA", [128, NCH, 128], F32)

        L = [kb.ldh, kb.ldh2]
        ST = [kb.sth, kb.sth2]
        SO = [kb.sts, kb.sts2]
        l0 = [L[0].v, L[1].v]
        st0 = [ST[0].v, ST[1].v]
        so0 = [SO[0].v, SO[1].v]
        ve0, pe0, se0, ge0 = kb.ve.v, kb.pe.v, kb.se.v, kb.ge.v

        veV1 = {n: ve0 + 1 + 5 * n + 1 for n in range(NC)}
        veV2 = {n: ve0 + 1 + 5 * n + 2 for n in range(NC)}
        veEV = {n: ve0 + 1 + 5 * n + 3 for n in range(NC)}
        veSU = {n: ve0 + 1 + 5 * n + 4 for n in range(NC)}
        veOG = {n: ve0 + 1 + 5 * n + 5 for n in range(NC)}
        seS1 = {n: se0 + 2 * n + 1 for n in range(NC)}
        seED = {n: se0 + 2 * n + 2 for n in range(NC)}
        geSB = {n: ge0 + 1 + 7 * n + 1 for n in range(NC)}
        geG1 = {n: ge0 + 1 + 7 * n + 2 for n in range(NC)}
        geKD = {n: ge0 + 1 + 7 * n + 3 for n in range(NC)}
        geR = {(n, i): ge0 + 1 + 7 * n + 3 + i for n in range(NC)
               for i in (1, 2, 3)}
        geKS = {n: ge0 + 1 + 7 * n + 7 for n in range(NC)}
        peS = {}
        pePV = {}
        peSA = {}
        v = pe0
        for n in range(NC):
            for i in range(4):
                v += 1
                peS[(n, i)] = v
            for h in range(8):
                v += 1
                pePV[(n, h)] = v
            v += 1
            peSA[n] = v

        def ld_done(n):
            return l0[n % 2] + 96 * (n // 2 + 1)

        def so_done(n):
            return so0[n % 2] + 16 * (n // 2 + 1)

        def tr_done(n):
            return st0[n % 2] + 128 * (n // 2 + 1)

        def init_v(e):
            e.memset(S.ap(), 0.0)
            e.drain()
            e.memset(ptb.ap(), 0.0)
            kb.ve.tag(e.drain())
        block.vector(init_v)

        def init_g(e):
            for q in range(4):
                e.affine_select(tri[q * 32:(q + 1) * 32, :],
                                bufs["ones1k"][q * 32:(q + 1) * 32, 0:32],
                                [[1, 32]], AL.is_ge, 0.0, base=0,
                                channel_multiplier=-1)
                e.drain()
            kb.ge.tag(e.drain())
        block.gpsimd(init_g)

        def ld(e):
            for n in range(NC):
                par = n % 2
                if n >= 2:
                    e.wait_ge(kb.pe.h, peSA[n - 2])
                    e.wait_ge(kb.ve.h, veOG[n - 2])
                    e.wait_ge(kb.se.h, seS1[n - 2])
                    e.wait_ge(kb.ge.h, geKD[n - 2])
                sl = slice(n * 128, (n + 1) * 128)
                rr = lambda apx: apx.rearrange("(c p) t -> p c t", p=128)
                L[par].tag(e.dma_start(out=Fb[par].ap(),
                                       in_=rr(f_s.ap()[:, sl])), 16)
                L[par].tag(e.dma_start(out=bb[par].ap(),
                                       in_=rr(b_s.ap()[:, sl])), 16)
                L[par].tag(e.dma_start(out=gb[par].ap(),
                                       in_=rr(g_s.ap()[:, sl])), 16)
                vsrc = vtok_s.ap()[sl, :]
                L[par].tag(e.dma_start(
                    out=Vb[par].ap(),
                    in_=vsrc.rearrange("p (c f) -> p c f", f=128)), 16)
                if n >= 1:
                    e.wait_ge(kb.ve.h, veV2[n - 1])
                    e.wait_ge(kb.ge.h, geKD[n - 1])
                L[par].tag(e.dma_start(out=cb_.ap(),
                                       in_=rr(c_s.ap()[:, sl])), 16)
                L[par].tag(e.dma_start(out=Qb.ap(),
                                       in_=rr(q_s.ap()[:, sl])), 16)
                if n >= 1:
                    e.wait_ge(kb.ve.h, veOG[n - 1])
                    dst = og_s.ap()[:, (n - 1) * 128:n * 128]
                    SO[(n - 1) % 2].tag(e.dma_start(
                        out=dst.rearrange("(c p) t -> p c t", p=128),
                        in_=og[(n - 1) % 2].ap()), 16)
            e.wait_ge(kb.ve.h, veOG[NC - 1])
            dst = og_s.ap()[:, (NC - 1) * 128:NC * 128]
            SO[(NC - 1) % 2].tag(e.dma_start(
                out=dst.rearrange("(c p) t -> p c t", p=128),
                in_=og[(NC - 1) % 2].ap()), 16)
        block.sync(ld)

        def sca(e):
            for n in range(NC):
                par = n % 2
                e.wait_ge(kb.ve.h, veV1[n])
                if n >= 1:
                    e.wait_ge(kb.ge.h, geKD[n - 1])
                e.activation(eA4.ap(), Acol.ap(), AF.Exp, bias=0.0, scale=1.0)
                e.drain()
                e.activation(eDD.ap(), dd.ap(), AF.Exp, bias=0.0, scale=1.0)
                e.drain()
                e.activation(eC.ap(), dC.ap(), AF.Exp, bias=0.0, scale=1.0)
                e.drain()
                e.activation(dec[par].ap(), bb[par].ap()[:, :, 127:128],
                             AF.Exp, bias=0.0, scale=1.0)
                kb.se.tag(e.drain())
                assert kb.se.v == seS1[n]
                e.wait_ge(kb.ge.h, geG1[n])
                e.activation(EA.ap(), d1.ap(), AF.Exp, bias=0.0, scale=-1.0)
                e.drain()
                e.activation(d1.ap(), d1.ap(), AF.Exp, bias=0.0, scale=1.0)
                kb.se.tag(e.drain())
                assert kb.se.v == seED[n]
                e.wait_ge(kb.ge.h, geKS[n])
                if n >= 1:
                    e.wait_ge(kb.pe.h, peSA[n - 1])
                for h in range(8):
                    ST[par].tag(e.dma_start(out=Ktok[:, h, :],
                                            in_=KS[:, h, :],
                                            transpose=True), 16)
        block.scalar(sca)

        def vec(e):
            for n in range(NC):
                par = n % 2
                e.wait_ge(L[par].h, ld_done(n))
                if n >= 1:
                    e.wait_ge(kb.se.h, seS1[n - 1])
                    e.wait_ge(kb.ge.h, geG1[n - 1])
                e.memset(Acol[:, :, 0:1], 0.0)
                e.drain()
                e.tensor_copy(
                    Acol.ap()[:, :, 1:4].unsqueeze(3),
                    bb[par].ap().rearrange("p h (q s) -> p h q s",
                                           s=32)[:, :, 0:3, 31:32])
                e.drain()
                e.tensor_sub(dd.ap(), Acol.ap()[:, :, 1:4],
                             Acol.ap()[:, :, 0:3])
                e.drain()
                e.tensor_sub(dC.ap(), bb[par].ap()[:, :, 127:128],
                             Acol.ap()[:, :, 3:4])
                kb.ve.tag(e.drain())
                assert kb.ve.v == veV1[n]
                e.wait_ge(kb.se.h, seED[n])
                if n >= 2:
                    e.wait_ge(SO[par].h, so_done(n - 2))
                tmp = og[par]
                e.tensor_mul(tmp.ap(), cb_.ap(), EA.ap())
                e.drain()
                e.scalar_tensor_tensor(QH.ap(), Qb.ap(), 1.0 / 16.0,
                                       tmp.ap(), AL.mult, AL.mult)
                e.drain()
                e.tensor_mul(
                    tmp.ap().rearrange("p h (q s) -> p h q s", s=32),
                    tmp.ap().rearrange("p h (q s) -> p h q s", s=32),
                    eA4.ap().unsqueeze(3).to_broadcast([128, NCH, 4, 32]))
                e.drain()
                e.scalar_tensor_tensor(QP.ap(), Qb.ap(), 1.0 / 16.0,
                                       tmp.ap(), AL.mult, AL.mult)
                kb.ve.tag(e.drain())
                assert kb.ve.v == veV2[n]
                e.wait_ge(kb.pe.h, peS[(n, 3)])
                for q in range(4):
                    pr = slice(q * 32, (q + 1) * 32)
                    e.tensor_mul(ptb.ap()[pr, :, q * 32:(q + 1) * 32],
                                 Pp.ap()[pr, :, q * 32:(q + 1) * 32],
                                 tri.ap()[pr, :].unsqueeze(1).to_broadcast(
                                     [32, NCH, 32]))
                    e.drain()
                for q in range(3):
                    pr = slice(q * 32, (q + 1) * 32)
                    e.tensor_copy(ptb.ap()[pr, :, (q + 1) * 32:128],
                                  Pp.ap()[pr, :, (q + 1) * 32:128])
                    e.drain()
                kb.ve.tag(e.drain())
                assert kb.ve.v == veEV[n]
                e.wait_ge(kb.pe.h, peSA[n])
                e.tensor_mul(S.ap(), S.ap(),
                             dec[par].ap().to_broadcast([128, NCH, 128]))
                e.drain()
                e.tensor_add(S.ap(), S.ap(), SAp.ap())
                kb.ve.tag(e.drain())
                assert kb.ve.v == veSU[n]
                e.tensor_mul(og[par].ap(), oP[par].ap(), gb[par].ap())
                kb.ve.tag(e.drain())
                assert kb.ve.v == veOG[n]
        block.vector(vec)

        def gp(e):
            for n in range(NC):
                par = n % 2
                e.wait_ge(kb.ve.h, veSU[n - 1] if n >= 1 else ve0 + 1)
                if n >= 1:
                    e.wait_ge(kb.pe.h, pePV[(n - 1, 7)])
                e.tensor_copy(SBall.ap(), S.ap())
                kb.ge.tag(e.drain())
                assert kb.ge.v == geSB[n]
                e.wait_ge(kb.ve.h, veV1[n])
                if n >= 1:
                    e.wait_ge(kb.se.h, seED[n - 1])
                e.tensor_sub(
                    d1.ap().rearrange("p h (q s) -> p h q s", s=32),
                    Acol.ap().unsqueeze(3).to_broadcast([128, NCH, 4, 32]),
                    bb[par].ap().rearrange("p h (q s) -> p h q s", s=32))
                kb.ge.tag(e.drain())
                assert kb.ge.v == geG1[n]
                e.tensor_mul(kt.ap(), Fb[par].ap(), cb_.ap())
                e.drain()
                e.tensor_sub(kt.ap(), cb_.ap(), kt.ap())
                e.drain()
                e.wait_ge(kb.se.h, seED[n])
                if n >= 1:
                    e.wait_ge(kb.pe.h, peS[(n - 1, 3)])
                e.tensor_mul(KB.ap(), kt.ap(), d1.ap())
                kb.ge.tag(e.drain())
                assert kb.ge.v == geKD[n]
                for i in (1, 2, 3):
                    e.wait_ge(kb.pe.h, peS[(n, i - 1)])
                    e.tensor_mul(KB.ap()[:, :, 0:i * 32],
                                 KB.ap()[:, :, 0:i * 32],
                                 eDD.ap()[:, :, i - 1:i].to_broadcast(
                                     [128, NCH, i * 32]))
                    kb.ge.tag(e.drain())
                    assert kb.ge.v == geR[(n, i)]
                e.wait_ge(kb.pe.h, peS[(n, 3)])
                e.tensor_mul(KS.ap(), KB.ap(),
                             eC.ap().to_broadcast([128, NCH, 128]))
                kb.ge.tag(e.drain())
                assert kb.ge.v == geKS[n]
        block.gpsimd(gp)

        def ten(e):
            for n in range(NC):
                par = n % 2
                for i in range(4):
                    if i == 0:
                        e.wait_ge(kb.ve.h, veV2[n])
                        e.wait_ge(kb.ge.h, geKD[n])
                        if n >= 1:
                            e.wait_ge(kb.ve.h, veEV[n - 1])
                    else:
                        e.wait_ge(kb.ge.h, geR[(n, i)])
                    last = None
                    for h in range(8):
                        last = e.matmul(Pp[:, h, i * 32:(i + 1) * 32],
                                        KB[:, h, :],
                                        QH[:, h, i * 32:(i + 1) * 32],
                                        start=True, stop=True)
                    kb.pe.tag(last)
                    assert kb.pe.v == peS[(n, i)]
                e.wait_ge(kb.ve.h, veEV[n])
                e.wait_ge(kb.ge.h, geSB[n])
                if n >= 2:
                    e.wait_ge(kb.ve.h, veOG[n - 2])
                e.wait_ge(L[par].h, ld_done(n))
                for h in range(8):
                    e.matmul(oP[par][:, h, :], SBall[:, h, :], QP[:, h, :],
                             start=True, stop=False)
                    last = e.matmul(oP[par][:, h, :], Vb[par][:, h, :],
                                    ptb[:, h, :], start=False, stop=True)
                    kb.pe.tag(last)
                    assert kb.pe.v == pePV[(n, h)]
                e.wait_ge(ST[par].h, tr_done(n))
                if n >= 1:
                    e.wait_ge(kb.ve.h, veSU[n - 1])
                last = None
                for h in range(8):
                    last = e.matmul(SAp[:, h, :], Ktok[:, h, :],
                                    Vb[par][:, h, :], start=True, stop=True)
                kb.pe.tag(last)
                assert kb.pe.v == peSA[n]
        block.tensor(ten)

        kb.barrier(block)


def gn_phase(kb, block, bufs, src_f32, dst_bf, nw_sb):
    nc, T, NT = kb.nc, kb.T, kb.NT
    ones = bufs["ones128"]
    VCH = [0, 1, 2]
    GCH = [3, 4, 5, 6, 7]
    with (
        nc.sbuf_tensor(_u("gn_y0"), [128, NCH, TT], F32) as y0,
        nc.sbuf_tensor(_u("gn_y1"), [128, NCH, TT], F32) as y1,
        nc.sbuf_tensor(_u("gn_yb"), [128, NCH, TT], BF16) as ybf,
        nc.sbuf_tensor(_u("gn_ys"), [128, NCH, TT], BF16) as ysq,
        nc.sbuf_tensor(_u("gn_mu"), [128, TT], F32) as mu,
        nc.sbuf_tensor(_u("gn_es"), [128, TT], F32) as es,
        nc.sbuf_tensor(_u("gn_rstd"), [128, TT], F32) as rstd,
        nc.sbuf_tensor(_u("gn_ob"), [128, NCH, TT], BF16) as obf,
        nc.psum_tensor(_u("gn_psA"), [128, TT], F32) as psA,
        nc.psum_tensor(_u("gn_psB"), [128, TT], F32) as psB,
    ):
        yb = [y0, y1]
        L = [kb.ldh, kb.ldh2]
        S = [kb.sth, kb.sth2]
        ve0, pe0, se0, ge0 = kb.ve.v, kb.pe.v, kb.se.v, kb.ge.v
        l0 = [L[0].v, L[1].v]
        s0 = [S[0].v, S[1].v]

        def ld(e):
            for i in range(NT):
                if i >= 2:
                    e.wait_ge(kb.ve.h, ve0 + 4 * (i - 1))
                    e.wait_ge(kb.ge.h, ge0 + (i - 1))
                src = src_f32.ap()[:, i * TT:(i + 1) * TT]
                L[i % 2].tag(e.dma_start(
                    out=yb[i % 2].ap(),
                    in_=src.rearrange("(c p) t -> p c t", p=128)), 16)
        block.sync(ld)

        def vec(e):
            for i in range(NT):
                e.wait_ge(L[i % 2].h, l0[i % 2] + 16 * (i // 2 + 1))
                e.tensor_copy(ybf.ap(), yb[i % 2].ap())
                e.drain()
                e.tensor_tensor(ysq.ap(), ybf.ap(), ybf.ap(), AL.mult)
                kb.ve.tag(e.drain())
                e.wait_ge(kb.pe.h, pe0 + i + 1)
                if i > 0:
                    e.wait_ge(kb.ge.h, ge0 + i)
                e.tensor_scalar(mu[:], psA[:], 1.0 / D, None, AL.mult)
                e.drain()
                e.tensor_scalar(es[:], psB[:], 1.0 / D, None, AL.mult)
                e.drain()
                e.tensor_tensor(rstd[:], mu[:], mu[:], AL.mult)
                e.drain()
                e.tensor_sub(es[:], es[:], rstd[:])
                kb.ve.tag(e.drain())
                e.wait_ge(kb.se.h, se0 + i + 1)
                e.reciprocal(rstd[:], es[:])
                kb.ve.tag(e.drain())
                if i > 0:
                    j = (i - 1) % 2
                    e.wait_ge(S[j].h, s0[j] + 16 * ((i - 1) // 2 + 1))
                for dc in VCH:
                    y_ = yb[i % 2]
                    e.tensor_sub(y_[:, dc, :], y_[:, dc, :], mu[:])
                    e.drain()
                    e.tensor_mul(y_[:, dc, :], y_[:, dc, :], rstd[:])
                    e.drain()
                    e.tensor_mul(obf[:, dc, :], y_[:, dc, :],
                                 nw_sb[:, dc:dc + 1].to_broadcast([128, TT]))
                    e.drain()
                kb.ve.tag(e.drain())
        block.vector(vec)

        def sca(e):
            for i in range(NT):
                e.wait_ge(kb.ve.h, ve0 + 4 * i + 2)
                e.activation(es[:], es[:], AF.Sqrt, bias=bufs["ceps"][:],
                             scale=1.0)
                kb.se.tag(e.drain())
                e.wait_ge(kb.ve.h, ve0 + 4 * i + 4)
                e.wait_ge(kb.ge.h, ge0 + i + 1)
                dst = dst_bf.ap()[:, i * TT:(i + 1) * TT]
                S[i % 2].tag(e.dma_start(
                    out=dst.rearrange("(c p) t -> p c t", p=128),
                    in_=obf.ap()), 16)
        block.scalar(sca)

        def ten(e):
            for i in range(NT):
                e.wait_ge(kb.ve.h, ve0 + 4 * i + 1)
                for dc in range(NCH):
                    e.matmul(psA[:], ones[:, 0:128], ybf[:, dc, :],
                             start=(dc == 0), stop=(dc == NCH - 1))
                last = None
                for dc in range(NCH):
                    last = e.matmul(psB[:], ones[:, 0:128], ysq[:, dc, :],
                                    start=(dc == 0), stop=(dc == NCH - 1))
                kb.pe.tag(last)
        block.tensor(ten)

        def gp(e):
            for i in range(NT):
                e.wait_ge(kb.ve.h, ve0 + 4 * i + 3)
                for dc in GCH:
                    y_ = yb[i % 2]
                    e.tensor_sub(y_[:, dc, :], y_[:, dc, :], mu[:])
                    e.drain()
                    e.tensor_mul(y_[:, dc, :], y_[:, dc, :], rstd[:])
                    e.drain()
                    e.tensor_mul(obf[:, dc, :], y_[:, dc, :],
                                 nw_sb[:, dc:dc + 1].to_broadcast([128, TT]))
                    e.drain()
                kb.ge.tag(e.drain())
        block.gpsimd(gp)

        kb.barrier(block)


def quant_weights_phase(kb, block, bufs, w1, w2, w1qT_s, w2qT_s, sw1, sw2,
                        cc_in, cc_out, gcol):
    nc = kb.nc
    jobs = ([("w1", r) for r in range(FF1 // 128)] +
            [("w2", r) for r in range(D // 128)])
    units = []
    for j, (w, r) in enumerate(jobs):
        for sub in range(1 if w == "w1" else 4):
            units.append((j, w, r, sub))

    def units_upto(j):
        if j < 32:
            return j + 1
        return 32 + 4 * (j - 31)

    with (
        nc.sbuf_tensor(_u("qw_a0"), [128, FF1], F32) as a0,
        nc.sbuf_tensor(_u("qw_a1"), [128, FF1], F32) as a1,
        nc.sbuf_tensor(_u("qw_q0"), [128, FF1], BF16) as q0,
        nc.sbuf_tensor(_u("qw_q1"), [128, FF1], BF16) as q1,
        nc.sbuf_tensor(_u("qw_tq"), [128, 2, 8, 128], BF16) as tq,
        nc.sbuf_tensor(_u("qw_rc"), [128, 1], F32) as rcol,
        nc.sbuf_tensor(_u("qw_g1"), [1, 1], F32) as gm1,
    ):
        ab = [a0, a1]
        qb = [q0, q1]
        L = [kb.ldh, kb.ldh2]
        S = [kb.sth, kb.sth2]
        TS = [kb.sts, kb.sts2]
        l0 = [L[0].v, L[1].v]
        s0 = [S[0].v, S[1].v]
        t0 = [TS[0].v, TS[1].v]
        ve0, ge0, cc0 = kb.ve.v, kb.ge.v, kb.cc.v

        def srcof(w):
            return (w1 if w == "w1" else w2).ap()

        def emit_unit(e, u):
            j, w, r, sub = units[u]
            e.wait_ge(kb.ve.h, ve0 + 2 * j + 2)
            if u >= 2:
                e.wait_ge(S[u % 2].h, s0[u % 2] + 16 * ((u - 2) // 2 + 1))
            for c in range(8):
                col = (8 * sub + c) * 128
                TS[u % 2].tag(e.dma_start(
                    out=tq[:, u % 2, c, :],
                    in_=qb[j % 2][:, col:col + 128], transpose=True), 16)

        def ld(e):
            eu = 0
            for j, (w, r) in enumerate(jobs):
                if j >= 2:
                    while eu < units_upto(j - 2):
                        if eu % 2 == 0:
                            emit_unit(e, eu)
                        eu += 1
                    e.wait_ge(kb.ve.h, ve0 + (j - 2) * 2 + 2)
                src = srcof(w)
                C = src.shape[1]
                L[j % 2].tag(e.dma_start(out=ab[j % 2][:, 0:C],
                                         in_=src[r * 128:(r + 1) * 128, :]),
                             16)
            while eu < len(units):
                if eu % 2 == 0:
                    emit_unit(e, eu)
                eu += 1
        block.sync(ld)

        def do_store(e, u):
            j, w, r, sub = units[u]
            e.wait_ge(TS[u % 2].h, t0[u % 2] + 128 * (u // 2 + 1))
            if w == "w1":
                dst = w1qT_s.ap()[:, r * 128:(r + 1) * 128]
            else:
                dst = w2qT_s.ap()[sub * 1024:(sub + 1) * 1024,
                                  r * 128:(r + 1) * 128]
            S[u % 2].tag(e.dma_start(
                out=dst.rearrange("(c p) f -> p c f", p=128),
                in_=tq[:, u % 2, :, :]), 16)

        def st(e):
            assert len(units) % 2 == 0
            for k in range(len(units) // 2):
                emit_unit(e, 2 * k + 1)
                do_store(e, 2 * k)
                do_store(e, 2 * k + 1)
        block.scalar(st)

        def vec(e):
            for j, (w, r) in enumerate(jobs):
                C = srcof(w).shape[1]
                sw = sw1 if w == "w1" else sw2
                e.wait_ge(L[j % 2].h, l0[j % 2] + 16 * (j // 2 + 1))
                a = ab[j % 2]
                e.reduce_max(rcol[:], a[:, 0:C], axis=AX.X,
                             apply_absolute_value=True)
                e.drain()
                e.tensor_scalar(rcol[:], rcol[:], EPS_Q, 1.0 / 127.0, AL.max,
                                AL.mult)
                e.drain()
                e.tensor_copy(sw[:, r:r + 1], rcol[:])
                e.drain()
                e.reciprocal(rcol[:], rcol[:])
                e.drain()
                e.tensor_scalar(a[:, 0:C], a[:, 0:C], rcol[:], None, AL.mult)
                kb.ve.tag(e.drain())
                e.tensor_scalar(a[:, 0:C], a[:, 0:C], RMAGIC, -RMAGIC,
                                AL.add, AL.add)
                e.drain()
                e.tensor_scalar(a[:, 0:C], a[:, 0:C], -128.0, 127.0, AL.max,
                                AL.min)
                e.drain()
                if j >= 2:
                    U = units_upto(j - 2)
                    e.wait_ge(TS[0].h, t0[0] + 128 * ((U + 1) // 2))
                    e.wait_ge(TS[1].h, t0[1] + 128 * (U // 2))
                e.tensor_copy(qb[j % 2][:, 0:C], a[:, 0:C])
                kb.ve.tag(e.drain())
        block.vector(vec)

        def gp(e):
            kb.lds.tag(e.dma_start(out=cc_in[:, :], in_=bufs["mx1"][:, :]), 16)
            e.wait_ge(kb.lds.h, kb.lds.v)
            kb.cc.tag(e.collective_compute(
                "AllReduce", AL.max,
                replica_groups=[list(range(kb.ncores))],
                ins=[cc_in.ap().opt()],
                outs=[cc_out.ap().opt()]))
            e.wait_ge(kb.cc.h, cc0 + 1)
            kb.lds.tag(e.dma_start(out=gcol.ap(),
                                   in_=cc_out.ap().to_broadcast([128, 1])), 16)
            e.wait_ge(kb.lds.h, kb.lds.v)
            e.tensor_scalar(gcol.ap(), gcol.ap(), EPS_Q, None, AL.max)
            kb.ge.tag(e.drain())
        block.gpsimd(gp)

        kb.barrier(block)


def mm1_phase(kb, block, bufs, xn32_s, w1qT_s, y_s, sw1, b1col, gcol,
              cc_in2, cc_out2, g2col):
    nc, T = kb.nc, kb.T
    TH = T // 2
    SUB = min(256, TH)
    NSUB = TH // SUB
    FC = FF1 // 128
    NGH = FC * NSUB
    with ExitStack() as _stk:
        def sb(name, shape, dt):
            return _stk.enter_context(nc.sbuf_tensor(_u(name), shape, dt))
        xq = sb("m1_xq", [128, NCH, TH], BF16)
        xl = [sb("m1_x0", [128, NCH, SUB], F32),
              sb("m1_x1", [128, NCH, SUB], F32)]
        mb_ = sb("m1_m", [128, NCH, SUB], F32)
        wt = [sb("m1_w0", [128, NCH, 128], BF16),
              sb("m1_w1", [128, NCH, 128], BF16)]
        yb = sb("m1_y", [128, 4, SUB], F32)
        esc = sb("m1_es", [128, FC], F32)
        ebi = sb("m1_eb", [128, FC], F32)
        rin = sb("m1_ri", [128, 1], F32)
        hmx = sb("m1_hm", [128, 1], F32)
        hmt = sb("m1_ht", [128, 1], F32)
        hm1 = sb("m1_h1", [1, 1], F32)
        tcol = sb("m1_tc", [128, FC], F32)
        pY = [_stk.enter_context(nc.psum_tensor(_u("m1_p%d" % i),
                                                [128, SUB], F32))
              for i in range(4)]
        L = [kb.ldh, kb.ldh2]
        S = [kb.sth, kb.sth2]
        SO = [kb.sts, kb.sts2]
        l0 = [L[0].v, L[1].v]
        s0 = [S[0].v, S[1].v]
        so0 = [SO[0].v, SO[1].v]
        ve0, pe0, se0, ge0, cc0 = kb.ve.v, kb.pe.v, kb.se.v, kb.ge.v, kb.cc.v

        vq = ve0 + 2
        gq = ge0 + 1

        def veQmid(j):
            half, tq = divmod(j, NSUB)
            return vq + half * (NSUB + NGH) + tq + 1

        def veMax(g):
            half, gg = divmod(g, NGH)
            return vq + half * (NSUB + NGH) + NSUB + gg + 1

        ve_end = vq + 2 * (NSUB + NGH)


        def prol_v(e):
            e.reciprocal(rin[:], gcol[:])
            e.drain()
            e.tensor_scalar(rin[:], rin[:], 255.0, None, AL.mult)
            e.drain()
            e.tensor_scalar(esc.ap(), sw1.ap(), gcol[:], None, AL.mult)
            e.drain()
            e.tensor_scalar(esc.ap(), esc.ap(), 1.0 / 255.0, None, AL.mult)
            e.drain()
            e.reciprocal(tcol.ap(), esc.ap())
            e.drain()
            e.tensor_tensor(tcol.ap(), b1col.ap(), tcol.ap(), AL.mult)
            e.drain()
            e.tensor_scalar(tcol.ap(), tcol.ap(), RMAGIC, -RMAGIC,
                            AL.add, AL.add)
            kb.ve.tag(e.drain())
            e.tensor_tensor(ebi.ap(), tcol.ap(), esc.ap(), AL.mult)
            kb.ve.tag(e.drain())
        block.vector(prol_v)

        def prol_g(e):
            kb.ge.tag(e.engine_nop())
        block.gpsimd(prol_g)

        def emit_store(e, g):
            hh, gg = divmod(g, NGH)
            fcp, tqp = divmod(gg, NSUB)
            e.wait_ge(kb.ve.h, veMax(g))
            if g >= 2:
                e.wait_ge(SO[g % 2].h, so0[g % 2] + 16 * ((g - 2) // 2 + 1))
            t_lo = hh * TH + tqp * SUB
            dst = y_s.ap()[fcp * 128:(fcp + 1) * 128, t_lo:t_lo + SUB]
            SO[g % 2].tag(e.dma_start(out=dst, in_=yb[:, g % 4, :]), 16)

        def ld(e):
            for half in range(2):
                for tq in range(NSUB):
                    j = half * NSUB + tq
                    if j >= 2:
                        e.wait_ge(kb.ve.h, veQmid(j - 2))
                    src = xn32_s.ap()[:, half * TH + tq * SUB:
                                      half * TH + (tq + 1) * SUB]
                    L[j % 2].tag(e.dma_start(
                        out=xl[j % 2].ap(),
                        in_=src.rearrange("(c p) t -> p c t", p=128)), 16)
                for fc in range(FC):
                    gw = half * FC + fc
                    if gw >= 2:
                        e.wait_ge(kb.pe.h, pe0 + (gw - 1) * NSUB)
                    S[gw % 2].tag(e.dma_start(
                        out=wt[gw % 2].ap(),
                        in_=w1qT_s.ap()[:, fc * 128:(fc + 1) * 128].rearrange(
                            "(c p) f -> p c f", p=128)), 16)
                    if fc >= 1:
                        for g in range(half * NGH + (fc - 1) * NSUB,
                                       half * NGH + fc * NSUB):
                            emit_store(e, g)
                for g in range(half * NGH + (FC - 1) * NSUB,
                               half * NGH + FC * NSUB):
                    emit_store(e, g)
        block.sync(ld)

        def quant_ops(e, j):
            a = xl[j % 2]
            half, tq = divmod(j, NSUB)
            e.tensor_scalar(a.ap(), a.ap(), rin[:], None, AL.mult)
            e.drain()
            e.tensor_scalar(a.ap(), a.ap(), RMAGIC, -RMAGIC, AL.add, AL.add)
            e.drain()
            e.tensor_scalar(a.ap(), a.ap(), 0.0, 255.0, AL.max, AL.min)
            e.drain()
            e.tensor_copy(xq[:, :, tq * SUB:(tq + 1) * SUB], a.ap())
            e.drain()

        def vecq(e):
            for half in range(2):
                for tq in range(NSUB):
                    j = half * NSUB + tq
                    e.wait_ge(L[j % 2].h, l0[j % 2] + 16 * (j // 2 + 1))
                    if tq == 0 and half == 1:
                        e.wait_ge(kb.pe.h, pe0 + NGH)
                    quant_ops(e, j)
                    kb.ve.tag(e.drain())
                    assert kb.ve.v == veQmid(j)
                for gg in range(NGH):
                    g = half * NGH + gg
                    e.wait_ge(kb.se.h, se0 + g + 1)
                    e.reduce_max(hmt[:], yb[:, g % 4, :], axis=AX.X)
                    e.drain()
                    if g == 0:
                        e.tensor_copy(hmx[:], hmt[:])
                    else:
                        e.tensor_max(hmx[:], hmx[:], hmt[:])
                    kb.ve.tag(e.drain())
                    assert kb.ve.v == veMax(g)
        block.vector(vecq)

        def gpq(e):
            e.wait_ge(kb.ve.h, ve_end)
            e.tensor_reduce(hm1.ap(), hmx.ap(), AX.C, AL.max)
            e.drain()
            kb.lds.tag(e.dma_start(out=cc_in2[:, :], in_=hm1[:, :]), 16)
            e.wait_ge(kb.lds.h, kb.lds.v)
            kb.cc.tag(e.collective_compute(
                "AllReduce", AL.max,
                replica_groups=[list(range(kb.ncores))],
                ins=[cc_in2.ap().opt()],
                outs=[cc_out2.ap().opt()]))
            e.wait_ge(kb.cc.h, cc0 + 1)
            kb.lds.tag(e.dma_start(out=g2col.ap(),
                                   in_=cc_out2.ap().to_broadcast([128, 1])),
                       16)
            e.wait_ge(kb.lds.h, kb.lds.v)
            e.tensor_scalar(g2col.ap(), g2col.ap(), EPS_Q, None, AL.max)
            kb.ge.tag(e.drain())
        block.gpsimd(gpq)

        def wst_done(gw):
            return s0[gw % 2] + 16 * (gw // 2 + 1)

        def sca(e):
            e.wait_ge(kb.ve.h, vq)
            g = 0
            for half in range(2):
                for fc in range(FC):
                    for tq in range(NSUB):
                        e.wait_ge(kb.pe.h, pe0 + g + 1)
                        if g >= 4:
                            jj = (g - 4) % 2
                            e.wait_ge(SO[jj].h,
                                      so0[jj] + 16 * ((g - 4) // 2 + 1))
                        e.activation(yb[:, g % 4, :], pY[g % 4][:],
                                     AF.Identity, bias=ebi[:, fc:fc + 1],
                                     scale=esc[:, fc:fc + 1])
                        kb.se.tag(e.drain())
                        g += 1
        block.scalar(sca)

        def ten(e):
            g = 0
            for half in range(2):
                for fc in range(FC):
                    gw = half * FC + fc
                    e.wait_ge(S[gw % 2].h, wst_done(gw))
                    for tq in range(NSUB):
                        j = half * NSUB + tq
                        e.wait_ge(kb.ve.h, veQmid(j))
                        if g >= 4:
                            e.wait_ge(kb.se.h, se0 + g - 3)
                        last = None
                        for dc in range(NCH):
                            last = e.matmul(
                                pY[g % 4][:], wt[gw % 2][:, dc, :],
                                xq[:, dc, tq * SUB:(tq + 1) * SUB],
                                start=(dc == 0), stop=(dc == NCH - 1))
                        kb.pe.tag(last)
                        g += 1
        block.tensor(ten)

        kb.barrier(block)


def transpose_dram_phase(kb, block, bufs, src, dst):
    nc = kb.nc
    R, C = src.ap().shape
    jobs = [(r, c) for c in range(C // 128) for r in range(R // 128)]
    assert len(jobs) % 2 == 0
    with (
        nc.sbuf_tensor(_u("tw_bb"), [128, 8, 128], BF16) as bb,
    ):
        L = [kb.ldh, kb.ldh2]
        S = [kb.sth, kb.sth2]
        l0 = [L[0].v, L[1].v]
        s0 = [S[0].v, S[1].v]

        def ld(e):
            for j, (r, c) in enumerate(jobs):
                if j >= 2:
                    e.wait_ge(L[j % 2].h,
                              l0[j % 2] + 16 * ((j - 2) // 2 + 1))
                if j >= 8:
                    p = (j - 8) // 2
                    e.wait_ge(S[p % 2].h, s0[p % 2] + 16 * (p // 2 + 1))
                L[j % 2].tag(e.dma_start(
                    out=bb[:, j % 8, :],
                    in_=src.ap()[r * 128:(r + 1) * 128, c * 128:(c + 1) * 128],
                    transpose=True), 16)
        block.sync(ld)

        def st(e):
            for p in range(len(jobs) // 2):
                r, c = jobs[2 * p]
                if p >= 2:
                    e.wait_ge(S[p % 2].h,
                              s0[p % 2] + 16 * ((p - 2) // 2 + 1))
                e.wait_ge(L[0].h, l0[0] + 16 * (p + 1))
                e.wait_ge(L[1].h, l0[1] + 16 * (p + 1))
                sl = (2 * p) % 8
                S[p % 2].tag(e.dma_start(
                    out=dst.ap()[c * 128:(c + 1) * 128,
                                 r * 128:(r + 2) * 128],
                    in_=bb[:, sl:sl + 2, :]), 16)
        block.scalar(st)

        kb.barrier(block)


def hq_quant_phase(kb, block, bufs, y_s, hq_s, sw2, b2col, g2col, esc2, ebi2):
    nc, T = kb.nc, kb.T
    HW = T // 2
    jobs = [(r, hf) for r in range(FF1 // 128) for hf in range(2)]
    NJ = len(jobs)
    with (
        nc.sbuf_tensor(_u("hq_a0"), [128, HW], F32) as a0,
        nc.sbuf_tensor(_u("hq_a1"), [128, HW], F32) as a1,
        nc.sbuf_tensor(_u("hq_q0"), [128, HW], BF16) as q0,
        nc.sbuf_tensor(_u("hq_q1"), [128, HW], BF16) as q1,
        nc.sbuf_tensor(_u("hq_r2"), [128, 1], F32) as r2,
        nc.sbuf_tensor(_u("hq_tc"), [128, NCH], F32) as tcol,
    ):
        ab = [a0, a1]
        qb = [q0, q1]
        L = [kb.ldh, kb.ldh2]
        S = [kb.sth, kb.sth2]
        l0 = [L[0].v, L[1].v]
        s0 = [S[0].v, S[1].v]
        ve0, ge0 = kb.ve.v, kb.ge.v

        def prol_v(e):
            e.reciprocal(r2[:], g2col[:])
            e.drain()
            e.tensor_scalar(r2[:], r2[:], 255.0, None, AL.mult)
            e.drain()
            e.tensor_scalar(esc2.ap(), sw2.ap(), g2col[:], None, AL.mult)
            e.drain()
            e.tensor_scalar(esc2.ap(), esc2.ap(), 1.0 / 255.0, None, AL.mult)
            e.drain()
            e.reciprocal(tcol.ap(), esc2.ap())
            e.drain()
            e.tensor_tensor(tcol.ap(), b2col.ap(), tcol.ap(), AL.mult)
            e.drain()
            e.tensor_scalar(tcol.ap(), tcol.ap(), RMAGIC, -RMAGIC,
                            AL.add, AL.add)
            kb.ve.tag(e.drain())
            e.tensor_tensor(ebi2.ap(), tcol.ap(), esc2.ap(), AL.mult)
            kb.ve.tag(e.drain())
        block.vector(prol_v)

        def prol_g(e):
            kb.ge.tag(e.engine_nop())
        block.gpsimd(prol_g)

        vq = ve0 + 2
        gq = ge0 + 1

        def jdone(j):
            return kb.ve.h, vq + j + 1

        def ld(e):
            for j, (r, hf) in enumerate(jobs):
                if j >= 2:
                    hh, vv = jdone(j - 2)
                    e.wait_ge(hh, vv)
                L[j % 2].tag(e.dma_start(
                    out=ab[j % 2][:, :],
                    in_=y_s.ap()[r * 128:(r + 1) * 128,
                                 hf * HW:(hf + 1) * HW]), 16)
        block.sync(ld)

        def quant_slab(e, j, tag):
            a = ab[j % 2]
            e.wait_ge(L[j % 2].h, l0[j % 2] + 16 * (j // 2 + 1))
            e.tensor_scalar(a[:], a[:], r2[:], None, AL.mult)
            e.drain()
            e.tensor_scalar(a[:], a[:], RMAGIC, -RMAGIC, AL.add, AL.add)
            e.drain()
            e.tensor_scalar(a[:], a[:], 0.0, 255.0, AL.max, AL.min)
            e.drain()
            if j >= 2:
                jj = (j - 2) % 2
                e.wait_ge(S[jj].h, s0[jj] + 16 * ((j - 2) // 2 + 1))
            e.tensor_copy(qb[j % 2][:], a[:])
            e.drain()
            tag()

        def vecq(e):
            for j in range(NJ):
                quant_slab(e, j, lambda: kb.ve.tag(e.drain()))
        block.vector(vecq)

        def st(e):
            for j, (r, hf) in enumerate(jobs):
                hh, vv = jdone(j)
                e.wait_ge(hh, vv)
                S[j % 2].tag(e.dma_start(
                    out=hq_s.ap()[r * 128:(r + 1) * 128,
                                  hf * HW:(hf + 1) * HW],
                    in_=qb[j % 2][:, :]), 16)
        block.scalar(st)

        kb.barrier(block)



def build_nc(T=4096, ncores=8, upto=99):
    nc = bass.Bass("TRN2", target_bir_lowering=False, debug=False)
    kb = Kb(nc, T, ncores)

    xT = nc.dram_tensor("xT", [D, T], F32, kind="ExternalInput")
    l0_wqT = nc.dram_tensor("l0_wqT", [D, D], F32, kind="ExternalInput")
    l0_wkT = nc.dram_tensor("l0_wkT", [D, D], F32, kind="ExternalInput")
    l0_wvT = nc.dram_tensor("l0_wvT", [D, D], F32, kind="ExternalInput")
    l0_woT = nc.dram_tensor("l0_woT", [D, D], F32, kind="ExternalInput")
    outT = nc.dram_tensor("outT", [D, T], F32, kind="ExternalOutput")

    l0_wgT = nc.dram_tensor("l0_wgT", [D, FF0], I32, kind="ExternalInput")
    l0_wuT = nc.dram_tensor("l0_wuT", [D, FF0], I32, kind="ExternalInput")
    l0_wdT = nc.dram_tensor("l0_wdT", [FF0, D], I32, kind="ExternalInput")
    l1_wq = nc.dram_tensor("l1_wq", [D, D], F32, kind="ExternalInput")
    l1_wk = nc.dram_tensor("l1_wk", [D, D], F32, kind="ExternalInput")
    l1_wv = nc.dram_tensor("l1_wv", [D, D], F32, kind="ExternalInput")
    l1_wo = nc.dram_tensor("l1_wo", [D, D], F32, kind="ExternalInput")
    l1_wg1 = nc.dram_tensor("l1_wg1", [D, D1], F32, kind="ExternalInput")
    l1_wg2 = nc.dram_tensor("l1_wg2", [D1, D], F32, kind="ExternalInput")
    l1_nw = nc.dram_tensor("l1_nw", [128, NCH], F32, kind="ExternalInput")
    l1_w1 = nc.dram_tensor("l1_w1", [FF1, D], F32, kind="ExternalInput")
    l1_b1 = nc.dram_tensor("l1_b1", [128, FF1 // 128], F32, kind="ExternalInput")
    l1_w2 = nc.dram_tensor("l1_w2", [D, FF1], F32, kind="ExternalInput")
    l1_b2 = nc.dram_tensor("l1_b2", [128, NCH], F32, kind="ExternalInput")

    xn_s = nc.dram_tensor("xn_s", [D, T], BF16)
    q_s = nc.dram_tensor("q_s", [D, T], BF16)
    k_s = nc.dram_tensor("k_s", [D, T], BF16)
    v_s = nc.dram_tensor("v_s", [T, 2 * D], BF16)
    o_s = nc.dram_tensor("o_s", [D, T], BF16)
    wg_b = nc.dram_tensor("wg_b", [D, FF0], BF16)
    wu_b = nc.dram_tensor("wu_b", [D, FF0], BF16)
    wd_b = nc.dram_tensor("wd_b", [FF0, D], BF16)
    h_s = nc.dram_tensor("h_s", [FF0, T], BF16)
    f_s = nc.dram_tensor("f_s", [D, T], F32)
    g_s = nc.dram_tensor("g_s", [D, T], F32)
    vtok_s = nc.dram_tensor("vtok_s", [T, D], BF16)
    b_s = nc.dram_tensor("b_s", [D, T], F32)
    c_s = nc.dram_tensor("c_s", [D, T], F32)
    og_s = nc.dram_tensor("og_s", [D, T], F32)
    xn32_s = nc.dram_tensor("xn32_s", [D, T], F32)
    y_s = nc.dram_tensor("y_s", [FF1, T], F32)
    hq_s = nc.dram_tensor("hq_s", [FF1, T], BF16)
    w1qT_s = nc.dram_tensor("w1qT_s", [D, FF1], BF16)
    w2qT_s = nc.dram_tensor("w2qT_s", [FF1, D], BF16)
    cc_in = nc.dram_tensor("cc_in", [1, 1], F32)
    cc_out = nc.dram_tensor("cc_out", [1, 1], F32)
    cc_in2 = nc.dram_tensor("cc_in2", [1, 1], F32)
    cc_out2 = nc.dram_tensor("cc_out2", [1, 1], F32)

    with (
        nc.sbuf_tensor("X", [128, NCH, T], F32) as X,
        nc.sbuf_tensor("ones1k", [128, 1024], BF16) as ones1k,
        nc.sbuf_tensor("ident128", [128, 128], BF16) as ident,
        nc.sbuf_tensor("mx1", [1, 1], F32) as mx1,
        nc.sbuf_tensor("ceps", [128, 1], F32) as ceps,
        nc.sbuf_tensor("nw_sb", [128, NCH], F32) as nw_sb,
        nc.sbuf_tensor("sw1", [128, FF1 // 128], F32) as sw1,
        nc.sbuf_tensor("sw2", [128, NCH], F32) as sw2,
        nc.sbuf_tensor("gcol", [128, 1], F32) as gcol,
        nc.sbuf_tensor("g2col", [128, 1], F32) as g2col,
        nc.sbuf_tensor("b1col", [128, FF1 // 128], F32) as b1col,
        nc.sbuf_tensor("b2col", [128, NCH], F32) as b2col,
        nc.sbuf_tensor("esc2", [128, NCH], F32) as esc2,
        nc.sbuf_tensor("ebi2", [128, NCH], F32) as ebi2,
        nc.Block() as block,
    ):
        ones = ones1k.ap()[:, 0:128]
        bufs = dict(X=X, ones128=ones, ident128=ident, mx1=mx1, ceps=ceps,
                    ones1k=ones1k)

        def ld(e):
            kb.ldh.tag(e.dma_start(
                out=X.ap(),
                in_=xT.ap().rearrange("(c p) t -> p c t", p=128)), 16)
            kb.ldh.tag(e.dma_start(out=nw_sb.ap(), in_=l1_nw.ap()), 16)
            kb.ldh.tag(e.dma_start(out=b1col.ap(), in_=l1_b1.ap()), 16)
            kb.ldh.tag(e.dma_start(out=b2col.ap(), in_=l1_b2.ap()), 16)
        block.sync(ld)

        def init_v(e):
            e.memset(ones1k[:], 1.0)
            e.memset(ceps[:], EPS_LN)
            e.wait_ge(kb.ldh.h, kb.ldh.v)
            kb.ve.tag(e.drain())
        block.vector(init_v)

        def init_g(e):
            e.wait_ge(kb.ve.h, kb.ve.v)
            e.affine_select(ident[:], ones[:, 0:128], [[1, 128]], AL.is_equal,
                            0.0, base=0, channel_multiplier=-1)
            kb.ge.tag(e.drain())
        block.gpsimd(init_g)

        def init_vpad(e):
            e.wait_ge(kb.ve.h, kb.ve.v)
            for i in range(T // 128):
                dst = v_s.ap()[i * 128:(i + 1) * 128, :]
                dst = dst.rearrange("p (h c) -> p h c", c=128)[:, :, 64:128]
                kb.sth.tag(e.dma_start(
                    out=dst,
                    in_=ones1k.ap().rearrange("p (h c) -> p h c", c=64)), 16)
        block.scalar(init_vpad)

        kb.barrier(block)

        ln_pre_phase(kb, block, bufs, dst_bf=xn_s)
        if upto >= 2:
            simple_proj_phase(kb, block, bufs, None, xn_s, None, None,
                              None, None, jobs=[
                                  (l0_wqT, q_s, AF.Copy, 1.0, BF16),
                                  (l0_wkT, k_s, AF.Copy, 1.0, BF16)])
            v_proj_phase(kb, block, bufs, l0_wvT, xn_s, v_s, padded=True)
        if upto >= 3:
            attn0_phase(kb, block, bufs, q_s, k_s, v_s, o_s)
        if upto >= 4:
            proj_postln_phase(kb, block, bufs, o_s, NCH, w_dram=l0_woT)
        if upto >= 5:
            cast_weights_phase(kb, block, bufs, [(l0_wgT, wg_b), (l0_wuT, wu_b),
                                                 (l0_wdT, wd_b)])
            ln_pre_phase(kb, block, bufs, dst_bf=xn_s)
            swiglu_phase(kb, block, bufs, wg_b, wu_b, xn_s, h_s)
            proj_postln_phase(kb, block, bufs, h_s, FF0 // 128,
                              w_bf_dram=wd_b, evac_scale=INV127, tt=256)
        if upto >= 5.1:
            ln_pre_phase(kb, block, bufs, dst_bf=xn_s)
            simple_proj_phase(kb, block, bufs, None, xn_s, None, None,
                              None, None, jobs=[
                                  (l1_wq, q_s, AF.Silu, 1.0, BF16),
                                  (l1_wk, f_s, AF.Sigmoid, -1.0, F32)])
        if upto >= 5.2:
            pass
        if upto >= 5.3:
            v_proj_phase(kb, block, bufs, l1_wv, xn_s, vtok_s, padded=False)
        if upto >= 5.4:
            gate_phase(kb, block, bufs, l1_wg1, l1_wg2, xn_s, g_s)
        if upto >= 5.5:
            decay_prep_phase(kb, block, bufs, f_s, b_s, c_s)
        if upto >= 5.6:
            gla_phase(kb, block, bufs, q_s, f_s, b_s, c_s, g_s, vtok_s, og_s)
        if upto >= 5.7:
            gn_phase(kb, block, bufs, og_s, o_s, nw_sb)
        if upto >= 6:
            proj_postln_phase(kb, block, bufs, o_s, NCH, w_dram=l1_wo)
        if upto >= 6.1:
            ln_pre_phase(kb, block, bufs, dst_f32=xn32_s, want_max=True)
        if upto >= 6.2:
            quant_weights_phase(kb, block, bufs, l1_w1, l1_w2, w1qT_s,
                                w2qT_s, sw1, sw2, cc_in, cc_out, gcol)
        if upto >= 6.3:
            pass
        if upto >= 6.4:
            mm1_phase(kb, block, bufs, xn32_s, w1qT_s, y_s, sw1, b1col, gcol,
                      cc_in2, cc_out2, g2col)
        if upto >= 6.5:
            hq_quant_phase(kb, block, bufs, y_s, hq_s, sw2, b2col, g2col,
                           esc2, ebi2)
        if upto >= 7:
            proj_postln_phase(kb, block, bufs, hq_s, FF1 // 128,
                              w_bf_dram=w2qT_s, evac_scale_col=esc2,
                              evac_bias_col=ebi2, tt=256)

        def st(e):
            last = e.dma_start(out=outT.ap().rearrange("(c p) t -> p c t", p=128),
                               in_=X.ap())
            kb.ldh.tag(last, 16)
            e.wait_ge(kb.ldh.h, kb.ldh.v)
            e.wait_ge(kb.sth.h, kb.sth.v)
            e.wait_ge(kb.lds.h, kb.lds.v)
        block.sync(st)

    return nc


_NC_CACHE = {}


def kernel(**inputs):
    B, Tfull = 8, 4096

    def A(x, dt=np.float32):
        return np.ascontiguousarray(np.asarray(x), dtype=dt)

    def T_(x, dt=np.float32):
        return np.ascontiguousarray(np.asarray(x).T, dtype=dt)

    if "nc" not in _NC_CACHE:
        _NC_CACHE["nc"] = build_nc(T=Tfull, ncores=B)
    nc = _NC_CACHE["nc"]
    x = np.asarray(inputs["L0_attn_x"], dtype=np.float32)
    xT = np.ascontiguousarray(x.transpose(0, 2, 1))
    common = {
        "l0_wqT": T_(inputs["L0_attn_Wq"]),
        "l0_wkT": T_(inputs["L0_attn_Wk"]),
        "l0_wvT": T_(inputs["L0_attn_Wv"]),
        "l0_woT": T_(inputs["L0_attn_Wo"]),
        "l0_wgT": T_(inputs["L0_ffn_w_gate"], np.int32),
        "l0_wuT": T_(inputs["L0_ffn_w_up"], np.int32),
        "l0_wdT": T_(inputs["L0_ffn_w_down"], np.int32),
        "l1_wq": A(inputs["L1_attn_wq"]),
        "l1_wk": A(inputs["L1_attn_wk"]),
        "l1_wv": A(inputs["L1_attn_wv"]),
        "l1_wo": A(inputs["L1_attn_wo"]),
        "l1_wg1": A(inputs["L1_attn_wg1"]),
        "l1_wg2": A(inputs["L1_attn_wg2"]),
        "l1_nw": np.ascontiguousarray(
            np.asarray(inputs["L1_attn_norm_w"], dtype=np.float32)
            .reshape(NCH, 128).T),
        "l1_w1": A(inputs["L1_ffn_w1"]),
        "l1_b1": np.ascontiguousarray(
            np.asarray(inputs["L1_ffn_b1"], dtype=np.float32)
            .reshape(FF1 // 128, 128).T),
        "l1_w2": A(inputs["L1_ffn_w2"]),
        "l1_b2": np.ascontiguousarray(
            np.asarray(inputs["L1_ffn_b2"], dtype=np.float32)
            .reshape(NCH, 128).T),
    }
    in_maps = [{"xT": np.ascontiguousarray(xT[i]), **common} for i in range(B)]
    res = run_bass_kernel_spmd(nc, in_maps, core_ids=list(range(B)))
    out = np.stack([np.asarray(r["outT"], dtype=np.float32).T
                    for r in res.results])
    return np.ascontiguousarray(out)
